# version 06
# speedup vs baseline: 1.2967x; 1.2167x over previous
import numpy as np
import ml_dtypes
from contextlib import ExitStack

import concourse.bass as bass
import concourse.mybir as mybir
from concourse.bass import AP
from concourse.bass_utils import run_bass_kernel_spmd

F32 = mybir.dt.float32
BF16 = mybir.dt.bfloat16
AF = mybir.ActivationFunctionType
ALU = mybir.AluOpType
AX = mybir.AxisListType

T = 2048
L2F_PARTS = "all"
L2F_DEBUG = False
QUIET_GROUPS = False
D = 1024
NC = 8
NTB = 4
TB = 512
NEG = -1.0e30
EPS = 1e-5

ENGINE_NAMES = ("sync", "tensor", "vector", "scalar", "gpsimd")



class Buf:
    __slots__ = ("name", "writer", "readers")

    def __init__(self, name):
        self.name = name
        self.writer = None
        self.readers = {}


class EngRec:
    def __init__(self, name, csem, dsem):
        self.name = name
        self.csem = csem
        self.dsem = dsem
        self.cn = 0
        self.dn = 0
        self.items = []
        self.watermark = {}
        self.pend_r = []
        self.pend_w = []


class Bld:
    def __init__(self, nc):
        self.nc = nc
        self.es = ExitStack()
        self.engs = {}
        for n in ENGINE_NAMES:
            csem = self.es.enter_context(nc.semaphore(f"c_{n}"))
            dsem = self.es.enter_context(nc.semaphore(f"d_{n}"))
            self.engs[n] = EngRec(n, csem, dsem)
        self.bufs = []

    def buf(self, name):
        b = Buf(name)
        self.bufs.append(b)
        return b

    def emit(self, ename, method, *args, reads=(), writes=(), **kw):
        e = self.engs[ename]
        waits = []

        def need(tok):
            if tok is None:
                return
            sem, val = tok
            if sem is e.csem and ename == "tensor":
                return
            if e.watermark.get(id(sem), 0) >= val:
                return
            waits.append((sem, val))
            e.watermark[id(sem)] = val

        for b in reads:
            need(b.writer)
        for b in writes:
            need(b.writer)
            for rt in b.readers.values():
                need(rt)
        is_dma = method in ("dma_start", "dma_start_transpose")
        quiet = QUIET_GROUPS and (method == "matmul" and kw.get("stop") is False)
        if quiet:
            e.items.append((waits, method, args, kw, None))
            e.pend_r.extend(reads)
            e.pend_w.extend(writes)
            return None
        if is_dma:
            e.dn += 16
            tok = (e.dsem, e.dn)
            inc = (e.dsem, 16)
        else:
            e.cn += 1
            tok = (e.csem, e.cn)
            inc = (e.csem, 1)
        e.items.append((waits, method, args, kw, inc))
        for b in e.pend_r:
            b.readers[ename] = tok
        for b in e.pend_w:
            b.writer = tok
            b.readers = {}
        e.pend_r = []
        e.pend_w = []
        for b in reads:
            b.readers[ename] = tok
        for b in writes:
            b.writer = tok
            b.readers = {}
        return tok

    def barrier(self):
        toks = []
        for n in ENGINE_NAMES:
            e = self.engs[n]
            toks.append((e.csem, e.cn))
            toks.append((e.dsem, e.dn))
        for n in ENGINE_NAMES:
            e = self.engs[n]
            for sem, val in toks:
                if sem is e.csem or val == 0:
                    continue
                if e.watermark.get(id(sem), 0) >= val:
                    continue
                e.items.append(([(sem, val)], None, None, None, None))
                e.watermark[id(sem)] = val

    def finalize(self):
        nc = self.nc
        with nc.Block() as block:
            for n in ENGINE_NAMES:
                rec = self.engs[n]

                def run(eng, rec=rec):
                    for waits, method, args, kw, inc in rec.items:
                        for sem, val in waits:
                            eng.wait_ge(sem, val)
                        if method is None:
                            continue
                        ins = getattr(eng, method)(*args, **kw)
                        if inc is not None:
                            ins.then_inc(*inc)
                getattr(block, n)(run)



class Arena:
    def __init__(self, bld, name, dtype, cols):
        self.bld = bld
        self.t = bld.es.enter_context(
            bld.nc.sbuf_tensor(name, [128, cols], dtype))
        self.cols = cols
        self.off = 0
        self.marks = []

    def take(self, cols, name):
        self.off = (self.off + 15) & ~15
        assert self.off + cols <= self.cols, \
            f"arena {self.t.name} overflow: {self.off}+{cols}>{self.cols} ({name})"
        off = self.off
        self.off += cols
        return off, self.bld.buf(name)

    def view(self, off, cols, p0=0, p1=128):
        return self.t[p0:p1, off:off + cols]

    def push(self):
        self.marks.append(self.off)

    def pop(self):
        self.off = self.marks.pop()



def _hp(x):
    return np.ascontiguousarray(np.asarray(x, dtype=np.float32))


def _chunk_vec(v):
    return _hp(np.asarray(v, np.float32).reshape(NC, 128))


def host_prep_common(inp):
    d = {}
    d["c_id_bf"] = _hp(np.eye(128))
    d["c_id_f32"] = _hp(np.eye(128))
    d["c_J_f32"] = _hp(np.eye(128)[::-1])
    d["c_ones_row_f32"] = _hp(np.ones((1, 128)))
    d["c_ones_col_f32"] = _hp(np.ones((128, 1)))

    d["l0_wq_t"] = _hp(np.asarray(inp["L0_attn_Wq"]).T)
    d["l0_wk_t"] = _hp(np.asarray(inp["L0_attn_Wk"]).T)
    d["l0_wv_t"] = _hp(np.asarray(inp["L0_attn_Wv"]).T)
    d["l0_wo_t"] = _hp(np.asarray(inp["L0_attn_Wo"]).T)
    d["l0_bq"] = _chunk_vec(inp["L0_attn_bq"])
    d["l0_bk"] = _chunk_vec(inp["L0_attn_bk"])
    d["l0_bv"] = _chunk_vec(inp["L0_attn_bv"])
    d["l0_bo"] = _chunk_vec(inp["L0_attn_bo"])
    rel = np.asarray(inp["L0_attn_rel_bias"], np.float32)
    ext = np.pad(rel, ((0, 0), (639, 639)), mode="edge")
    d["l0_ext"] = _hp(ext)
    d["l0_ext_rev"] = _hp(ext[:, ::-1])
    d["l0_rel_raw"] = _hp(rel)

    d["l0f_rails"] = _hp(inp["L0_ffn_rail_maker"])
    d["l0f_up_t"] = _hp(np.asarray(inp["L0_ffn_rail_up_proj"]).transpose(0, 1, 3, 2))
    d["l0f_wc_t"] = _hp(np.asarray(inp["L0_ffn_w_cproj"]).T)

    d["l1_w_qkv"] = _hp(inp["L1_attn_w_qkv"])
    d["l1_w_out"] = _hp(inp["L1_attn_w_out"])
    d["l1_b_out"] = _chunk_vec(inp["L1_attn_b_out"])
    d["l1_g"] = _chunk_vec(inp["L1_attn_g"])
    d["l1_b"] = _chunk_vec(inp["L1_attn_b"])
    d["l1f_wr_t"] = _hp(np.asarray(inp["L1_ffn_W_r"]).T)
    d["l1f_wk_t"] = _hp(np.asarray(inp["L1_ffn_W_k"]).T)
    d["l1f_wv_t"] = _hp(np.asarray(inp["L1_ffn_W_v"]).T)
    d["l1f_mu_r"] = _chunk_vec(inp["L1_ffn_mu_r"])
    d["l1f_mu_k"] = _chunk_vec(inp["L1_ffn_mu_k"])

    d["l2_wq"] = _hp(inp["L2_attn_Wq"])
    d["l2_wkv"] = _hp(inp["L2_attn_Wkv"])
    d["l2_bq"] = _chunk_vec(inp["L2_attn_bq"])
    bkv = np.asarray(inp["L2_attn_bkv"], np.float32)
    d["l2_bk"] = _chunk_vec(bkv[0:1024])
    d["l2_bv"] = _hp(bkv[1024:2048])
    d["l2_wo"] = _hp(inp["L2_attn_Wo"])
    d["l2_bo"] = _chunk_vec(inp["L2_attn_bo"])
    pos = np.arange(T, dtype=np.float64)
    invf = 1.0 / 10000.0 ** (np.arange(0, 64, 2, dtype=np.float64) / 64.0)
    ang = pos[None, :] * invf[:, None]
    cs = np.cos(ang); sn = np.sin(ang)
    d["l2_cos"] = _hp(np.concatenate([cs, cs], axis=0))
    d["l2_sin"] = _hp(np.concatenate([sn, sn], axis=0))
    tri = np.where(np.arange(128)[:, None] <= np.arange(128)[None, :], 0.0, -1e30)
    d["l2_mtri"] = _hp(tri)
    d["l2_mtri_t"] = _hp(tri.T)
    d["l2f_kv_d_t"] = _hp(np.asarray(inp["L2_ffn_kv_d"]).T)
    d["l2f_q_d_t"] = _hp(np.asarray(inp["L2_ffn_q_d"]).T)
    d["l2f_k_u_t"] = _hp(np.asarray(inp["L2_ffn_k_u"]).T)
    d["l2f_q_u_t"] = _hp(np.asarray(inp["L2_ffn_q_u"]).T)
    d["l2f_v_u_t"] = _hp(np.asarray(inp["L2_ffn_v_u"]).T)
    d["l2f_rope_q_t"] = _hp(np.asarray(inp["L2_ffn_rope_q_w"]).T)
    d["l2f_rope_k_t"] = _hp(np.asarray(inp["L2_ffn_rope_k_w"]).T)
    d["l2f_o_w_t"] = _hp(np.asarray(inp["L2_ffn_o_w"]).T)
    d["l2f_ln1"] = _chunk_vec(inp["L2_ffn_ln1_w"])
    d["l2f_ln2"] = _chunk_vec(inp["L2_ffn_ln2_w"])
    d["l2f_router_t"] = _hp(np.asarray(inp["L2_ffn_router_w"]).T)
    d["l2f_rbias"] = _hp(np.tile(np.asarray(inp["L2_ffn_routing_bias"],
                                            np.float32)[None, :], (128, 1)))
    pos2 = np.arange(T, dtype=np.float64)
    invf2 = 1.0 / 10000.0 ** (np.arange(0, 32, 2, dtype=np.float64) / 32.0)
    idx2 = np.arange(128) % 32 % 16
    ang2 = pos2[None, :] * invf2[idx2][:, None]
    d["l2f_cos2"] = _hp(np.cos(ang2))
    d["l2f_sin2"] = _hp(np.sin(ang2))
    P = np.zeros((128, 128), np.float32)
    for blk in range(4):
        for i in range(16):
            P[blk * 32 + i, blk * 32 + i + 16] = -1.0
            P[blk * 32 + 16 + i, blk * 32 + i] = 1.0
    d["l2f_p32t"] = _hp(P.T)
    gate = np.concatenate([np.asarray(inp["L2_ffn_shared_gate"], np.float32),
                           np.asarray(inp["L2_ffn_routed_gate"], np.float32)])
    up = np.concatenate([np.asarray(inp["L2_ffn_shared_up"], np.float32),
                         np.asarray(inp["L2_ffn_routed_up"], np.float32)])
    down = np.concatenate([np.asarray(inp["L2_ffn_shared_down"], np.float32),
                           np.asarray(inp["L2_ffn_routed_down"], np.float32)])
    d["l2f_gate_t"] = _hp(gate.transpose(0, 2, 1))
    d["l2f_up_t"] = _hp(up.transpose(0, 2, 1))
    d["l2f_down_t"] = _hp(down.transpose(0, 2, 1))

    d["l3_wq"] = _hp(inp["L3_attn_Wq"])
    d["l3_wk"] = _hp(inp["L3_attn_Wk"])
    d["l3_wv"] = _hp(inp["L3_attn_Wv"])
    d["l3_bq"] = _chunk_vec(inp["L3_attn_bq"])
    d["l3_bk"] = _hp(inp["L3_attn_bk"])
    d["l3_bv"] = _hp(inp["L3_attn_bv"])
    d["l3_temp"] = _hp(np.tile(np.asarray(inp["L3_attn_temperature"],
                                          np.float32).reshape(16, 1), (1, 128)))
    d["l3f_w_t"] = _hp(np.asarray(inp["L3_ffn_W"]).T)
    d["l3f_vr"] = _hp(inp["L3_ffn_V_r"])
    d["l3f_c_t"] = _hp(np.asarray(inp["L3_ffn_C"]).T)
    d["l3f_b"] = _chunk_vec(inp["L3_ffn_b"])
    return d



class Ker:

    def __init__(self, nc, inputs_avail):
        self.nc = nc
        self.b = Bld(nc)
        self.dram = {}
        self.inputs_avail = inputs_avail
        es = self.b.es
        self.X = es.enter_context(nc.sbuf_tensor("X", [128, NC * T], F32))
        self.XBufs = [self.b.buf(f"X{c}") for c in range(NC)]
        self.XH = es.enter_context(nc.sbuf_tensor("XH", [128, NC * T], BF16))
        self.XHBufs = [self.b.buf(f"XH{c}") for c in range(NC)]
        self.OF = es.enter_context(nc.sbuf_tensor("OF", [128, NC * T], BF16))
        self.OFBufs = [self.b.buf(f"OF{c}") for c in range(NC)]
        self.af = Arena(self.b, "AF32", F32, 6144)
        self.ab = Arena(self.b, "ABF16", BF16, 21504)
        self.m_row = es.enter_context(nc.sbuf_tensor("m_row", [1, TB], F32))
        self.r_row = es.enter_context(nc.sbuf_tensor("r_row", [1, TB], F32))
        self.srow1 = es.enter_context(nc.sbuf_tensor("srow1", [1, TB], F32))
        self.srow2 = es.enter_context(nc.sbuf_tensor("srow2", [1, TB], F32))
        self.m_rowB = self.b.buf("m_row")
        self.r_rowB = self.b.buf("r_row")
        self.srow1B = self.b.buf("srow1")
        self.srow2B = self.b.buf("srow2")
        self.cols = es.enter_context(nc.sbuf_tensor("cols", [128, 224], F32))
        self.wsmall = es.enter_context(nc.sbuf_tensor("wsmall", [128, 16], BF16))
        self.wsmallB = self.b.buf("wsmall")
        self.augst = es.enter_context(nc.sbuf_tensor("augst", [16, 128], BF16))
        self.brow_hi = es.enter_context(nc.sbuf_tensor("brow_hi", [1, TB], BF16))
        self.brow_lo = es.enter_context(nc.sbuf_tensor("brow_lo", [1, TB], BF16))
        self.browB = self.b.buf("brow")
        self.ones_row_bf = es.enter_context(nc.sbuf_tensor("ones_row_bf", [1, 128], BF16))
        self.colsB = self.b.buf("cols")
        self.col_alloc = 0
        self.id_bf = es.enter_context(nc.sbuf_tensor("id_bf", [128, 128], BF16))
        self.id_f32 = es.enter_context(nc.sbuf_tensor("id_f32", [128, 128], F32))
        self.J_bf = es.enter_context(nc.sbuf_tensor("J_bf", [128, 128], BF16))
        self.ones_row = es.enter_context(nc.sbuf_tensor("ones_row", [1, 128], F32))
        self.ones_col_bf = es.enter_context(nc.sbuf_tensor("ones_col_bf", [128, 1], BF16))
        self.constB = self.b.buf("consts")
        self.ps_big = es.enter_context(nc.psum_tensor("ps_big", [128, 2048], F32))
        self.ps_bigB = self.b.buf("ps_big")
        self.ps_bigQ = [self.b.buf(f"ps_bigq{i}") for i in range(4)]
        self.ps_a = es.enter_context(nc.psum_tensor("ps_a", [128, 512], F32))
        self.ps_aB = self.b.buf("ps_a")
        self.ps_b = es.enter_context(nc.psum_tensor("ps_b", [128, 512], F32))
        self.ps_bB = self.b.buf("ps_b")
        self.ps_c = es.enter_context(nc.psum_tensor("ps_c", [128, 512], F32))
        self.ps_cB = self.b.buf("ps_c")
        self.ps_d = es.enter_context(nc.psum_tensor("ps_d", [128, 512], F32))
        self.ps_dB = self.b.buf("ps_d")

    def din(self, name, shape):
        if name not in self.dram:
            self.dram[name] = self.nc.dram_tensor(
                name, list(shape), F32, kind="ExternalInput").ap()
            assert name in self.inputs_avail, name
        return self.dram[name]

    def col(self, n=1):
        off = self.col_alloc
        self.col_alloc += n
        assert self.col_alloc <= 224
        return off

    def zero_init(self):
        b = self.b
        b.emit("gpsimd", "memset", self.ab.view(0, self.ab.cols), 0.0,
               writes=[self.b.buf("z0")])
        b.emit("gpsimd", "memset", self.af.view(0, self.af.cols), 0.0,
               writes=[self.b.buf("z1")])
        b.emit("gpsimd", "memset", self.OF[:, :], 0.0, writes=self.OFBufs)
        b.emit("gpsimd", "memset", self.cols[:, :], 0.0, writes=[self.colsB])
        b.emit("vector", "memset", self.ps_big[:, :], 0.0,
               writes=[self.ps_bigB] + self.ps_bigQ)
        b.emit("vector", "memset", self.ps_a[:, :], 0.0, writes=[self.ps_aB])
        b.emit("vector", "memset", self.ps_b[:, :], 0.0, writes=[self.ps_bB])
        b.emit("vector", "memset", self.ps_c[:, :], 0.0, writes=[self.ps_cB])
        b.emit("vector", "memset", self.ps_d[:, :], 0.0, writes=[self.ps_dB])

    def load_consts(self):
        b = self.b
        idb = self.din("c_id_bf", (128, 128))
        idf = self.din("c_id_f32", (128, 128))
        jf = self.din("c_J_f32", (128, 128))
        onr = self.din("c_ones_row_f32", (1, 128))
        b.emit("sync", "dma_start", out=self.id_f32[:], in_=idf,
               writes=[self.constB])
        b.emit("sync", "dma_start", out=self.ones_row[:], in_=onr,
               writes=[self.constB])
        b.emit("gpsimd", "dma_start", out=self.id_bf[:], in_=idb,
               reads=[self.constB], writes=[self.constB])
        b.emit("gpsimd", "dma_start", out=self.J_bf[:], in_=jf,
               reads=[self.constB], writes=[self.constB])
        b.emit("gpsimd", "memset", self.ones_col_bf[:], 1.0,
               writes=[self.constB])
        b.emit("gpsimd", "memset", self.ones_row_bf[:], 1.0,
               writes=[self.constB])
        self.c_eps = self.col(1)
        b.emit("gpsimd", "memset", self.cols[:, self.c_eps:self.c_eps + 1], EPS,
               writes=[self.colsB])

    def prologue(self, xname="x_in"):
        b = self.b
        x = self.din(xname, (T, D))
        self.af.push()
        soff, sB = self.af.take(1024, "prolog_stage")
        for tt in range(16):
            b.emit("sync", "dma_start", out=self.af.view(soff, 1024),
                   in_=x[tt * 128:(tt + 1) * 128, :],
                   reads=[sB], writes=[sB])
            for c in range(NC):
                ps = self.ps_a if (c % 2 == 0) else self.ps_b
                psB = self.ps_aB if (c % 2 == 0) else self.ps_bB
                b.emit("tensor", "transpose", ps[:, 0:128],
                       self.af.view(soff, 1024)[:, c * 128:(c + 1) * 128],
                       self.id_f32[:],
                       reads=[sB, self.constB], writes=[psB])
                b.emit("vector", "tensor_copy",
                       out=self.X[:, c * T + tt * 128: c * T + (tt + 1) * 128],
                       in_=ps[:, 0:128], reads=[psB], writes=[self.XBufs[c]])
        self.af.pop()

    def epilogue(self, yname="y_out"):
        b = self.b
        y = self.nc.dram_tensor(yname, [T, D], F32, kind="ExternalOutput").ap()
        self.af.push()
        soff, sB = self.af.take(1024, "epilog_stage")
        for tt in range(16):
            for c in range(NC):
                ps = self.ps_a if (c % 2 == 0) else self.ps_b
                psB = self.ps_aB if (c % 2 == 0) else self.ps_bB
                b.emit("tensor", "transpose", ps[:, 0:128],
                       self.X[:, c * T + tt * 128: c * T + (tt + 1) * 128],
                       self.id_f32[:],
                       reads=[self.XBufs[c], self.constB], writes=[psB])
                b.emit("vector", "tensor_copy",
                       out=self.af.view(soff, 1024)[:, c * 128:(c + 1) * 128],
                       in_=ps[:, 0:128], reads=[psB], writes=[sB])
            b.emit("sync", "dma_start", out=y[tt * 128:(tt + 1) * 128, :],
                   in_=self.af.view(soff, 1024), reads=[sB], writes=[sB])
        self.af.pop()
        e = self.b.engs["sync"]
        e.items.append(([(e.dsem, e.dn)], None, None, None, None))

    def bcast_row(self, ps, psB, row_ap, rowB, width):
        b = self.b
        b.emit("vector", "tensor_copy", out=self.brow_hi[0:1, 0:width],
               in_=row_ap, reads=[rowB], writes=[self.browB])
        b.emit("vector", "tensor_sub", out=self.brow_lo[0:1, 0:width],
               in0=row_ap, in1=self.brow_hi[0:1, 0:width],
               reads=[rowB, self.browB], writes=[self.browB])
        b.emit("tensor", "matmul", ps[:, 0:width],
               self.ones_row_bf[0:1, :], self.brow_hi[0:1, 0:width],
               start=True, stop=False,
               reads=[self.browB, self.constB], writes=[psB])
        b.emit("tensor", "matmul", ps[:, 0:width],
               self.ones_row_bf[0:1, :], self.brow_lo[0:1, 0:width],
               start=False, stop=True,
               reads=[self.browB, self.constB], writes=[psB])

    def flush_aug(self, qoffh, qBh):
        b = self.b
        b.emit("tensor", "matmul", self.ps_c[0:16, 0:128],
               self.wsmall[:, 0:16], self.id_bf[:],
               start=True, stop=True,
               reads=[self.wsmallB, self.constB], writes=[self.ps_cB])
        b.emit("scalar", "copy", out=self.augst[:],
               in_=self.ps_c[0:16, 0:128],
               reads=[self.ps_cB], writes=[self.wsmallB])
        b.emit("gpsimd", "dma_start",
               out=self.ab.view(qoffh, T, 64, 65),
               in_=self.augst[:],
               reads=[self.wsmallB], writes=[qBh])

    def layernorm(self, scale_col=None, bias_col=None):
        b = self.b
        self.af.push()
        self.ab.push()
        sqoff, sqB = self.ab.take(TB, "ln_sq")
        tmpoff, tmpB = self.af.take(TB, "ln_tmp")
        for tb in range(NTB):
            for c in range(NC):
                b.emit("scalar", "copy",
                       out=self.XH[:, c * T + tb * TB: c * T + (tb + 1) * TB],
                       in_=self.X[:, c * T + tb * TB: c * T + (tb + 1) * TB],
                       reads=[self.XBufs[c]], writes=[self.XHBufs[c]])
                b.emit("scalar", "activation",
                       out=self.ab.view(sqoff, TB),
                       in_=self.XH[:, c * T + tb * TB: c * T + (tb + 1) * TB],
                       func=AF.Square,
                       reads=[self.XHBufs[c]], writes=[sqB])
                b.emit("tensor", "matmul", self.ps_c[0:1, :],
                       self.ones_col_bf[:],
                       self.XH[:, c * T + tb * TB: c * T + (tb + 1) * TB],
                       start=(c == 0), stop=(c == NC - 1),
                       reads=[self.XHBufs[c], self.constB], writes=[self.ps_cB])
                b.emit("tensor", "matmul", self.ps_d[0:1, :],
                       self.ones_col_bf[:], self.ab.view(sqoff, TB),
                       start=(c == 0), stop=(c == NC - 1),
                       reads=[sqB, self.constB], writes=[self.ps_dB])
            mr = self.m_row[0:1, :]
            rr = self.r_row[0:1, :]
            s1 = self.srow1[0:1, :]
            s2 = self.srow2[0:1, :]
            b.emit("vector", "tensor_scalar", out=mr, in0=self.ps_c[0:1, :],
                   scalar1=1.0 / D, scalar2=None, op0=ALU.mult,
                   reads=[self.ps_cB], writes=[self.m_rowB])
            b.emit("vector", "tensor_scalar", out=s1, in0=self.ps_d[0:1, :],
                   scalar1=1.0 / D, scalar2=None, op0=ALU.mult,
                   reads=[self.ps_dB], writes=[self.srow1B])
            b.emit("vector", "tensor_mul", out=s2, in0=mr, in1=mr,
                   reads=[self.m_rowB], writes=[self.srow2B])
            b.emit("vector", "tensor_sub", out=s1, in0=s1, in1=s2,
                   reads=[self.srow1B, self.srow2B], writes=[self.srow1B])
            b.emit("scalar", "activation", out=s1, in_=s1, func=AF.Sqrt,
                   bias=self.cols[0:1, self.c_eps:self.c_eps + 1],
                   reads=[self.srow1B, self.colsB], writes=[self.srow1B])
            b.emit("vector", "reciprocal", out=rr, in_=s1,
                   reads=[self.srow1B], writes=[self.r_rowB])
            self.bcast_row(self.ps_c, self.ps_cB, mr, self.m_rowB, TB)
            self.bcast_row(self.ps_d, self.ps_dB, rr, self.r_rowB, TB)
            for c in range(NC):
                b.emit("vector", "tensor_sub",
                       out=self.af.view(tmpoff, TB),
                       in0=self.X[:, c * T + tb * TB: c * T + (tb + 1) * TB],
                       in1=self.ps_c[:],
                       reads=[self.XBufs[c], self.ps_cB], writes=[tmpB])
                b.emit("vector", "tensor_mul",
                       out=self.XH[:, c * T + tb * TB: c * T + (tb + 1) * TB],
                       in0=self.af.view(tmpoff, TB), in1=self.ps_d[:],
                       reads=[tmpB, self.ps_dB], writes=[self.XHBufs[c]])
                if scale_col is not None:
                    sc_t, sc_B, sc_off = scale_col
                    bi_t, bi_B, bi_off = bias_col
                    b.emit("vector", "tensor_scalar",
                           out=self.XH[:, c * T + tb * TB: c * T + (tb + 1) * TB],
                           in0=self.XH[:, c * T + tb * TB: c * T + (tb + 1) * TB],
                           scalar1=sc_t[:, sc_off + c:sc_off + c + 1],
                           scalar2=bi_t[:, bi_off + c:bi_off + c + 1],
                           op0=ALU.mult, op1=ALU.add,
                           reads=[self.XHBufs[c], sc_B], writes=[self.XHBufs[c]])
        self.af.pop()
        self.ab.pop()
        self.b.barrier()

    def load_wt_bf16(self, dram_ap, rows, cols, ab_off, abB, st_off=None,
                     stB=None, cast_eng=None):
        b = self.b
        nk = rows // 128
        for kc in range(nk):
            b.emit("gpsimd", "dma_start",
                   out=self.ab.view(ab_off + kc * cols, cols),
                   in_=dram_ap[kc * 128:(kc + 1) * 128, :],
                   reads=[abB], writes=[abB])

    def load_cols(self, dram_name, shape, coff, n):
        b = self.b
        v = self.din(dram_name, shape)
        for c in range(n):
            b.emit("sync", "dma_start",
                   out=self.cols[:, coff + c:coff + c + 1],
                   in_=v[c:c + 1, :].rearrange("a b -> b a"),
                   reads=[self.colsB], writes=[self.colsB])

    def proj_fm(self, wt_ab_off, wtB, out_t, out_bufs, out_base_col, osize,
                bias_coff=None, scale=None, in_t=None, in_bufs=None,
                in_base=0, ksize=D, evac="scalar", accum_ps=None):
        b = self.b
        if in_t is None:
            in_t, in_bufs = self.XH, self.XHBufs
        nk = ksize // 128
        no = osize // 128
        for oc in range(no):
            for tb in range(NTB):
                ps = self.ps_a if ((oc * NTB + tb) % 2 == 0) else self.ps_b
                psB = self.ps_aB if ((oc * NTB + tb) % 2 == 0) else self.ps_bB
                for kc in range(nk):
                    b.emit("tensor", "matmul", ps[:],
                           self.ab.view(wt_ab_off + kc * osize, osize)[:, oc * 128:(oc + 1) * 128],
                           in_t[:, in_base + kc * T + tb * TB: in_base + kc * T + (tb + 1) * TB],
                           start=(kc == 0), stop=(kc == nk - 1),
                           reads=[wtB, in_bufs[kc]], writes=[psB])
                kwargs = {}
                if bias_coff is not None:
                    kwargs["bias"] = self.cols[:, bias_coff + oc:bias_coff + oc + 1]
                if scale is not None:
                    kwargs["scale"] = scale
                b.emit(evac, "activation",
                       out=out_t[:, out_base_col + oc * T + tb * TB: out_base_col + oc * T + (tb + 1) * TB],
                       in_=ps[:], func=AF.Copy,
                       reads=[psB, self.colsB], writes=[out_bufs[oc]], **kwargs)

    def l0_attn(self):
        b = self.b
        nc = self.nc
        self.layernorm()
        wq = self.din("l0_wq_t", (D, D))
        wk = self.din("l0_wk_t", (D, D))
        wv = self.din("l0_wv_t", (D, D))
        wo = self.din("l0_wo_t", (D, D))
        ext = self.din("l0_ext", (16, 1535))
        ext_rev = self.din("l0_ext_rev", (16, 1535))
        rel_raw = self.din("l0_rel_raw", (16, 257))
        cb_q = self.col(NC); self.load_cols("l0_bq", (NC, 128), cb_q, NC)
        cb_k = self.col(NC); self.load_cols("l0_bk", (NC, 128), cb_k, NC)
        cb_o = self.col(NC); self.load_cols("l0_bo", (NC, 128), cb_o, NC)
        bv_dram = self.din("l0_bv", (NC, 128))
        b.emit("vector", "tensor_scalar",
               out=self.cols[:, cb_q:cb_q + NC], in0=self.cols[:, cb_q:cb_q + NC],
               scalar1=0.125, scalar2=None, op0=ALU.mult,
               reads=[self.colsB], writes=[self.colsB])
        c_lo = self.col(2)

        self.ab.push()
        self.af.push()
        wsl_off, wslB = self.ab.take(3 * 1024, "l0_wslices")
        q1_off, q1B = self.ab.take(T, "Q1")
        q2_off, q2B = self.ab.take(T, "Q2")
        k1_off, k1B = self.ab.take(T, "K1")
        k2_off, k2B = self.ab.take(T, "K2")
        va_off, vaB = self.ab.take(16 * 130, "VA")
        pt_off, ptB = self.ab.take(2 * TB, "PT")
        u1_off, u1B = self.ab.take(1152, "U_h1")
        u2_off, u2B = self.ab.take(1152, "U_h2")
        w1_off, w1B = self.ab.take(1152, "U2_h1")
        w2_off, w2B = self.ab.take(1152, "U2_h2")
        st_off, stB = self.af.take(1024, "wstage")
        hk_off, hkB = self.af.take(512, "hankel")
        nr_off, nrB = self.af.take(1024, "near")
        vb_off, vbB = self.af.take(128, "vbias")
        c_m = self.col(4)

        def build_toeplitz(dram_row_t, h, base_idx, dst_off, dstB):
            for g0 in range(0, 1152, 512):
                w = min(512, 1152 - g0)
                b.emit("gpsimd", "dma_start",
                       out=self.ab.view(pt_off, TB)[:, 0:w],
                       in_=AP(dram_row_t.tensor, h * 1535 + base_idx + g0,
                              [[1, 128], [1, w]]),
                       reads=[ptB], writes=[ptB])
                b.emit("tensor", "matmul", self.ps_c[:, 0:w],
                       self.J_bf[:], self.ab.view(pt_off, TB)[:, 0:w],
                       start=True, stop=True,
                       reads=[ptB, self.constB], writes=[self.ps_cB])
                b.emit("scalar", "activation",
                       out=self.ab.view(dst_off + g0, w),
                       in_=self.ps_c[:, 0:w], func=AF.Copy,
                       reads=[self.ps_cB], writes=[dstB])

        b.emit("gpsimd", "memset", self.ab.view(k1_off, T, 64, 65), 1.0, writes=[k1B])
        b.emit("gpsimd", "memset", self.ab.view(k2_off, T, 64, 65), 1.0, writes=[k2B])
        for tt in range(16):
            b.emit("gpsimd", "memset",
                   self.ab.view(va_off + tt * 130 + 64, 1), 1.0, writes=[vaB])
            b.emit("gpsimd", "memset",
                   self.ab.view(va_off + tt * 130 + 129, 1), 1.0, writes=[vaB])

        for hp in range(8):
            h1, h2 = 2 * hp, 2 * hp + 1
            for wi, wdr in enumerate((wq, wk, wv)):
                for kc in range(NC):
                    b.emit("gpsimd", "dma_start",
                           out=self.ab.view(wsl_off + wi * 1024 + kc * 128, 128),
                           in_=wdr[kc * 128:(kc + 1) * 128, hp * 128:(hp + 1) * 128],
                           reads=[wslB], writes=[wslB])
            for (dst_off, dstB, head_half, wi, coff, scl) in (
                    (q1_off, q1B, 0, 0, cb_q, 0.125),
                    (q2_off, q2B, 1, 0, cb_q, 0.125),
                    (k1_off, k1B, 0, 1, cb_k, None),
                    (k2_off, k2B, 1, 1, cb_k, None)):
                for tb in range(NTB):
                    ps = self.ps_a if tb % 2 == 0 else self.ps_b
                    psB = self.ps_aB if tb % 2 == 0 else self.ps_bB
                    for kc in range(NC):
                        b.emit("tensor", "matmul", ps[:],
                               self.ab.view(wsl_off + wi * 1024 + kc * 128, 128),
                               self.XH[:, kc * T + tb * TB: kc * T + (tb + 1) * TB],
                               start=(kc == 0), stop=(kc == NC - 1),
                               reads=[wslB, self.XHBufs[kc]], writes=[psB])
                    bias_ap = self.cols[head_half * 64:(head_half + 1) * 64,
                                        coff + hp:coff + hp + 1]
                    if scl is not None:
                        b.emit("vector", "tensor_scalar",
                               out=self.ab.view(dst_off + tb * TB, TB, 0, 64),
                               in0=ps[head_half * 64:(head_half + 1) * 64, :],
                               scalar1=scl, scalar2=bias_ap,
                               op0=ALU.mult, op1=ALU.add,
                               reads=[psB, self.colsB], writes=[dstB])
                    else:
                        b.emit("vector", "tensor_scalar",
                               out=self.ab.view(dst_off + tb * TB, TB, 0, 64),
                               in0=ps[head_half * 64:(head_half + 1) * 64, :],
                               scalar1=bias_ap, scalar2=None, op0=ALU.add,
                               reads=[psB, self.colsB], writes=[dstB])
            b.emit("sync", "dma_start",
                   out=self.af.view(vb_off, 128),
                   in_=AP(bv_dram.tensor, hp * 128, [[0, 128], [1, 128]]),
                   reads=[vbB], writes=[vbB])
            for tt in range(16):
                ps = self.ps_a if tt % 2 == 0 else self.ps_b
                psB = self.ps_aB if tt % 2 == 0 else self.ps_bB
                for kc in range(NC):
                    b.emit("tensor", "matmul", ps[:, 0:128],
                           self.XH[:, kc * T + tt * 128: kc * T + (tt + 1) * 128],
                           self.ab.view(wsl_off + 2 * 1024 + kc * 128, 128),
                           start=(kc == 0), stop=(kc == NC - 1),
                           reads=[wslB, self.XHBufs[kc]], writes=[psB])
                b.emit("vector", "tensor_add",
                       out=self.ab.view(va_off, 16 * 130)[:, tt * 130: tt * 130 + 64],
                       in0=ps[:, 0:64], in1=self.af.view(vb_off, 128)[:, 0:64],
                       reads=[psB, vbB], writes=[vaB])
                b.emit("vector", "tensor_add",
                       out=self.ab.view(va_off, 16 * 130)[:, tt * 130 + 65: tt * 130 + 129],
                       in0=ps[:, 64:128], in1=self.af.view(vb_off, 128)[:, 64:128],
                       reads=[psB, vbB], writes=[vaB])
            build_toeplitz(ext_rev, h1, 128, u1_off, u1B)
            build_toeplitz(ext_rev, h2, 128, u2_off, u2B)
            build_toeplitz(ext, h1, 128, w1_off, w1B)
            build_toeplitz(ext, h2, 128, w2_off, w2B)
            for i, h in enumerate((h1, h2)):
                b.emit("sync", "dma_start",
                       out=self.cols[:, c_lo + i:c_lo + i + 1],
                       in_=AP(rel_raw.tensor, h * 257 + 0, [[0, 128], [1, 1]]),
                       reads=[self.colsB], writes=[self.colsB])
                b.emit("sync", "dma_start",
                       out=self.cols[:, c_m + i:c_m + i + 1],
                       in_=AP(rel_raw.tensor, h * 257 + 256, [[0, 128], [1, 1]]),
                       reads=[self.colsB], writes=[self.colsB])

            for hi, (qoffh, qBh, koffh, kBh, uoff, uB, woff, wB) in enumerate((
                    (q1_off, q1B, k1_off, k1B, u1_off, u1B, w1_off, w1B),
                    (q2_off, q2B, k2_off, k2B, u2_off, u2B, w2_off, w2B))):
                h = 2 * hp + hi
                for qt in range(16):
                    for kb in range(NTB):
                        b.emit("tensor", "matmul", self.ps_big[:, kb * TB:(kb + 1) * TB],
                               self.ab.view(qoffh + qt * 128, 128, 0, 64),
                               self.ab.view(koffh + kb * TB, TB, 0, 64),
                               start=True, stop=True,
                               reads=[qBh, kBh], writes=[self.ps_bigQ[kb]])
                    near = [kb for kb in range(NTB)
                            if -128 <= 128 * qt - 512 * kb <= 512]
                    lows = [kb for kb in range(NTB) if 128 * qt - 512 * kb < -128]
                    highs = [kb for kb in range(NTB) if 128 * qt - 512 * kb > 512]
                    parts = []
                    for j, kb in enumerate(near):
                        d1 = 128 * qt - 512 * kb
                        o_m = 512 - d1
                        b.emit("vector", "tensor_add",
                               out=self.af.view(nr_off + j * TB, TB),
                               in0=self.ps_big[:, kb * TB:(kb + 1) * TB],
                               in1=self.ab.view(uoff + o_m, TB),
                               reads=[self.ps_bigQ[kb], uB], writes=[nrB])
                    b.emit("vector", "reduce_max",
                           out=self.cols[:, c_m + 2:c_m + 3],
                           in_=self.af.view(nr_off, len(near) * TB),
                           axis=AX.X, reads=[nrB], writes=[self.colsB])
                    for kbs, ccol in ((lows, c_lo + hi), (highs, c_m + hi)):
                        if not kbs:
                            continue
                        kb0, kb1 = min(kbs), max(kbs) + 1
                        b.emit("vector", "reduce_max",
                               out=self.cols[:, c_m + 3:c_m + 4],
                               in_=self.ps_big[:, kb0 * TB:kb1 * TB],
                               axis=AX.X, reads=self.ps_bigQ[kb0:kb1],
                               writes=[self.colsB])
                        b.emit("vector", "tensor_add",
                               out=self.cols[:, c_m + 3:c_m + 4],
                               in0=self.cols[:, c_m + 3:c_m + 4],
                               in1=self.cols[:, ccol:ccol + 1],
                               reads=[self.colsB], writes=[self.colsB])
                        b.emit("vector", "tensor_max",
                               out=self.cols[:, c_m + 2:c_m + 3],
                               in0=self.cols[:, c_m + 2:c_m + 3],
                               in1=self.cols[:, c_m + 3:c_m + 4],
                               reads=[self.colsB], writes=[self.colsB])
                    b.emit("vector", "tensor_scalar",
                           out=self.wsmall[:, qt:qt + 1],
                           in0=self.cols[:, c_m + 2:c_m + 3],
                           scalar1=-1.0, scalar2=None, op0=ALU.mult,
                           reads=[self.colsB], writes=[self.wsmallB])
                self.flush_aug(qoffh, qBh)
                for qb in range(NTB):
                    for kc in range(16):
                        d2 = 512 * qb - 128 * kc
                        ps = self.ps_a if kc % 2 == 0 else self.ps_b
                        psB = self.ps_aB if kc % 2 == 0 else self.ps_bB
                        b.emit("tensor", "matmul", ps[:],
                               self.ab.view(koffh + kc * 128, 128, 0, 65),
                               self.ab.view(qoffh + qb * TB, TB, 0, 65),
                               start=True, stop=True,
                               reads=[kBh, qBh], writes=[psB])
                        kw = {}
                        if d2 <= -639:
                            kw["bias"] = self.cols[:, c_lo + hi:c_lo + hi + 1]
                        elif d2 >= 255:
                            kw["bias"] = self.cols[:, c_m + hi:c_m + hi + 1]
                        else:
                            o2 = d2 + 512
                            b.emit("vector", "tensor_add",
                                   out=ps[:], in0=ps[:],
                                   in1=self.ab.view(woff + o2, TB),
                                   reads=[psB, wB], writes=[psB])
                        pslot = pt_off if kc % 2 == 0 else pt_off + TB
                        b.emit("scalar", "activation",
                               out=self.ab.view(pslot, TB),
                               in_=ps[:], func=AF.Exp,
                               reads=[psB, self.colsB], writes=[ptB], **kw)
                        b.emit("tensor", "matmul", self.ps_c[0:65, :],
                               self.ab.view(va_off, 16 * 130)[:, kc * 130 + hi * 65: kc * 130 + hi * 65 + 65],
                               self.ab.view(pslot, TB),
                               start=(kc == 0), stop=(kc == 15),
                               reads=[vaB, ptB], writes=[self.ps_cB])
                    b.emit("vector", "reciprocal",
                           out=self.srow1[0:1, 0:TB],
                           in_=self.ps_c[64:65, :],
                           reads=[self.ps_cB], writes=[self.srow1B])
                    self.bcast_row(self.ps_d, self.ps_dB,
                                   self.srow1[0:1, 0:TB], self.srow1B, TB)
                    oc = hp
                    b.emit("scalar", "activation",
                           out=self.ab.view(pt_off, TB, 0, 64),
                           in_=self.ps_c[0:64, :], func=AF.Copy,
                           reads=[self.ps_cB], writes=[ptB])
                    b.emit("vector", "tensor_mul",
                           out=self.OF[hi * 64:(hi + 1) * 64,
                                       oc * T + qb * TB: oc * T + (qb + 1) * TB],
                           in0=self.ab.view(pt_off, TB, 0, 64),
                           in1=self.ps_d[0:64, :],
                           reads=[ptB, self.ps_dB], writes=[self.OFBufs[oc]])
        self.ab.pop()
        self.af.pop()
        self.b.barrier()
        self.ab.push()
        self.af.push()
        st_off, stB = self.af.take(1024, "wstage2")
        wo_off, woB = self.ab.take(8 * 1024, "l0_wo")
        self.load_wt_bf16(wo, D, D, wo_off, woB, st_off, stB)
        for oc in range(NC):
            for tb in range(NTB):
                ps = self.ps_a if (oc + tb) % 2 == 0 else self.ps_b
                psB = self.ps_aB if (oc + tb) % 2 == 0 else self.ps_bB
                for kc in range(NC):
                    b.emit("tensor", "matmul", ps[:],
                           self.ab.view(wo_off + kc * 1024, 1024)[:, oc * 128:(oc + 1) * 128],
                           self.OF[:, kc * T + tb * TB: kc * T + (tb + 1) * TB],
                           start=(kc == 0), stop=(kc == NC - 1),
                           reads=[woB, self.OFBufs[kc]], writes=[psB])
                b.emit("vector", "scalar_tensor_tensor",
                       out=self.X[:, oc * T + tb * TB: oc * T + (tb + 1) * TB],
                       in0=ps[:], scalar=self.cols[:, cb_o + oc:cb_o + oc + 1],
                       in1=self.X[:, oc * T + tb * TB: oc * T + (tb + 1) * TB],
                       op0=ALU.add, op1=ALU.add,
                       reads=[psB, self.colsB, self.XBufs[oc]],
                       writes=[self.XBufs[oc]])
        self.ab.pop()
        self.af.pop()


    def l0_ffn(self):
        b = self.b
        self.layernorm()
        rails = self.din("l0f_rails", (4, 64, 16, 16))
        up_t = self.din("l0f_up_t", (4, 64, 16, 64))
        wc_t = self.din("l0f_wc_t", (16384, 1024))
        self.ab.push()
        self.af.push()
        OFall = self.b.buf("OFall")
        h_off, hB = self.ab.take(16 * TB, "l0f_h")
        b1_off, b1B = self.ab.take(4 * 128, "l0f_b1")
        b2_off, b2B = self.ab.take(16 * 128, "l0f_b2")
        y_off, yB = self.ab.take(4 * TB, "l0f_y")
        sg_off, sgB = self.ab.take(64, "l0f_stg_bf")
        st_off, stB = self.af.take(1024, "l0f_stage")
        r_off, rB = self.af.take(64, "l0f_railstage")

        for ig in range(8):
            h = ig // 2
            half = ig % 2
            for kc in range(16):
                b.emit("gpsimd", "dma_start",
                       out=self.OF[:, kc * 1024:(kc + 1) * 1024],
                       in_=wc_t[ig * 2048 + kc * 128: ig * 2048 + (kc + 1) * 128, :],
                       reads=[OFall], writes=[OFall])
            for i in range(4):
                yc = 4 * half + i
                b.emit("gpsimd", "memset", self.ab.view(b1_off + i * 128, 128),
                       0.0, writes=[b1B])
                b.emit("sync", "dma_start",
                       out=self.af.view(r_off, 16),
                       in_=AP(rails.tensor, (h * 64 + 8 * yc) * 256,
                              [[16, 128], [1, 16]]),
                       reads=[rB], writes=[rB])
                b.emit("scalar", "copy",
                       out=self.ab.view(sg_off, 64)[:, 0:16],
                       in_=self.af.view(r_off, 64)[:, 0:16],
                       reads=[rB], writes=[sgB])
                for rr in range(8):
                    b.emit("gpsimd", "dma_start",
                           out=self.ab.view(b1_off + i * 128 + rr * 16, 16,
                                            rr * 16, rr * 16 + 16),
                           in_=self.ab.view(sg_off, 16, rr * 16, rr * 16 + 16),
                           reads=[sgB], writes=[b1B])
            for zi in range(16):
                zc = ig * 16 + zi
                oc_h = zc % 32
                r0 = 2 * oc_h
                b.emit("gpsimd", "memset", self.ab.view(b2_off + zi * 128, 128),
                       0.0, writes=[b2B])
                b.emit("sync", "dma_start",
                       out=self.af.view(r_off, 64)[0:32, :],
                       in_=AP(up_t.tensor, (h * 64 + r0) * 1024,
                              [[64, 32], [1, 64]]),
                       reads=[rB], writes=[rB])
                b.emit("scalar", "copy",
                       out=self.ab.view(sg_off, 64)[0:32, :],
                       in_=self.af.view(r_off, 64)[0:32, :],
                       reads=[rB], writes=[sgB])
                for rr in range(2):
                    b.emit("gpsimd", "dma_start",
                           out=self.ab.view(b2_off + zi * 128 + rr * 64, 64,
                                            ((r0 + rr) % 8) * 16,
                                            ((r0 + rr) % 8) * 16 + 16),
                           in_=self.ab.view(sg_off, 64, rr * 16, rr * 16 + 16),
                           reads=[sgB], writes=[b2B])
            for tb in range(NTB):
                for i in range(4):
                    yc = 4 * half + i
                    b.emit("tensor", "matmul", self.ps_a[:],
                           self.ab.view(b1_off + i * 128, 128),
                           self.XH[:, yc * T + tb * TB: yc * T + (tb + 1) * TB],
                           start=True, stop=True,
                           reads=[b1B, self.XHBufs[yc]], writes=[self.ps_aB])
                    b.emit("scalar", "copy",
                           out=self.ab.view(y_off + i * TB, TB),
                           in_=self.ps_a[:], reads=[self.ps_aB], writes=[yB])
                for zi in range(16):
                    zc = ig * 16 + zi
                    i = (zc % 32) // 4 - 4 * half
                    b.emit("tensor", "matmul", self.ps_b[:],
                           self.ab.view(b2_off + zi * 128, 128),
                           self.ab.view(y_off + i * TB, TB),
                           start=True, stop=True,
                           reads=[b2B, yB], writes=[self.ps_bB])
                    b.emit("scalar", "activation",
                           out=self.ab.view(h_off + zi * TB, TB),
                           in_=self.ps_b[:], func=AF.Gelu,
                           reads=[self.ps_bB], writes=[hB])
                for oc in range(8):
                    ps = self.ps_c if oc % 2 == 0 else self.ps_d
                    psB = self.ps_cB if oc % 2 == 0 else self.ps_dB
                    for zi in range(16):
                        b.emit("tensor", "matmul", ps[:],
                               self.OF[:, zi * 1024 + oc * 128: zi * 1024 + (oc + 1) * 128],
                               self.ab.view(h_off + zi * TB, TB),
                               start=(zi == 0), stop=(zi == 15),
                               reads=[OFall, hB], writes=[psB])
                    b.emit("vector", "tensor_add",
                           out=self.X[:, oc * T + tb * TB: oc * T + (tb + 1) * TB],
                           in0=self.X[:, oc * T + tb * TB: oc * T + (tb + 1) * TB],
                           in1=ps[:],
                           reads=[self.XBufs[oc], psB], writes=[self.XBufs[oc]])
        self.ab.pop()
        self.af.pop()

    def l1_attn(self):
        b = self.b
        self.layernorm()
        self.layernorm_of_xh()
        wqkv = self.din("l1_w_qkv", (D, 3 * D))
        wout = self.din("l1_w_out", (D, D))
        cb_o = self.col(NC); self.load_cols("l1_b_out", (NC, 128), cb_o, NC)

        self.ab.push()
        self.af.push()
        wsl_off, wslB = self.ab.take(3 * 1024, "l1_wsl")
        q1_off, q1B = self.ab.take(T, "Q1")
        q2_off, q2B = self.ab.take(T, "Q2")
        k1_off, k1B = self.ab.take(T, "K1")
        k2_off, k2B = self.ab.take(T, "K2")
        va_off, vaB = self.ab.take(16 * 130, "VA")
        pt_off, ptB = self.ab.take(2 * TB, "PT")
        st_off, stB = self.af.take(1024, "wstage")
        c_m = self.col(2)

        b.emit("gpsimd", "memset", self.ab.view(k1_off, T, 64, 65), 1.0, writes=[k1B])
        b.emit("gpsimd", "memset", self.ab.view(k2_off, T, 64, 65), 1.0, writes=[k2B])
        for tt in range(16):
            b.emit("gpsimd", "memset",
                   self.ab.view(va_off + tt * 130 + 64, 1), 1.0, writes=[vaB])
            b.emit("gpsimd", "memset",
                   self.ab.view(va_off + tt * 130 + 129, 1), 1.0, writes=[vaB])

        for hp in range(8):
            for wi, colbase in enumerate((hp * 128, D + hp * 128, 2 * D + hp * 128)):
                for kc in range(NC):
                    b.emit("gpsimd", "dma_start",
                           out=self.ab.view(wsl_off + wi * 1024 + kc * 128, 128),
                           in_=wqkv[kc * 128:(kc + 1) * 128, colbase:colbase + 128],
                           reads=[wslB], writes=[wslB])
            for (dst_off, dstB, head_half, wi, scl) in (
                    (q1_off, q1B, 0, 0, 0.125), (q2_off, q2B, 1, 0, 0.125),
                    (k1_off, k1B, 0, 1, None), (k2_off, k2B, 1, 1, None)):
                for tb in range(NTB):
                    ps = self.ps_a if tb % 2 == 0 else self.ps_b
                    psB = self.ps_aB if tb % 2 == 0 else self.ps_bB
                    for kc in range(NC):
                        b.emit("tensor", "matmul", ps[:],
                               self.ab.view(wsl_off + wi * 1024 + kc * 128, 128),
                               self.XH[:, kc * T + tb * TB: kc * T + (tb + 1) * TB],
                               start=(kc == 0), stop=(kc == NC - 1),
                               reads=[wslB, self.XHBufs[kc]], writes=[psB])
                    if scl is not None:
                        b.emit("vector", "tensor_scalar",
                               out=self.ab.view(dst_off + tb * TB, TB, 0, 64),
                               in0=ps[head_half * 64:(head_half + 1) * 64, :],
                               scalar1=scl, scalar2=None, op0=ALU.mult,
                               reads=[psB], writes=[dstB])
                    else:
                        b.emit("scalar", "copy",
                               out=self.ab.view(dst_off + tb * TB, TB, 0, 64),
                               in_=ps[head_half * 64:(head_half + 1) * 64, :],
                               reads=[psB], writes=[dstB])
            for tt in range(16):
                ps = self.ps_a if tt % 2 == 0 else self.ps_b
                psB = self.ps_aB if tt % 2 == 0 else self.ps_bB
                for kc in range(NC):
                    b.emit("tensor", "matmul", ps[:, 0:128],
                           self.XH[:, kc * T + tt * 128: kc * T + (tt + 1) * 128],
                           self.ab.view(wsl_off + 2 * 1024 + kc * 128, 128),
                           start=(kc == 0), stop=(kc == NC - 1),
                           reads=[wslB, self.XHBufs[kc]], writes=[psB])
                b.emit("scalar", "copy",
                       out=self.ab.view(va_off, 16 * 130)[:, tt * 130: tt * 130 + 64],
                       in_=ps[:, 0:64], reads=[psB], writes=[vaB])
                b.emit("scalar", "copy",
                       out=self.ab.view(va_off, 16 * 130)[:, tt * 130 + 65: tt * 130 + 129],
                       in_=ps[:, 64:128], reads=[psB], writes=[vaB])
            for hi, (qoffh, qBh, koffh, kBh) in enumerate((
                    (q1_off, q1B, k1_off, k1B), (q2_off, q2B, k2_off, k2B))):
                for qt in range(16):
                    wq_allow = (qt // 2 + 1) * 256
                    nkb = (wq_allow + TB - 1) // TB
                    for kb in range(nkb):
                        w = min(TB, wq_allow - kb * TB)
                        b.emit("tensor", "matmul",
                               self.ps_big[:, kb * TB: kb * TB + w],
                               self.ab.view(qoffh + qt * 128, 128, 0, 64),
                               self.ab.view(koffh + kb * TB, w, 0, 64),
                               start=True, stop=True,
                               reads=[qBh, kBh], writes=[self.ps_bigQ[kb]])
                    b.emit("vector", "reduce_max",
                           out=self.cols[:, c_m:c_m + 1],
                           in_=self.ps_big[:, 0:wq_allow],
                           axis=AX.X, reads=self.ps_bigQ[0:nkb], writes=[self.colsB])
                    b.emit("vector", "tensor_scalar",
                           out=self.wsmall[:, qt:qt + 1],
                           in0=self.cols[:, c_m:c_m + 1],
                           scalar1=-1.0, scalar2=None, op0=ALU.mult,
                           reads=[self.colsB], writes=[self.wsmallB])
                self.flush_aug(qoffh, qBh)
                for qb in range(NTB):
                    nkc = 4 * qb + 4
                    for kc in range(nkc):
                        half_chunk = (kc >= 4 * qb + 2)
                        ps = self.ps_a if kc % 2 == 0 else self.ps_b
                        psB = self.ps_aB if kc % 2 == 0 else self.ps_bB
                        pslot = pt_off if kc % 2 == 0 else pt_off + TB
                        if half_chunk:
                            b.emit("gpsimd", "memset",
                                   self.ab.view(pslot, 256), 0.0, writes=[ptB])
                            b.emit("tensor", "matmul", ps[:, 0:256],
                                   self.ab.view(koffh + kc * 128, 128, 0, 65),
                                   self.ab.view(qoffh + qb * TB + 256, 256, 0, 65),
                                   start=True, stop=True,
                                   reads=[kBh, qBh], writes=[psB])
                            b.emit("scalar", "activation",
                                   out=self.ab.view(pslot + 256, 256),
                                   in_=ps[:, 0:256], func=AF.Exp,
                                   reads=[psB], writes=[ptB])
                        else:
                            b.emit("tensor", "matmul", ps[:],
                                   self.ab.view(koffh + kc * 128, 128, 0, 65),
                                   self.ab.view(qoffh + qb * TB, TB, 0, 65),
                                   start=True, stop=True,
                                   reads=[kBh, qBh], writes=[psB])
                            b.emit("scalar", "activation",
                                   out=self.ab.view(pslot, TB),
                                   in_=ps[:], func=AF.Exp,
                                   reads=[psB], writes=[ptB])
                        b.emit("tensor", "matmul", self.ps_c[0:65, :],
                               self.ab.view(va_off, 16 * 130)[:, kc * 130 + hi * 65: kc * 130 + hi * 65 + 65],
                               self.ab.view(pslot, TB),
                               start=(kc == 0), stop=(kc == nkc - 1),
                               reads=[vaB, ptB], writes=[self.ps_cB])
                    b.emit("vector", "reciprocal",
                           out=self.srow1[0:1, 0:TB], in_=self.ps_c[64:65, :],
                           reads=[self.ps_cB], writes=[self.srow1B])
                    self.bcast_row(self.ps_d, self.ps_dB,
                                   self.srow1[0:1, 0:TB], self.srow1B, TB)
                    b.emit("scalar", "activation",
                           out=self.ab.view(pt_off, TB, 0, 64),
                           in_=self.ps_c[0:64, :], func=AF.Copy,
                           reads=[self.ps_cB], writes=[ptB])
                    b.emit("vector", "tensor_mul",
                           out=self.OF[hi * 64:(hi + 1) * 64,
                                       hp * T + qb * TB: hp * T + (qb + 1) * TB],
                           in0=self.ab.view(pt_off, TB, 0, 64),
                           in1=self.ps_d[0:64, :],
                           reads=[ptB, self.ps_dB], writes=[self.OFBufs[hp]])
        self.ab.pop()
        self.af.pop()
        self.b.barrier()
        self.ab.push()
        self.af.push()
        st_off, stB = self.af.take(1024, "wstage2")
        wo_off, woB = self.ab.take(8 * 1024, "l1_wo")
        self.load_wt_bf16(wout, D, D, wo_off, woB, st_off, stB)
        for oc in range(NC):
            for tb in range(NTB):
                ps = self.ps_a if (oc + tb) % 2 == 0 else self.ps_b
                psB = self.ps_aB if (oc + tb) % 2 == 0 else self.ps_bB
                for kc in range(NC):
                    b.emit("tensor", "matmul", ps[:],
                           self.ab.view(wo_off + kc * 1024, 1024)[:, oc * 128:(oc + 1) * 128],
                           self.OF[:, kc * T + tb * TB: kc * T + (tb + 1) * TB],
                           start=(kc == 0), stop=(kc == NC - 1),
                           reads=[woB, self.OFBufs[kc]], writes=[psB])
                b.emit("vector", "scalar_tensor_tensor",
                       out=self.X[:, oc * T + tb * TB: oc * T + (tb + 1) * TB],
                       in0=ps[:], scalar=self.cols[:, cb_o + oc:cb_o + oc + 1],
                       in1=self.X[:, oc * T + tb * TB: oc * T + (tb + 1) * TB],
                       op0=ALU.add, op1=ALU.add,
                       reads=[psB, self.colsB, self.XBufs[oc]],
                       writes=[self.XBufs[oc]])
        self.ab.pop()
        self.af.pop()

    def layernorm_of_xh(self):
        cg = self.col(NC); self.load_cols("l1_g", (NC, 128), cg, NC)
        cbb = self.col(NC); self.load_cols("l1_b", (NC, 128), cbb, NC)
        self.ln_inner(self.XH, self.XHBufs, cg, cbb)

    def ln_inner(self, dst_t, dst_bufs, cg=None, cbb=None):
        b = self.b
        self.af.push()
        self.ab.push()
        sqoff, sqB = self.ab.take(TB, "ln2_sq")
        for tb in range(NTB):
            for c in range(NC):
                b.emit("scalar", "activation",
                       out=self.ab.view(sqoff, TB),
                       in_=self.XH[:, c * T + tb * TB: c * T + (tb + 1) * TB],
                       func=AF.Square,
                       reads=[self.XHBufs[c]], writes=[sqB])
                b.emit("tensor", "matmul", self.ps_c[0:1, :],
                       self.ones_col_bf[:],
                       self.XH[:, c * T + tb * TB: c * T + (tb + 1) * TB],
                       start=(c == 0), stop=(c == NC - 1),
                       reads=[self.XHBufs[c], self.constB], writes=[self.ps_cB])
                b.emit("tensor", "matmul", self.ps_d[0:1, :],
                       self.ones_col_bf[:], self.ab.view(sqoff, TB),
                       start=(c == 0), stop=(c == NC - 1),
                       reads=[sqB, self.constB], writes=[self.ps_dB])
            mr = self.m_row[0:1, :]
            rr = self.r_row[0:1, :]
            s1 = self.srow1[0:1, :]
            s2 = self.srow2[0:1, :]
            b.emit("vector", "tensor_scalar", out=mr, in0=self.ps_c[0:1, :],
                   scalar1=1.0 / D, scalar2=None, op0=ALU.mult,
                   reads=[self.ps_cB], writes=[self.m_rowB])
            b.emit("vector", "tensor_scalar", out=s1, in0=self.ps_d[0:1, :],
                   scalar1=1.0 / D, scalar2=None, op0=ALU.mult,
                   reads=[self.ps_dB], writes=[self.srow1B])
            b.emit("vector", "tensor_mul", out=s2, in0=mr, in1=mr,
                   reads=[self.m_rowB], writes=[self.srow2B])
            b.emit("vector", "tensor_sub", out=s1, in0=s1, in1=s2,
                   reads=[self.srow1B, self.srow2B], writes=[self.srow1B])
            b.emit("scalar", "activation", out=s1, in_=s1, func=AF.Sqrt,
                   bias=self.cols[0:1, self.c_eps:self.c_eps + 1],
                   reads=[self.srow1B, self.colsB], writes=[self.srow1B])
            b.emit("vector", "reciprocal", out=rr, in_=s1,
                   reads=[self.srow1B], writes=[self.r_rowB])
            self.bcast_row(self.ps_c, self.ps_cB, mr, self.m_rowB, TB)
            self.bcast_row(self.ps_d, self.ps_dB, rr, self.r_rowB, TB)
            for c in range(NC):
                sl = slice(c * T + tb * TB, c * T + (tb + 1) * TB)
                b.emit("vector", "tensor_sub",
                       out=self.ab.view(sqoff, TB),
                       in0=self.XH[:, sl], in1=self.ps_c[:],
                       reads=[self.XHBufs[c], self.ps_cB], writes=[sqB])
                b.emit("vector", "tensor_mul",
                       out=dst_t[:, sl],
                       in0=self.ab.view(sqoff, TB), in1=self.ps_d[:],
                       reads=[sqB, self.ps_dB], writes=[dst_bufs[c]])
                if cg is not None:
                    if cbb is not None:
                        b.emit("vector", "tensor_scalar",
                               out=dst_t[:, sl], in0=dst_t[:, sl],
                               scalar1=self.cols[:, cg + c:cg + c + 1],
                               scalar2=self.cols[:, cbb + c:cbb + c + 1],
                               op0=ALU.mult, op1=ALU.add,
                               reads=[dst_bufs[c], self.colsB], writes=[dst_bufs[c]])
                    else:
                        b.emit("vector", "tensor_scalar",
                               out=dst_t[:, sl], in0=dst_t[:, sl],
                               scalar1=self.cols[:, cg + c:cg + c + 1],
                               scalar2=None, op0=ALU.mult,
                               reads=[dst_bufs[c], self.colsB], writes=[dst_bufs[c]])
        self.af.pop()
        self.ab.pop()
        self.b.barrier()

    def l1_ffn(self):
        b = self.b
        self.layernorm()
        wr_t = self.din("l1f_wr_t", (D, D))
        wk_t = self.din("l1f_wk_t", (D, D))
        wv_t = self.din("l1f_wv_t", (D, D))
        cmu_r = self.col(NC); self.load_cols("l1f_mu_r", (NC, 128), cmu_r, NC)
        cmu_k = self.col(NC); self.load_cols("l1f_mu_k", (NC, 128), cmu_k, NC)
        c1m_r = self.col(NC)
        c1m_k = self.col(NC)
        for src, dst in ((cmu_r, c1m_r), (cmu_k, c1m_k)):
            b.emit("vector", "tensor_scalar",
                   out=self.cols[:, dst:dst + NC], in0=self.cols[:, src:src + NC],
                   scalar1=-1.0, scalar2=1.0, op0=ALU.mult, op1=ALU.add,
                   reads=[self.colsB], writes=[self.colsB])
        self.ab.push()
        self.af.push()
        xr_off, xrB = self.ab.take(NC * T, "l1f_xr")
        wsl_off, wslB = self.ab.take(2048, "l1f_wsl")
        st_off, stB = self.af.take(1024, "wstage")
        tm_off, tmB = self.af.take(T, "shift_tmp")
        OFall = self.b.buf("l1f_OFall")

        def token_shift(dst_off, dstB, cmu, c1m):
            for c in range(NC):
                b.emit("vector", "tensor_scalar",
                       out=self.af.view(tm_off, T),
                       in0=self.XH[:, c * T:(c + 1) * T],
                       scalar1=self.cols[:, cmu + c:cmu + c + 1], scalar2=None,
                       op0=ALU.mult,
                       reads=[self.XHBufs[c], self.colsB], writes=[tmB])
                b.emit("vector", "scalar_tensor_tensor",
                       out=self.ab.view(dst_off + c * T + 1, T - 1),
                       in0=self.XH[:, c * T: c * T + T - 1],
                       scalar=self.cols[:, c1m + c:c1m + c + 1],
                       in1=self.af.view(tm_off + 1, T - 1),
                       op0=ALU.mult, op1=ALU.add,
                       reads=[self.XHBufs[c], self.colsB, tmB], writes=[dstB])
                b.emit("vector", "tensor_copy",
                       out=self.ab.view(dst_off + c * T, 1),
                       in_=self.af.view(tm_off, 1),
                       reads=[tmB], writes=[dstB])

        def proj_from(in_off, inB, wdr, dst_t, dst_wr, dst_rd, act):
            for oc in range(NC):
                dbuf = oc % 2
                for kc in range(NC):
                    b.emit("sync", "dma_start",
                           out=self.af.view(st_off, 1024)[:, dbuf * 128:(dbuf + 1) * 128],
                           in_=wdr[kc * 128:(kc + 1) * 128, oc * 128:(oc + 1) * 128],
                           reads=[stB], writes=[stB])
                    b.emit("gpsimd", "tensor_copy",
                           out=self.ab.view(wsl_off + dbuf * 1024 + kc * 128, 128),
                           in_=self.af.view(st_off, 1024)[:, dbuf * 128:(dbuf + 1) * 128],
                           reads=[stB], writes=[wslB])
                for tb in range(NTB):
                    ps = self.ps_a if tb % 2 == 0 else self.ps_b
                    psB = self.ps_aB if tb % 2 == 0 else self.ps_bB
                    for kc in range(NC):
                        b.emit("tensor", "matmul", ps[:],
                               self.ab.view(wsl_off + dbuf * 1024 + kc * 128, 128),
                               self.ab.view(in_off + kc * T + tb * TB, TB),
                               start=(kc == 0), stop=(kc == NC - 1),
                               reads=[wslB, inB], writes=[psB])
                    b.emit("scalar", "activation",
                           out=dst_t[:, oc * T + tb * TB: oc * T + (tb + 1) * TB],
                           in_=ps[:], func=(AF.Sigmoid if act == "sig" else AF.Relu),
                           reads=[psB] + dst_rd, writes=[dst_wr[oc]])

        token_shift(xr_off, xrB, cmu_r, c1m_r)
        proj_from(xr_off, xrB, wr_t, self.OF, [OFall] * NC, [], "sig")
        token_shift(xr_off, xrB, cmu_k, c1m_k)
        proj_from(xr_off, xrB, wk_t, self.XH, self.XHBufs, [], "relu")
        for c in range(NC):
            sl = slice(c * T, (c + 1) * T)
            b.emit("scalar", "activation",
                   out=self.XH[:, sl], in_=self.XH[:, sl], func=AF.Square,
                   reads=[self.XHBufs[c]], writes=[self.XHBufs[c]])
        for oc in range(NC):
            dbuf = oc % 2
            for kc in range(NC):
                b.emit("gpsimd", "dma_start",
                       out=self.ab.view(wsl_off + dbuf * 1024 + kc * 128, 128),
                       in_=wv_t[kc * 128:(kc + 1) * 128, oc * 128:(oc + 1) * 128],
                       reads=[wslB], writes=[wslB])
            for tb in range(NTB):
                ps = self.ps_a if tb % 2 == 0 else self.ps_b
                psB = self.ps_aB if tb % 2 == 0 else self.ps_bB
                for kc in range(NC):
                    b.emit("tensor", "matmul", ps[:],
                           self.ab.view(wsl_off + dbuf * 1024 + kc * 128, 128),
                           self.XH[:, kc * T + tb * TB: kc * T + (tb + 1) * TB],
                           start=(kc == 0), stop=(kc == NC - 1),
                           reads=[wslB, self.XHBufs[kc]], writes=[psB])
                b.emit("vector", "tensor_mul",
                       out=self.af.view(tm_off, TB),
                       in0=ps[:],
                       in1=self.OF[:, oc * T + tb * TB: oc * T + (tb + 1) * TB],
                       reads=[psB, OFall], writes=[tmB])
                b.emit("vector", "tensor_add",
                       out=self.X[:, oc * T + tb * TB: oc * T + (tb + 1) * TB],
                       in0=self.X[:, oc * T + tb * TB: oc * T + (tb + 1) * TB],
                       in1=self.af.view(tm_off, TB),
                       reads=[self.XBufs[oc], tmB], writes=[self.XBufs[oc]])
        self.ab.pop()
        self.af.pop()

    def _rope64(self, dst_off, dstB, cs_off, sn_off, tabB, sc_off, scB, scale):
        b = self.b
        b.emit("vector", "tensor_mul",
               out=self.ab.view(sc_off, T, 0, 64),
               in0=self.ab.view(dst_off, T, 0, 64),
               in1=self.af.view(cs_off, T, 0, 64),
               reads=[dstB, tabB], writes=[scB])
        b.emit("vector", "tensor_mul",
               out=self.ab.view(sc_off + T, T, 0, 32),
               in0=self.ab.view(dst_off, T, 32, 64),
               in1=self.af.view(sn_off, T, 32, 64),
               reads=[dstB, tabB], writes=[scB])
        b.emit("vector", "tensor_mul",
               out=self.ab.view(sc_off + T, T, 32, 64),
               in0=self.ab.view(dst_off, T, 0, 32),
               in1=self.af.view(sn_off, T, 0, 32),
               reads=[dstB, tabB], writes=[scB])
        b.emit("vector", "tensor_sub",
               out=self.ab.view(dst_off, T, 0, 32),
               in0=self.ab.view(sc_off, T, 0, 32),
               in1=self.ab.view(sc_off + T, T, 0, 32),
               reads=[scB], writes=[dstB])
        b.emit("vector", "tensor_add",
               out=self.ab.view(dst_off, T, 32, 64),
               in0=self.ab.view(sc_off, T, 32, 64),
               in1=self.ab.view(sc_off + T, T, 32, 64),
               reads=[scB], writes=[dstB])
        if scale is not None:
            b.emit("vector", "tensor_scalar",
                   out=self.ab.view(dst_off, T, 0, 64),
                   in0=self.ab.view(dst_off, T, 0, 64),
                   scalar1=scale, scalar2=None, op0=ALU.mult,
                   reads=[dstB], writes=[dstB])

    def _attn_causal_core(self, hp, q1_off, q1B, q2_off, q2B, k1_off, k1B,
                          k2_off, k2B, va_off, vaB, pt_off, ptB,
                          mtri_off, mtrit_off, mtB, c_m, of_base=None):
        b = self.b
        for hi, (qoffh, qBh, koffh, kBh) in enumerate((
                (q1_off, q1B, k1_off, k1B), (q2_off, q2B, k2_off, k2B))):
            for qt in range(16):
                w_allow = 128 * qt + 128
                nkb = (w_allow + TB - 1) // TB
                for kb in range(nkb):
                    w = min(TB, w_allow - kb * TB)
                    b.emit("tensor", "matmul",
                           self.ps_big[:, kb * TB: kb * TB + w],
                           self.ab.view(qoffh + qt * 128, 128, 0, 64),
                           self.ab.view(koffh + kb * TB, w, 0, 64),
                           start=True, stop=True,
                           reads=[qBh, kBh], writes=[self.ps_bigQ[kb]])
                b.emit("vector", "tensor_add",
                       out=self.ps_big[:, w_allow - 128:w_allow],
                       in0=self.ps_big[:, w_allow - 128:w_allow],
                       in1=self.af.view(mtrit_off, 128),
                       reads=[self.ps_bigQ[nkb - 1], mtB],
                       writes=[self.ps_bigQ[nkb - 1]])
                b.emit("vector", "reduce_max",
                       out=self.cols[:, c_m:c_m + 1],
                       in_=self.ps_big[:, 0:w_allow],
                       axis=AX.X, reads=self.ps_bigQ[0:nkb], writes=[self.colsB])
                b.emit("vector", "tensor_scalar",
                       out=self.wsmall[:, qt:qt + 1],
                       in0=self.cols[:, c_m:c_m + 1],
                       scalar1=-1.0, scalar2=None, op0=ALU.mult,
                       reads=[self.colsB], writes=[self.wsmallB])
            self.flush_aug(qoffh, qBh)
            for qb in range(NTB):
                nkc = 4 * qb + 4
                for kc in range(nkc):
                    diag = kc - 4 * qb
                    ps = self.ps_a if kc % 2 == 0 else self.ps_b
                    psB = self.ps_aB if kc % 2 == 0 else self.ps_bB
                    pslot = pt_off if kc % 2 == 0 else pt_off + TB
                    if diag < 0:
                        b.emit("tensor", "matmul", ps[:],
                               self.ab.view(koffh + kc * 128, 128, 0, 65),
                               self.ab.view(qoffh + qb * TB, TB, 0, 65),
                               start=True, stop=True,
                               reads=[kBh, qBh], writes=[psB])
                        b.emit("scalar", "activation",
                               out=self.ab.view(pslot, TB),
                               in_=ps[:], func=AF.Exp,
                               reads=[psB], writes=[ptB])
                    else:
                        u0 = 128 * diag
                        if u0 > 0:
                            b.emit("gpsimd", "memset",
                                   self.ab.view(pslot, u0), 0.0, writes=[ptB])
                        b.emit("tensor", "matmul", ps[:, u0:TB],
                               self.ab.view(koffh + kc * 128, 128, 0, 65),
                               self.ab.view(qoffh + qb * TB + u0, TB - u0, 0, 65),
                               start=True, stop=True,
                               reads=[kBh, qBh], writes=[psB])
                        b.emit("vector", "tensor_add",
                               out=ps[:, u0:u0 + 128],
                               in0=ps[:, u0:u0 + 128],
                               in1=self.af.view(mtri_off, 128),
                               reads=[psB, mtB], writes=[psB])
                        b.emit("scalar", "activation",
                               out=self.ab.view(pslot + u0, TB - u0),
                               in_=ps[:, u0:TB], func=AF.Exp,
                               reads=[psB], writes=[ptB])
                    b.emit("tensor", "matmul", self.ps_c[0:65, :],
                           self.ab.view(va_off, 16 * 130)[:, kc * 130 + hi * 65: kc * 130 + hi * 65 + 65],
                           self.ab.view(pslot, TB),
                           start=(kc == 0), stop=(kc == nkc - 1),
                           reads=[vaB, ptB], writes=[self.ps_cB])
                b.emit("vector", "reciprocal",
                       out=self.srow1[0:1, 0:TB], in_=self.ps_c[64:65, :],
                       reads=[self.ps_cB], writes=[self.srow1B])
                self.bcast_row(self.ps_d, self.ps_dB,
                               self.srow1[0:1, 0:TB], self.srow1B, TB)
                b.emit("scalar", "activation",
                       out=self.ab.view(pt_off, TB, 0, 64),
                       in_=self.ps_c[0:64, :], func=AF.Copy,
                       reads=[self.ps_cB], writes=[ptB])
                b.emit("vector", "tensor_mul",
                       out=self.OF[hi * 64:(hi + 1) * 64,
                                   hp * T + qb * TB: hp * T + (qb + 1) * TB],
                       in0=self.ab.view(pt_off, TB, 0, 64),
                       in1=self.ps_d[0:64, :],
                       reads=[ptB, self.ps_dB], writes=[self.OFBufs[hp]])

    def l2_attn(self):
        b = self.b
        self.layernorm()
        wq = self.din("l2_wq", (D, D))
        wkv = self.din("l2_wkv", (D, 2 * D))
        wo = self.din("l2_wo", (D, D))
        cb_q = self.col(NC); self.load_cols("l2_bq", (NC, 128), cb_q, NC)
        cb_k = self.col(NC); self.load_cols("l2_bk", (NC, 128), cb_k, NC)
        cb_o = self.col(NC); self.load_cols("l2_bo", (NC, 128), cb_o, NC)
        bv_dram = self.din("l2_bv", (D,))
        cos_dr = self.din("l2_cos", (64, T))
        sin_dr = self.din("l2_sin", (64, T))
        mtri_dr = self.din("l2_mtri", (128, 128))
        mtrit_dr = self.din("l2_mtri_t", (128, 128))
        b.emit("vector", "tensor_scalar",
               out=self.cols[:, cb_q:cb_q + NC], in0=self.cols[:, cb_q:cb_q + NC],
               scalar1=0.125, scalar2=None, op0=ALU.mult,
               reads=[self.colsB], writes=[self.colsB])

        self.ab.push()
        self.af.push()
        wsl_off, wslB = self.ab.take(3 * 1024, "l2_wsl")
        q1_off, q1B = self.ab.take(T, "Q1")
        q2_off, q2B = self.ab.take(T, "Q2")
        k1_off, k1B = self.ab.take(T, "K1")
        k2_off, k2B = self.ab.take(T, "K2")
        va_off, vaB = self.ab.take(16 * 130, "VA")
        pt_off, ptB = self.ab.take(2 * TB, "PT")
        rs_off, rsB = self.ab.take(2 * T, "rope_scr")
        st_off, stB = self.af.take(1024, "wstage")
        cs_off, tabB = self.af.take(T, "cos")
        sn_off, _tab2 = self.af.take(T, "sin")
        mtri_off, mtB = self.af.take(128, "mtri")
        mtrit_off, _mt2 = self.af.take(128, "mtrit")
        vb_off, vbB = self.af.take(128, "vbias")
        c_m = self.col(2)
        b.emit("sync", "dma_start", out=self.af.view(cs_off, T)[0:64, :],
               in_=cos_dr, reads=[tabB], writes=[tabB])
        b.emit("sync", "dma_start", out=self.af.view(sn_off, T)[0:64, :],
               in_=sin_dr, reads=[tabB], writes=[tabB])
        b.emit("sync", "dma_start", out=self.af.view(mtri_off, 128),
               in_=mtri_dr, reads=[mtB], writes=[mtB])
        b.emit("sync", "dma_start", out=self.af.view(mtrit_off, 128),
               in_=mtrit_dr, reads=[mtB], writes=[mtB])

        b.emit("gpsimd", "memset", self.ab.view(k1_off, T, 64, 65), 1.0, writes=[k1B])
        b.emit("gpsimd", "memset", self.ab.view(k2_off, T, 64, 65), 1.0, writes=[k2B])
        for tt in range(16):
            b.emit("gpsimd", "memset",
                   self.ab.view(va_off + tt * 130 + 64, 1), 1.0, writes=[vaB])
            b.emit("gpsimd", "memset",
                   self.ab.view(va_off + tt * 130 + 129, 1), 1.0, writes=[vaB])

        for hp in range(8):
            for wi, (wdr, colbase) in enumerate(
                    ((wq, hp * 128), (wkv, hp * 128), (wkv, D + hp * 128))):
                for kc in range(NC):
                    b.emit("gpsimd", "dma_start",
                           out=self.ab.view(wsl_off + wi * 1024 + kc * 128, 128),
                           in_=wdr[kc * 128:(kc + 1) * 128, colbase:colbase + 128],
                           reads=[wslB], writes=[wslB])
            b.emit("sync", "dma_start",
                   out=self.af.view(vb_off, 128),
                   in_=AP(bv_dram.tensor, hp * 128, [[0, 128], [1, 128]]),
                   reads=[vbB], writes=[vbB])
            for (dst_off, dstB, head_half, wi, coff) in (
                    (q1_off, q1B, 0, 0, cb_q), (q2_off, q2B, 1, 0, cb_q),
                    (k1_off, k1B, 0, 1, cb_k), (k2_off, k2B, 1, 1, cb_k)):
                for tb in range(NTB):
                    ps = self.ps_a if tb % 2 == 0 else self.ps_b
                    psB = self.ps_aB if tb % 2 == 0 else self.ps_bB
                    for kc in range(NC):
                        b.emit("tensor", "matmul", ps[:],
                               self.ab.view(wsl_off + wi * 1024 + kc * 128, 128),
                               self.XH[:, kc * T + tb * TB: kc * T + (tb + 1) * TB],
                               start=(kc == 0), stop=(kc == NC - 1),
                               reads=[wslB, self.XHBufs[kc]], writes=[psB])
                    scl = 0.125 if coff == cb_q else None
                    if scl is not None:
                        b.emit("vector", "tensor_scalar",
                               out=self.ab.view(dst_off + tb * TB, TB, 0, 64),
                               in0=ps[head_half * 64:(head_half + 1) * 64, :],
                               scalar1=scl,
                               scalar2=self.cols[head_half * 64:(head_half + 1) * 64,
                                                 coff + hp:coff + hp + 1],
                               op0=ALU.mult, op1=ALU.add,
                               reads=[psB, self.colsB], writes=[dstB])
                    else:
                        b.emit("vector", "tensor_scalar",
                               out=self.ab.view(dst_off + tb * TB, TB, 0, 64),
                               in0=ps[head_half * 64:(head_half + 1) * 64, :],
                               scalar1=self.cols[head_half * 64:(head_half + 1) * 64,
                                                 coff + hp:coff + hp + 1],
                               scalar2=None, op0=ALU.add,
                               reads=[psB, self.colsB], writes=[dstB])
            for doff, dB in ((q1_off, q1B), (q2_off, q2B), (k1_off, k1B), (k2_off, k2B)):
                self._rope64(doff, dB, cs_off, sn_off, tabB, rs_off, rsB, None)
            for tt in range(16):
                ps = self.ps_a if tt % 2 == 0 else self.ps_b
                psB = self.ps_aB if tt % 2 == 0 else self.ps_bB
                for kc in range(NC):
                    b.emit("tensor", "matmul", ps[:, 0:128],
                           self.XH[:, kc * T + tt * 128: kc * T + (tt + 1) * 128],
                           self.ab.view(wsl_off + 2 * 1024 + kc * 128, 128),
                           start=(kc == 0), stop=(kc == NC - 1),
                           reads=[wslB, self.XHBufs[kc]], writes=[psB])
                b.emit("vector", "tensor_add",
                       out=self.ab.view(va_off, 16 * 130)[:, tt * 130: tt * 130 + 64],
                       in0=ps[:, 0:64], in1=self.af.view(vb_off, 128)[:, 0:64],
                       reads=[psB, vbB], writes=[vaB])
                b.emit("vector", "tensor_add",
                       out=self.ab.view(va_off, 16 * 130)[:, tt * 130 + 65: tt * 130 + 129],
                       in0=ps[:, 64:128], in1=self.af.view(vb_off, 128)[:, 64:128],
                       reads=[psB, vbB], writes=[vaB])
            self._attn_causal_core(hp, q1_off, q1B, q2_off, q2B, k1_off, k1B,
                                   k2_off, k2B, va_off, vaB, pt_off, ptB,
                                   mtri_off, mtrit_off, mtB, c_m)
        self.ab.pop()
        self.af.pop()
        self.b.barrier()
        self.ab.push()
        self.af.push()
        st_off, stB = self.af.take(1024, "wstage2")
        wo_off, woB = self.ab.take(8 * 1024, "l2_wo")
        self.load_wt_bf16(wo, D, D, wo_off, woB, st_off, stB)
        for oc in range(NC):
            for tb in range(NTB):
                ps = self.ps_a if (oc + tb) % 2 == 0 else self.ps_b
                psB = self.ps_aB if (oc + tb) % 2 == 0 else self.ps_bB
                for kc in range(NC):
                    b.emit("tensor", "matmul", ps[:],
                           self.ab.view(wo_off + kc * 1024, 1024)[:, oc * 128:(oc + 1) * 128],
                           self.OF[:, kc * T + tb * TB: kc * T + (tb + 1) * TB],
                           start=(kc == 0), stop=(kc == NC - 1),
                           reads=[woB, self.OFBufs[kc]], writes=[psB])
                b.emit("vector", "scalar_tensor_tensor",
                       out=self.X[:, oc * T + tb * TB: oc * T + (tb + 1) * TB],
                       in0=ps[:], scalar=self.cols[:, cb_o + oc:cb_o + oc + 1],
                       in1=self.X[:, oc * T + tb * TB: oc * T + (tb + 1) * TB],
                       op0=ALU.add, op1=ALU.add,
                       reads=[psB, self.colsB, self.XBufs[oc]],
                       writes=[self.XBufs[oc]])
        self.ab.pop()
        self.af.pop()

    def _rope32(self, dst_off, dstB, rows, cs2_off, sn2_off, tabB,
                p32_off, p32B, sc_off, scB):
        b = self.b
        for tb in range(NTB):
            b.emit("vector", "tensor_mul",
                   out=self.ab.view(sc_off, TB, 0, rows),
                   in0=self.ab.view(dst_off + tb * TB, TB, 0, rows),
                   in1=self.af.view(sn2_off + tb * TB, TB, 0, rows),
                   reads=[dstB, tabB], writes=[scB])
            b.emit("vector", "tensor_mul",
                   out=self.ab.view(sc_off + TB, TB, 0, rows),
                   in0=self.ab.view(dst_off + tb * TB, TB, 0, rows),
                   in1=self.af.view(cs2_off + tb * TB, TB, 0, rows),
                   reads=[dstB, tabB], writes=[scB])
            ps = self.ps_a if tb % 2 == 0 else self.ps_b
            psB = self.ps_aB if tb % 2 == 0 else self.ps_bB
            b.emit("tensor", "matmul", ps[0:rows, :],
                   self.ab.view(p32_off, 128, 0, rows)[:, 0:rows],
                   self.ab.view(sc_off, TB, 0, rows),
                   start=True, stop=True,
                   reads=[p32B, scB], writes=[psB])
            b.emit("vector", "scalar_tensor_tensor",
                   out=self.ab.view(dst_off + tb * TB, TB, 0, rows),
                   in0=ps[0:rows, :], scalar=1.0,
                   in1=self.ab.view(sc_off + TB, TB, 0, rows),
                   op0=ALU.bypass, op1=ALU.add,
                   reads=[psB, scB], writes=[dstB])

    def l2_ffn(self):
        b = self.b
        nc = self.nc
        self.layernorm()
        for c in range(NC):
            b.emit("vector", "tensor_add",
                   out=self.X[:, c * T:(c + 1) * T],
                   in0=self.X[:, c * T:(c + 1) * T],
                   in1=self.XH[:, c * T:(c + 1) * T],
                   reads=[self.XBufs[c], self.XHBufs[c]], writes=[self.XBufs[c]])
        cln1 = self.col(NC); self.load_cols("l2f_ln1", (NC, 128), cln1, NC)
        cln2 = self.col(NC); self.load_cols("l2f_ln2", (NC, 128), cln2, NC)
        OFall = self.b.buf("l2f_xn1")
        self.ln_inner(self.OF, [OFall] * NC, cln1, None)

        kv_d_t = self.din("l2f_kv_d_t", (D, 256))
        q_d_t = self.din("l2f_q_d_t", (D, 256))
        k_u_t = self.din("l2f_k_u_t", (256, D))
        q_u_t = self.din("l2f_q_u_t", (256, D))
        v_u_t = self.din("l2f_v_u_t", (256, D))
        rope_q_t = self.din("l2f_rope_q_t", (256, 512))
        rope_k_t = self.din("l2f_rope_k_t", (D, 512))
        o_w_t = self.din("l2f_o_w_t", (D, D))
        mtri_dr = self.din("l2_mtri", (128, 128))
        mtrit_dr = self.din("l2_mtri_t", (128, 128))
        cos2_dr = self.din("l2f_cos2", (128, T))
        sin2_dr = self.din("l2f_sin2", (128, T))
        p32_dr = self.din("l2f_p32t", (128, 128))
        krot_dr = nc.dram_tensor("l2f_krot_scr", [512, T], BF16, kind="Internal").ap()
        krB = self.b.buf("l2f_krotdr")

        self.ab.push()
        self.af.push()
        kvl_off, kvlB = self.ab.take(2 * T, "kvl")
        ql_off, qlB = self.ab.take(2 * T, "ql")
        p32_off, p32B = self.ab.take(128, "p32")
        cs2_off, tab2B = self.af.take(T, "cos2")
        sn2_off, _tb2 = self.af.take(T, "sin2")
        mtri_off, mtB = self.af.take(128, "mtri")
        mtrit_off, _m2 = self.af.take(128, "mtrit")
        b.emit("gpsimd", "dma_start", out=self.ab.view(p32_off, 128),
               in_=p32_dr, reads=[p32B], writes=[p32B])
        b.emit("sync", "dma_start", out=self.af.view(cs2_off, T),
               in_=cos2_dr, reads=[tab2B], writes=[tab2B])
        b.emit("sync", "dma_start", out=self.af.view(sn2_off, T),
               in_=sin2_dr, reads=[tab2B], writes=[tab2B])
        b.emit("sync", "dma_start", out=self.af.view(mtri_off, 128),
               in_=mtri_dr, reads=[mtB], writes=[mtB])
        b.emit("sync", "dma_start", out=self.af.view(mtrit_off, 128),
               in_=mtrit_dr, reads=[mtB], writes=[mtB])

        self.ab.push()
        wsl_off, wslB = self.ab.take(1024, "wsl")
        for wdr, doff, dB in ((kv_d_t, kvl_off, kvlB), (q_d_t, ql_off, qlB)):
            for oc in range(2):
                for kc in range(NC):
                    b.emit("gpsimd", "dma_start",
                           out=self.ab.view(wsl_off + kc * 128, 128),
                           in_=wdr[kc * 128:(kc + 1) * 128, oc * 128:(oc + 1) * 128],
                           reads=[wslB], writes=[wslB])
                for tb in range(NTB):
                    ps = self.ps_a if tb % 2 == 0 else self.ps_b
                    psB = self.ps_aB if tb % 2 == 0 else self.ps_bB
                    for kc in range(NC):
                        b.emit("tensor", "matmul", ps[:],
                               self.ab.view(wsl_off + kc * 128, 128),
                               self.OF[:, kc * T + tb * TB: kc * T + (tb + 1) * TB],
                               start=(kc == 0), stop=(kc == NC - 1),
                               reads=[wslB, OFall], writes=[psB])
                    b.emit("scalar", "copy",
                           out=self.ab.view(doff + oc * T + tb * TB, TB),
                           in_=ps[:], reads=[psB], writes=[dB])
        kr_off, krsB = self.ab.take(T, "krot")
        rsc_off, rscB = self.ab.take(2 * TB, "rope_scr")
        for oc4 in range(4):
            for kc in range(NC):
                b.emit("gpsimd", "dma_start",
                       out=self.ab.view(wsl_off + kc * 128, 128),
                       in_=rope_k_t[kc * 128:(kc + 1) * 128, oc4 * 128:(oc4 + 1) * 128],
                       reads=[wslB], writes=[wslB])
            for tb in range(NTB):
                ps = self.ps_a if tb % 2 == 0 else self.ps_b
                psB = self.ps_aB if tb % 2 == 0 else self.ps_bB
                for kc in range(NC):
                    b.emit("tensor", "matmul", ps[:],
                           self.ab.view(wsl_off + kc * 128, 128),
                           self.OF[:, kc * T + tb * TB: kc * T + (tb + 1) * TB],
                           start=(kc == 0), stop=(kc == NC - 1),
                           reads=[wslB, OFall], writes=[psB])
                b.emit("scalar", "copy",
                       out=self.ab.view(kr_off + tb * TB, TB),
                       in_=ps[:], reads=[psB], writes=[krsB])
            self._rope32(kr_off, krsB, 128, cs2_off, sn2_off, tab2B,
                         p32_off, p32B, rsc_off, rscB)
            b.emit("gpsimd", "dma_start",
                   out=krot_dr[oc4 * 128:(oc4 + 1) * 128, :],
                   in_=self.ab.view(kr_off, T), reads=[krsB], writes=[krB])
        self.ab.pop()
        self.b.barrier()
        wsl_off, wslB = self.ab.take(512, "wslB")
        rsc_off, rscB = self.ab.take(2 * TB, "rope_scrB")
        q1_off, q1B = self.ab.take(T, "Q1")
        q2_off, q2B = self.ab.take(T, "Q2")
        k1_off, k1B = self.ab.take(T, "K1")
        k2_off, k2B = self.ab.take(T, "K2")
        va_off, vaB = self.ab.take(16 * 130, "VA")
        pt_off, ptB = self.ab.take(2 * TB, "PT")
        c_m = self.col(2)
        b.emit("gpsimd", "memset", self.ab.view(k1_off, T, 64, 65), 1.0, writes=[k1B])
        b.emit("gpsimd", "memset", self.ab.view(k2_off, T, 64, 65), 1.0, writes=[k2B])
        for tt in range(16):
            b.emit("gpsimd", "memset",
                   self.ab.view(va_off + tt * 130 + 64, 1), 1.0, writes=[vaB])
            b.emit("gpsimd", "memset",
                   self.ab.view(va_off + tt * 130 + 129, 1), 1.0, writes=[vaB])
        for hp in range(8):
            for kc2 in range(2):
                b.emit("gpsimd", "dma_start",
                       out=self.ab.view(wsl_off + kc2 * 128, 64),
                       in_=rope_q_t[kc2 * 128:(kc2 + 1) * 128, hp * 64:(hp + 1) * 64],
                       reads=[wslB], writes=[wslB])
            for tb in range(NTB):
                ps = self.ps_a if tb % 2 == 0 else self.ps_b
                psB = self.ps_aB if tb % 2 == 0 else self.ps_bB
                for kc2 in range(2):
                    b.emit("tensor", "matmul", ps[0:64, :],
                           self.ab.view(wsl_off + kc2 * 128, 64),
                           self.ab.view(ql_off + kc2 * T + tb * TB, TB),
                           start=(kc2 == 0), stop=(kc2 == 1),
                           reads=[wslB, qlB], writes=[psB])
                b.emit("scalar", "copy",
                       out=self.ab.view(q1_off + tb * TB, TB, 0, 32),
                       in_=ps[0:32, :], reads=[psB], writes=[q1B])
                b.emit("scalar", "copy",
                       out=self.ab.view(q2_off + tb * TB, TB, 0, 32),
                       in_=ps[32:64, :], reads=[psB], writes=[q2B])
            self._rope32(q1_off, q1B, 32, cs2_off, sn2_off, tab2B,
                         p32_off, p32B, rsc_off, rscB)
            self._rope32(q2_off, q2B, 32, cs2_off, sn2_off, tab2B,
                         p32_off, p32B, rsc_off, rscB)
            b.emit("sync", "dma_start",
                   out=self.ab.view(k1_off, T, 0, 32),
                   in_=AP(krot_dr.tensor, (64 * hp) * T, [[T, 32], [1, T]]),
                   reads=[krB], writes=[k1B])
            b.emit("sync", "dma_start",
                   out=self.ab.view(k2_off, T, 0, 32),
                   in_=AP(krot_dr.tensor, (64 * hp + 32) * T, [[T, 32], [1, T]]),
                   reads=[krB], writes=[k2B])
            for wdr, d1o, d1B, d2o, d2B, scl in (
                    (q_u_t, q1_off, q1B, q2_off, q2B, None),
                    (k_u_t, k1_off, k1B, k2_off, k2B, None)):
                for kc2 in range(2):
                    b.emit("gpsimd", "dma_start",
                           out=self.ab.view(wsl_off + kc2 * 128, 128),
                           in_=wdr[kc2 * 128:(kc2 + 1) * 128, hp * 128:(hp + 1) * 128],
                           reads=[wslB], writes=[wslB])
                for tb in range(NTB):
                    ps = self.ps_a if tb % 2 == 0 else self.ps_b
                    psB = self.ps_aB if tb % 2 == 0 else self.ps_bB
                    for kc2 in range(2):
                        b.emit("tensor", "matmul", ps[:],
                               self.ab.view(wsl_off + kc2 * 128, 128),
                               self.ab.view(kvl_off + kc2 * T + tb * TB, TB) if wdr is k_u_t
                               else self.ab.view(ql_off + kc2 * T + tb * TB, TB),
                               start=(kc2 == 0), stop=(kc2 == 1),
                               reads=[wslB, kvlB if wdr is k_u_t else qlB],
                               writes=[psB])
                    for half, (do_, dB_) in enumerate(((d1o, d1B), (d2o, d2B))):
                        if scl is not None:
                            b.emit("vector", "tensor_scalar",
                                   out=self.ab.view(do_ + tb * TB, TB, 32, 64),
                                   in0=ps[half * 64 + 32:(half + 1) * 64, :],
                                   scalar1=scl, scalar2=None, op0=ALU.mult,
                                   reads=[psB], writes=[dB_])
                        else:
                            b.emit("scalar", "copy",
                                   out=self.ab.view(do_ + tb * TB, TB, 32, 64),
                                   in_=ps[half * 64 + 32:(half + 1) * 64, :],
                                   reads=[psB], writes=[dB_])
            for qo, qB_ in ((q1_off, q1B), (q2_off, q2B)):
                b.emit("vector", "tensor_scalar",
                       out=self.ab.view(qo, T, 0, 64),
                       in0=self.ab.view(qo, T, 0, 64),
                       scalar1=0.125, scalar2=None, op0=ALU.mult,
                       reads=[qB_], writes=[qB_])
            for kc2 in range(2):
                b.emit("gpsimd", "dma_start",
                       out=self.ab.view(wsl_off + kc2 * 128, 128),
                       in_=v_u_t[kc2 * 128:(kc2 + 1) * 128, hp * 128:(hp + 1) * 128],
                       reads=[wslB], writes=[wslB])
            for tt in range(16):
                ps = self.ps_a if tt % 2 == 0 else self.ps_b
                psB = self.ps_aB if tt % 2 == 0 else self.ps_bB
                for kc2 in range(2):
                    b.emit("tensor", "matmul", ps[:, 0:128],
                           self.ab.view(kvl_off + kc2 * T + tt * 128, 128),
                           self.ab.view(wsl_off + kc2 * 128, 128),
                           start=(kc2 == 0), stop=(kc2 == 1),
                           reads=[wslB, kvlB], writes=[psB])
                b.emit("scalar", "copy",
                       out=self.ab.view(va_off, 16 * 130)[:, tt * 130: tt * 130 + 64],
                       in_=ps[:, 0:64], reads=[psB], writes=[vaB])
                b.emit("scalar", "copy",
                       out=self.ab.view(va_off, 16 * 130)[:, tt * 130 + 65: tt * 130 + 129],
                       in_=ps[:, 64:128], reads=[psB], writes=[vaB])
            self._attn_causal_core(hp, q1_off, q1B, q2_off, q2B, k1_off, k1B,
                                   k2_off, k2B, va_off, vaB, pt_off, ptB,
                                   mtri_off, mtrit_off, mtB, c_m)
        self.ab.pop()
        self.af.pop()
        self.b.barrier()
        self.ab.push()
        self.af.push()
        wo_off, woB = self.ab.take(8 * 1024, "l2f_wo")
        for kc in range(NC):
            b.emit("gpsimd", "dma_start",
                   out=self.ab.view(wo_off + kc * 1024, 1024),
                   in_=o_w_t[kc * 128:(kc + 1) * 128, :],
                   reads=[woB], writes=[woB])
        for oc in range(NC):
            for tb in range(NTB):
                ps = self.ps_a if (oc + tb) % 2 == 0 else self.ps_b
                psB = self.ps_aB if (oc + tb) % 2 == 0 else self.ps_bB
                for kc in range(NC):
                    b.emit("tensor", "matmul", ps[:],
                           self.ab.view(wo_off + kc * 1024, 1024)[:, oc * 128:(oc + 1) * 128],
                           self.OF[:, kc * T + tb * TB: kc * T + (tb + 1) * TB],
                           start=(kc == 0), stop=(kc == NC - 1),
                           reads=[woB, self.OFBufs[kc]], writes=[psB])
                sl = slice(oc * T + tb * TB, oc * T + (tb + 1) * TB)
                b.emit("vector", "tensor_add",
                       out=self.X[:, sl], in0=self.X[:, sl], in1=ps[:],
                       reads=[self.XBufs[oc], psB], writes=[self.XBufs[oc]])
                b.emit("vector", "tensor_add",
                       out=self.XH[:, sl], in0=self.XH[:, sl], in1=ps[:],
                       reads=[self.XHBufs[oc], psB], writes=[self.XHBufs[oc]])
        self.ab.pop()
        self.af.pop()
        self.b.barrier()
        if L2F_PARTS == "attn":
            return
        self.ln_inner(self.XH, self.XHBufs, cln2, None)
        router_t = self.din("l2f_router_t", (D, 7))
        rbias_dr = self.din("l2f_rbias", (128, 7))
        gate_dr = self.din("l2f_gate_t", (8, D, 2048))
        up_dr = self.din("l2f_up_t", (8, D, 2048))
        down_dr = self.din("l2f_down_t", (8, 2048, D))
        self.ab.push()
        self.af.push()
        rw_off, rwB = self.ab.take(56, "routerw")
        pr_off, prB = self.af.take(7, "probs")

        rb_off, rbB = self.af.take(7, "rbias")
        wg_off, wgB = self.af.take(128, "WGT_stage")
        wb_off, wbB = self.af.take(4 * TB, "wgt_bcast")
        wgt_dr = nc.dram_tensor("l2f_wgt_scr", [7, T], F32, kind="Internal").ap()
        wgdB = self.b.buf("l2f_wgtdr")
        c_t2 = self.col(4)
        b.emit("gpsimd", "memset", self.wsmall[:, 7:16], 0.0,
               writes=[self.wsmallB])
        for kc in range(NC):
            b.emit("gpsimd", "dma_start",
                   out=self.ab.view(rw_off + kc * 7, 7),
                   in_=router_t[kc * 128:(kc + 1) * 128, :],
                   reads=[rwB], writes=[rwB])
        b.emit("sync", "dma_start", out=self.af.view(rb_off, 7),
               in_=rbias_dr, reads=[rbB], writes=[rbB])
        for tt in range(16):
            for kc in range(NC):
                b.emit("tensor", "matmul", self.ps_a[:, 0:7],
                       self.XH[:, kc * T + tt * 128: kc * T + (tt + 1) * 128],
                       self.ab.view(rw_off + kc * 7, 7),
                       start=(kc == 0), stop=(kc == NC - 1),
                       reads=[self.XHBufs[kc], rwB], writes=[self.ps_aB])
            b.emit("vector", "tensor_add",
                   out=self.af.view(pr_off, 7), in0=self.ps_a[:, 0:7],
                   in1=self.af.view(rb_off, 7),
                   reads=[self.ps_aB, rbB], writes=[prB])
            b.emit("scalar", "activation", out=self.af.view(pr_off, 7),
                   in_=self.af.view(pr_off, 7), func=AF.Sigmoid,
                   reads=[prB], writes=[prB])
            b.emit("vector", "reduce_max", out=self.cols[:, c_t2:c_t2 + 1],
                   in_=self.af.view(pr_off, 7), axis=AX.X,
                   reads=[prB], writes=[self.colsB])
            b.emit("vector", "tensor_scalar", out=self.wsmall[:, 0:7],
                   in0=self.af.view(pr_off, 7),
                   scalar1=self.cols[:, c_t2:c_t2 + 1], scalar2=None,
                   op0=ALU.is_ge, reads=[prB, self.colsB], writes=[self.wsmallB])
            b.emit("vector", "scalar_tensor_tensor",
                   out=self.af.view(pr_off, 7),
                   in0=self.wsmall[:, 0:7], scalar=-2.0,
                   in1=self.af.view(pr_off, 7),
                   op0=ALU.mult, op1=ALU.add,
                   reads=[prB, self.wsmallB], writes=[prB])
            b.emit("vector", "reduce_max", out=self.cols[:, c_t2 + 1:c_t2 + 2],
                   in_=self.af.view(pr_off, 7), axis=AX.X,
                   reads=[prB], writes=[self.colsB])
            b.emit("vector", "tensor_scalar", out=self.af.view(pr_off, 7),
                   in0=self.af.view(pr_off, 7),
                   scalar1=self.cols[:, c_t2 + 1:c_t2 + 2], scalar2=None,
                   op0=ALU.is_ge, reads=[prB, self.colsB], writes=[prB])
            b.emit("vector", "tensor_add", out=self.wsmall[:, 0:7],
                   in0=self.wsmall[:, 0:7],
                   in1=self.af.view(pr_off, 7),
                   reads=[self.wsmallB, prB], writes=[self.wsmallB])
            b.emit("vector", "tensor_add",
                   out=self.af.view(pr_off, 7), in0=self.ps_a[:, 0:7],
                   in1=self.af.view(rb_off, 7),
                   reads=[self.ps_aB, rbB], writes=[prB])
            b.emit("scalar", "activation", out=self.af.view(pr_off, 7),
                   in_=self.af.view(pr_off, 7), func=AF.Sigmoid,
                   reads=[prB], writes=[prB])
            b.emit("vector", "tensor_add", out=self.cols[:, c_t2 + 2:c_t2 + 3],
                   in0=self.cols[:, c_t2:c_t2 + 1],
                   in1=self.cols[:, c_t2 + 1:c_t2 + 2],
                   reads=[self.colsB], writes=[self.colsB])
            b.emit("vector", "reciprocal", out=self.cols[:, c_t2 + 3:c_t2 + 4],
                   in_=self.cols[:, c_t2 + 2:c_t2 + 3],
                   reads=[self.colsB], writes=[self.colsB])
            b.emit("vector", "tensor_mul", out=self.af.view(pr_off, 7),
                   in0=self.af.view(pr_off, 7),
                   in1=self.wsmall[:, 0:7],
                   reads=[prB, self.wsmallB], writes=[prB])
            b.emit("vector", "tensor_scalar", out=self.wsmall[:, 0:7],
                   in0=self.af.view(pr_off, 7),
                   scalar1=self.cols[:, c_t2 + 3:c_t2 + 4], scalar2=None,
                   op0=ALU.mult, reads=[prB, self.colsB], writes=[self.wsmallB])
            b.emit("tensor", "matmul", self.ps_b[0:16, 0:128],
                   self.wsmall[:, 0:16], self.id_bf[:],
                   start=True, stop=True,
                   reads=[self.wsmallB, self.constB], writes=[self.ps_bB])
            if L2F_DEBUG and tt == 15:
                did = self.nc.dram_tensor("dbg_id", [128, 128], F32,
                                          kind="ExternalOutput").ap()
                o4, s4B = self.af.take(128, "dbg4")
                b.emit("vector", "tensor_copy",
                       out=self.af.view(o4, 128),
                       in_=self.id_bf[:], reads=[self.constB], writes=[s4B])
                b.emit("gpsimd", "dma_start", out=did,
                       in_=self.af.view(o4, 128),
                       reads=[s4B], writes=[s4B])
                dps = self.nc.dram_tensor("dbg_ps", [7, 128], F32,
                                          kind="ExternalOutput").ap()
                o3, s3B = self.af.take(128, "dbg3")
                b.emit("vector", "tensor_copy",
                       out=self.af.view(o3, 128)[0:7, :],
                       in_=self.ps_b[0:7, 0:128],
                       reads=[self.ps_bB], writes=[s3B])
                b.emit("gpsimd", "dma_start", out=dps,
                       in_=self.af.view(o3, 128)[0:7, :],
                       reads=[s3B], writes=[s3B])
            b.emit("scalar", "copy",
                   out=self.af.view(wg_off, 128)[0:7, :],
                   in_=self.ps_b[0:7, 0:128],
                   reads=[self.ps_bB], writes=[wgB])
            b.emit("gpsimd", "dma_start",
                   out=AP(wgt_dr.tensor, tt * 128, [[T, 7], [1, 128]]),
                   in_=self.af.view(wg_off, 128)[0:7, :],
                   reads=[wgB], writes=[wgdB])
        if L2F_DEBUG:
            dst_stage = self.nc.dram_tensor("dbg_stage", [7, 128], F32,
                                            kind="ExternalOutput").ap()
            self.b.barrier()
            b.emit("gpsimd", "dma_start", out=dst_stage,
                   in_=self.af.view(wg_off, 128)[0:7, :],
                   reads=[wgB], writes=[wgB])
            dwg = self.nc.dram_tensor("dbg_wgt", [7, T], F32,
                                      kind="ExternalOutput").ap()
            self.b.barrier()
            b.emit("gpsimd", "dma_start", out=self.af.view(wg_off, 128)[0:7, :],
                   in_=AP(wgt_dr.tensor, 0, [[T, 7], [1, 128]]),
                   reads=[wgdB], writes=[wgB])
            b.emit("gpsimd", "dma_start",
                   out=AP(dwg.tensor, 0, [[T, 7], [1, 128]]),
                   in_=self.af.view(wg_off, 128)[0:7, :],
                   reads=[wgB], writes=[wgB])
            dpb = self.nc.dram_tensor("dbg_wgtb", [128, 7], F32,
                                      kind="ExternalOutput").ap()
            off2, s2B = self.af.take(7, "dbg2")
            b.emit("vector", "tensor_copy", out=self.af.view(off2, 7),
                   in_=self.wsmall[:, 0:7], reads=[self.wsmallB], writes=[s2B])
            b.emit("gpsimd", "dma_start", out=dpb, in_=self.af.view(off2, 7),
                   reads=[s2B], writes=[s2B])
            self.b.barrier()
        gu_off, _gu = self.ab.take(4096, "gu")
        guBs = [self.b.buf("gu0"), self.b.buf("gu1")]
        dn_off, _dn = self.ab.take(16384, "down")
        dnBs = [self.b.buf("dn0"), self.b.buf("dn1")]
        hs_off, hsB = self.af.take(2 * TB, "silu")
        h2_off, h2B = self.af.take(TB, "hid_tmp")

        def gu_load(n):
            e, ic = n // 16, n % 16
            slot = n % 2
            base = e * D * 2048 + ic * 128
            b.emit("gpsimd", "dma_start",
                   out=self.ab.view(gu_off + slot * 2048, 1024),
                   in_=AP(gate_dr.tensor, base,
                          [[2048, 128], [128 * 2048, 8], [1, 128]]),
                   reads=[guBs[slot]], writes=[guBs[slot]])
            b.emit("gpsimd", "dma_start",
                   out=self.ab.view(gu_off + slot * 2048 + 1024, 1024),
                   in_=AP(up_dr.tensor, base,
                          [[2048, 128], [128 * 2048, 8], [1, 128]]),
                   reads=[guBs[slot]], writes=[guBs[slot]])

        def dn_load(m):
            e, ig = m // 2, m % 2
            slot = m % 2
            base = e * 2048 * D + ig * 8 * 128 * D
            b.emit("gpsimd", "dma_start",
                   out=self.ab.view(dn_off + slot * 8192, 8192),
                   in_=AP(down_dr.tensor, base,
                          [[D, 128], [128 * D, 8], [1, D]]),
                   reads=[dnBs[slot]], writes=[dnBs[slot]])

        gu_load(0)
        gu_load(1)
        dn_load(0)
        dn_load(1)
        for e in range(8):
            if e > 0:
                for tb in range(NTB):
                    b.emit("scalar", "dma_start",
                           out=self.af.view(wb_off + tb * TB, TB),
                           in_=AP(wgt_dr.tensor, (e - 1) * T + tb * TB,
                                  [[0, 128], [1, TB]]),
                           reads=[wgdB], writes=[wbB])
            for ig in range(2):
                m = e * 2 + ig
                dslot = m % 2
                for il in range(8):
                    n = e * 16 + ig * 8 + il
                    slot = n % 2
                    for tb in range(NTB):
                        psg, psgB = ((self.ps_a, self.ps_aB) if tb % 2 == 0
                                     else (self.ps_c, self.ps_cB))
                        psu, psuB = ((self.ps_b, self.ps_bB) if tb % 2 == 0
                                     else (self.ps_d, self.ps_dB))
                        for kc in range(NC):
                            b.emit("tensor", "matmul", psg[:],
                                   self.ab.view(gu_off + slot * 2048 + kc * 128, 128),
                                   self.XH[:, kc * T + tb * TB: kc * T + (tb + 1) * TB],
                                   start=(kc == 0), stop=(kc == NC - 1),
                                   reads=[guBs[slot], self.XHBufs[kc]], writes=[psgB])
                            b.emit("tensor", "matmul", psu[:],
                                   self.ab.view(gu_off + slot * 2048 + 1024 + kc * 128, 128),
                                   self.XH[:, kc * T + tb * TB: kc * T + (tb + 1) * TB],
                                   start=(kc == 0), stop=(kc == NC - 1),
                                   reads=[guBs[slot], self.XHBufs[kc]], writes=[psuB])
                        hsl = hs_off + (tb % 2) * TB
                        b.emit("scalar", "activation",
                               out=self.af.view(hsl, TB), in_=psg[:],
                               func=AF.Silu, reads=[psgB], writes=[hsB])
                        ofslot = il * T + tb * TB
                        if e == 0:
                            b.emit("vector", "tensor_mul",
                                   out=self.OF[:, ofslot:ofslot + TB],
                                   in0=psu[:], in1=self.af.view(hsl, TB),
                                   reads=[psuB, hsB], writes=[self.OFBufs[il]])
                        else:
                            b.emit("vector", "tensor_mul",
                                   out=self.af.view(h2_off, TB),
                                   in0=psu[:], in1=self.af.view(hsl, TB),
                                   reads=[psuB, hsB], writes=[h2B])
                            b.emit("vector", "tensor_mul",
                                   out=self.OF[:, ofslot:ofslot + TB],
                                   in0=self.af.view(h2_off, TB),
                                   in1=self.af.view(wb_off + tb * TB, TB),
                                   reads=[wbB, h2B], writes=[self.OFBufs[il]])
                    if n + 2 < 128:
                        gu_load(n + 2)
                for tb in range(NTB):
                    for oc in range(NC):
                        ps = self.ps_big[:, (oc % 4) * TB:(oc % 4 + 1) * TB]
                        for il in range(8):
                            b.emit("tensor", "matmul", ps,
                                   self.ab.view(dn_off + dslot * 8192 + il * 1024, 1024)[:, oc * 128:(oc + 1) * 128],
                                   self.OF[:, il * T + tb * TB: il * T + (tb + 1) * TB],
                                   start=(il == 0), stop=(il == 7),
                                   reads=[dnBs[dslot], self.OFBufs[il]],
                                   writes=[self.ps_bigQ[oc % 4]])
                        sl = slice(oc * T + tb * TB, oc * T + (tb + 1) * TB)
                        b.emit("vector", "tensor_add",
                               out=self.X[:, sl], in0=self.X[:, sl], in1=ps,
                               reads=[self.XBufs[oc], self.ps_bigQ[oc % 4]],
                               writes=[self.XBufs[oc]])
                if m + 2 < 16:
                    dn_load(m + 2)
        self.ab.pop()
        self.af.pop()

    def l3_attn(self):
        b = self.b
        self.layernorm()
        wq = self.din("l3_wq", (D, D))
        wk = self.din("l3_wk", (D, D))
        wv = self.din("l3_wv", (D, D))
        cb_q = self.col(NC); self.load_cols("l3_bq", (NC, 128), cb_q, NC)
        c_tmp = self.col(16); self.load_cols("l3_temp", (16, 128), c_tmp, 16)
        bk_b = self.din("l3_bk", (D,))
        bv_b = self.din("l3_bv", (D,))
        yk_dr = self.nc.dram_tensor("l3_yk_scr", [T, D], BF16, kind="Internal").ap()
        yv_dr = self.nc.dram_tensor("l3_yv_scr", [T, D], BF16, kind="Internal").ap()
        ykB = self.b.buf("l3_ykdr")
        yvB = self.b.buf("l3_yvdr")
        OFall = self.b.buf("l3_OFall")

        self.ab.push()
        self.af.push()
        st_off, stB = self.af.take(1024, "wstage")
        wsl_off, wslB = self.ab.take(2048, "wsl")
        for oc in range(NC):
            dbuf = oc % 2
            for kc in range(NC):
                b.emit("gpsimd", "dma_start",
                       out=self.ab.view(wsl_off + dbuf * 1024 + kc * 128, 128),
                       in_=wq[kc * 128:(kc + 1) * 128, oc * 128:(oc + 1) * 128],
                       reads=[wslB], writes=[wslB])
            for tb in range(NTB):
                ps = self.ps_a if tb % 2 == 0 else self.ps_b
                psB = self.ps_aB if tb % 2 == 0 else self.ps_bB
                for kc in range(NC):
                    b.emit("tensor", "matmul", ps[:],
                           self.ab.view(wsl_off + dbuf * 1024 + kc * 128, 128),
                           self.XH[:, kc * T + tb * TB: kc * T + (tb + 1) * TB],
                           start=(kc == 0), stop=(kc == NC - 1),
                           reads=[wslB, self.XHBufs[kc]], writes=[psB])
                b.emit("vector", "tensor_scalar",
                       out=self.OF[:, oc * T + tb * TB: oc * T + (tb + 1) * TB],
                       in0=ps[:], scalar1=self.cols[:, cb_q + oc:cb_q + oc + 1],
                       scalar2=None, op0=ALU.add,
                       reads=[psB, self.colsB], writes=[OFall])
        bk_off, bkbB = self.af.take(512, "bkb")
        ev_off, evB = self.ab.take(512, "yevac")
        wh_off, whB = self.ab.take(8192, "whalf")
        for wdr, bias_dr, dst, dstB, temp_scaled in (
                (wk, bk_b, yk_dr, ykB, True), (wv, bv_b, yv_dr, yvB, False)):
            for oh in range(2):
                for kc in range(NC):
                    b.emit("gpsimd", "dma_start",
                           out=self.ab.view(wh_off + kc * 512, 512),
                           in_=wdr[kc * 128:(kc + 1) * 128, oh * 512:(oh + 1) * 512],
                           reads=[whB], writes=[whB])
                b.emit("gpsimd", "dma_start",
                       out=self.af.view(bk_off, 512),
                       in_=AP(bias_dr.tensor, oh * 512, [[0, 128], [1, 512]]),
                       reads=[bkbB], writes=[bkbB])
                for tt in range(16):
                    h = tt
                    ps = self.ps_a if tt % 2 == 0 else self.ps_b
                    psB = self.ps_aB if tt % 2 == 0 else self.ps_bB
                    for kc in range(NC):
                        b.emit("tensor", "matmul", ps[:],
                               self.XH[:, kc * T + tt * 128: kc * T + (tt + 1) * 128],
                               self.ab.view(wh_off + kc * 512, 512),
                               start=(kc == 0), stop=(kc == NC - 1),
                               reads=[whB, self.XHBufs[kc]], writes=[psB])
                    if temp_scaled:
                        b.emit("vector", "scalar_tensor_tensor",
                               out=self.ab.view(ev_off, 512),
                               in0=ps[:], scalar=self.cols[:, c_tmp + h:c_tmp + h + 1],
                               in1=self.af.view(bk_off, 512),
                               op0=ALU.bypass, op1=ALU.add,
                               reads=[psB, bkbB, self.colsB], writes=[evB])
                        b.emit("vector", "tensor_scalar",
                               out=self.ab.view(ev_off, 512),
                               in0=self.ab.view(ev_off, 512),
                               scalar1=self.cols[:, c_tmp + h:c_tmp + h + 1],
                               scalar2=None, op0=ALU.mult,
                               reads=[evB, self.colsB], writes=[evB])
                    else:
                        b.emit("vector", "tensor_add",
                               out=self.ab.view(ev_off, 512),
                               in0=ps[:], in1=self.af.view(bk_off, 512),
                               reads=[psB, bkbB], writes=[evB])
                    b.emit("gpsimd", "dma_start",
                           out=dst[tt * 128:(tt + 1) * 128, oh * 512:(oh + 1) * 512],
                           in_=self.ab.view(ev_off, 512),
                           reads=[evB], writes=[dstB])
        self.ab.pop()
        self.af.pop()
        self.b.barrier()
        self.ab.push()
        self.af.push()
        km_off, kmB = self.ab.take(T, "Kmat")
        vm_off, vmB = self.ab.take(T, "Vmat")
        p_off, pB = self.ab.take(T, "Ptile")
        qs_off, qsB = self.ab.take(64, "QS")
        os_off, osB = self.af.take(T, "out_f32")
        rc_off = self.col(16)
        cm2 = self.col(2)
        for h in range(16):
            for src_dr, srcB, moff, mB in ((yk_dr, ykB, km_off, kmB),
                                           (yv_dr, yvB, vm_off, vmB)):
                for flag in range(2):
                    b.emit("sync", "dma_start",
                           out=self.ab.view(moff + flag * 1024, 1024, 0, 64),
                           in_=AP(src_dr.tensor, (128 * h + flag) * D,
                                  [[2 * D, 64], [1, D]]),
                           reads=[srcB], writes=[mB])
            psacc = (self.ps_a, self.ps_b, self.ps_c, self.ps_d)
            psaccB = (self.ps_aB, self.ps_bB, self.ps_cB, self.ps_dB)
            for tc in range(16):
                for sb in range(NTB):
                    b.emit("tensor", "matmul",
                           self.ps_big[:, sb * TB:(sb + 1) * TB],
                           self.ab.view(km_off + tc * 128, 128, 0, 64),
                           self.ab.view(vm_off + sb * TB, TB, 0, 64),
                           start=True, stop=True,
                           reads=[kmB, vmB], writes=[self.ps_bigQ[sb]])
                b.emit("vector", "reduce_max",
                       out=self.cols[:, cm2:cm2 + 1], in_=self.ps_big[:, :],
                       axis=AX.X, negate=True,
                       reads=self.ps_bigQ, writes=[self.colsB])
                for sb in range(NTB):
                    b.emit("scalar", "activation",
                           out=self.ab.view(p_off + sb * TB, TB),
                           in_=self.ps_big[:, sb * TB:(sb + 1) * TB],
                           func=AF.Exp, bias=self.cols[:, cm2:cm2 + 1],
                           reads=[self.ps_bigQ[sb], self.colsB], writes=[pB])
                b.emit("vector", "reduce_sum",
                       out=self.cols[:, cm2 + 1:cm2 + 2], in_=self.ab.view(p_off, T),
                       axis=AX.X, reads=[pB], writes=[self.colsB])
                b.emit("vector", "reciprocal",
                       out=self.cols[:, rc_off:rc_off + 1],
                       in_=self.cols[:, cm2 + 1:cm2 + 2],
                       reads=[self.colsB], writes=[self.colsB])
                qview = AP(self.OF, (tc % 8) * T + 128 * h + (tc // 8),
                           [[NC * T, 128], [2, 64]])
                b.emit("vector", "tensor_scalar",
                       out=self.ab.view(qs_off, 64), in0=qview,
                       scalar1=self.cols[:, rc_off:rc_off + 1], scalar2=None,
                       op0=ALU.mult,
                       reads=[OFall, self.colsB], writes=[qsB])
                for sb in range(NTB):
                    b.emit("tensor", "matmul", psacc[sb][0:64, :],
                           self.ab.view(qs_off, 64),
                           self.ab.view(p_off + sb * TB, TB),
                           start=(tc == 0), stop=(tc == 15),
                           reads=[qsB, pB], writes=[psaccB[sb]])
            for sb in range(NTB):
                b.emit("scalar", "copy",
                       out=self.af.view(os_off + sb * TB, TB)[0:64, :],
                       in_=psacc[sb][0:64, :],
                       reads=[psaccB[sb]], writes=[osB])
            for sc in range(16):
                cf = sc % 8
                flag = sc // 8
                ps = self.ps_a if sc % 2 == 0 else self.ps_b
                psB = self.ps_aB if sc % 2 == 0 else self.ps_bB
                b.emit("tensor", "transpose", ps[:, 0:64],
                       self.af.view(os_off + sc * 128, 128)[0:64, :],
                       self.id_f32[0:64, 0:64],
                       reads=[osB, self.constB], writes=[psB])
                xview = AP(self.X, cf * T + 128 * h + flag, [[NC * T, 128], [2, 64]])
                b.emit("vector", "tensor_add",
                       out=xview, in0=xview, in1=ps[:, 0:64],
                       reads=[self.XBufs[cf], psB], writes=[self.XBufs[cf]])
        self.ab.pop()
        self.af.pop()

    def l3_ffn(self):
        b = self.b
        self.layernorm()
        w_t = self.din("l3f_w_t", (D, D))
        vr = self.din("l3f_vr", (D, 32))
        c_t = self.din("l3f_c_t", (32, D))
        cb = self.col(NC); self.load_cols("l3f_b", (NC, 128), cb, NC)
        self.ab.push()
        self.af.push()
        st_off, stB = self.af.take(1024, "wstage")
        w_off, wB = self.ab.take(8 * 1024, "l3f_w")
        xr_off, xrB = self.ab.take(T, "l3f_xr")
        ct_off, ctB = self.ab.take(1024, "l3f_ct")
        self.load_wt_bf16(w_t, D, D, w_off, wB, st_off, stB)
        b.emit("gpsimd", "dma_start", out=self.ab.view(ct_off, 1024)[0:32, :],
               in_=c_t, reads=[ctB], writes=[ctB])
        vr_off, vrB = self.ab.take(256, "l3f_vr")
        for kc in range(NC):
            b.emit("gpsimd", "dma_start",
                   out=self.ab.view(vr_off + kc * 32, 32),
                   in_=vr[kc * 128:(kc + 1) * 128, :], reads=[vrB], writes=[vrB])
        for tb in range(NTB):
            ps = self.ps_c
            for kc in range(NC):
                b.emit("tensor", "matmul", ps[0:32, :],
                       self.ab.view(vr_off + kc * 32, 32),
                       self.XH[:, kc * T + tb * TB: kc * T + (tb + 1) * TB],
                       start=(kc == 0), stop=(kc == NC - 1),
                       reads=[vrB, self.XHBufs[kc]], writes=[self.ps_cB])
            b.emit("scalar", "copy",
                   out=self.ab.view(xr_off + tb * TB, TB)[0:32, :],
                   in_=ps[0:32, :], reads=[self.ps_cB], writes=[xrB])
        for oc in range(NC):
            for tb in range(NTB):
                ps = self.ps_a if (oc + tb) % 2 == 0 else self.ps_b
                psB = self.ps_aB if (oc + tb) % 2 == 0 else self.ps_bB
                for kc in range(NC):
                    b.emit("tensor", "matmul", ps[:],
                           self.ab.view(w_off + kc * 1024, 1024)[:, oc * 128:(oc + 1) * 128],
                           self.XH[:, kc * T + tb * TB: kc * T + (tb + 1) * TB],
                           start=(kc == 0), stop=False,
                           reads=[wB, self.XHBufs[kc]], writes=[psB])
                b.emit("tensor", "matmul", ps[:],
                       self.ab.view(ct_off, 1024)[0:32, oc * 128:(oc + 1) * 128],
                       self.ab.view(xr_off + tb * TB, TB)[0:32, :],
                       start=False, stop=True,
                       reads=[ctB, xrB], writes=[psB])
                b.emit("vector", "scalar_tensor_tensor",
                       out=self.X[:, oc * T + tb * TB: oc * T + (tb + 1) * TB],
                       in0=ps[:], scalar=self.cols[:, cb + oc:cb + oc + 1],
                       in1=self.X[:, oc * T + tb * TB: oc * T + (tb + 1) * TB],
                       op0=ALU.add, op1=ALU.add,
                       reads=[psB, self.colsB, self.XBufs[oc]],
                       writes=[self.XBufs[oc]])
        self.ab.pop()
        self.af.pop()



BLOCK_INPUT_KEYS = {
    "l0_attn": ["l0_wq_t", "l0_wk_t", "l0_wv_t", "l0_wo_t", "l0_bq", "l0_bk",
                "l0_bv", "l0_bo", "l0_ext", "l0_ext_rev", "l0_rel_raw"],
}
CONST_KEYS = ["c_id_bf", "c_id_f32", "c_J_f32", "c_ones_row_f32",
              "c_ones_col_f32"]


def build_program(blocks, avail):
    nc = bass.Bass("TRN2", target_bir_lowering=False, debug=False)
    k = Ker(nc, avail)
    k.zero_init()
    k.b.barrier()
    k.load_consts()
    k.prologue()
    for blk in blocks:
        getattr(k, blk)()
        k.b.barrier()
    k.epilogue()
    k.b.finalize()
    k.b.es.close()
    return nc, sorted(k.dram.keys())


def run_blocks(inputs, blocks, n_cores=8):
    common = host_prep_common(inputs)
    x = np.ascontiguousarray(np.asarray(inputs["L0_attn_x"], np.float32))
    xs = [x[i] for i in range(n_cores)]
    avail = set(common.keys()) | {"x_in"}
    nc, used = build_program(blocks, avail)
    in_maps = []
    for i in range(n_cores):
        m = {n: common[n] for n in used if n != "x_in"}
        m["x_in"] = xs[i]
        in_maps.append(m)
    res = run_bass_kernel_spmd(nc, in_maps, core_ids=list(range(n_cores)))
    out = np.stack([np.asarray(r["y_out"]) for r in res.results], axis=0)
    return out


ALL_BLOCKS = ["l0_attn", "l0_ffn", "l1_attn", "l1_ffn", "l2_attn", "l2_ffn", "l3_attn", "l3_ffn"]


def kernel(**inputs):
    return run_blocks(inputs, ALL_BLOCKS).astype(np.float32)


# version 07
# speedup vs baseline: 1.3998x; 1.0795x over previous
import numpy as np
import ml_dtypes
from contextlib import ExitStack

import concourse.bass as bass
import concourse.mybir as mybir
from concourse.bass import AP
from concourse.bass_utils import run_bass_kernel_spmd

F32 = mybir.dt.float32
BF16 = mybir.dt.bfloat16
AF = mybir.ActivationFunctionType
ALU = mybir.AluOpType
AX = mybir.AxisListType

T = 2048
L2F_PARTS = "all"
L2F_DEBUG = False
QUIET_GROUPS = False
D = 1024
NC = 8
NTB = 4
TB = 512
NEG = -1.0e30
EPS = 1e-5

ENGINE_NAMES = ("sync", "tensor", "vector", "scalar", "gpsimd")



class Buf:
    __slots__ = ("name", "writer", "readers")

    def __init__(self, name):
        self.name = name
        self.writer = None
        self.readers = {}


class EngRec:
    def __init__(self, name, csem, dsem):
        self.name = name
        self.csem = csem
        self.dsem = dsem
        self.cn = 0
        self.dn = 0
        self.items = []
        self.watermark = {}
        self.pend_r = []
        self.pend_w = []


class Bld:
    def __init__(self, nc):
        self.nc = nc
        self.es = ExitStack()
        self.engs = {}
        for n in ENGINE_NAMES:
            csem = self.es.enter_context(nc.semaphore(f"c_{n}"))
            dsem = self.es.enter_context(nc.semaphore(f"d_{n}"))
            self.engs[n] = EngRec(n, csem, dsem)
        self.bufs = []

    def buf(self, name):
        b = Buf(name)
        self.bufs.append(b)
        return b

    def emit(self, ename, method, *args, reads=(), writes=(), **kw):
        e = self.engs[ename]
        waits = []

        def need(tok):
            if tok is None:
                return
            sem, val = tok
            if sem is e.csem and ename == "tensor":
                return
            if e.watermark.get(id(sem), 0) >= val:
                return
            waits.append((sem, val))
            e.watermark[id(sem)] = val

        for b in reads:
            need(b.writer)
        for b in writes:
            need(b.writer)
            for rt in b.readers.values():
                need(rt)
        is_dma = method in ("dma_start", "dma_start_transpose")
        quiet = QUIET_GROUPS and (method == "matmul" and kw.get("stop") is False)
        if quiet:
            e.items.append((waits, method, args, kw, None))
            e.pend_r.extend(reads)
            e.pend_w.extend(writes)
            return None
        if is_dma:
            e.dn += 16
            tok = (e.dsem, e.dn)
            inc = (e.dsem, 16)
        else:
            e.cn += 1
            tok = (e.csem, e.cn)
            inc = (e.csem, 1)
        e.items.append((waits, method, args, kw, inc))
        for b in e.pend_r:
            b.readers[ename] = tok
        for b in e.pend_w:
            b.writer = tok
            b.readers = {}
        e.pend_r = []
        e.pend_w = []
        for b in reads:
            b.readers[ename] = tok
        for b in writes:
            b.writer = tok
            b.readers = {}
        return tok

    def barrier(self):
        toks = []
        for n in ENGINE_NAMES:
            e = self.engs[n]
            toks.append((e.csem, e.cn))
            toks.append((e.dsem, e.dn))
        for n in ENGINE_NAMES:
            e = self.engs[n]
            for sem, val in toks:
                if sem is e.csem or val == 0:
                    continue
                if e.watermark.get(id(sem), 0) >= val:
                    continue
                e.items.append(([(sem, val)], None, None, None, None))
                e.watermark[id(sem)] = val

    def finalize(self):
        nc = self.nc
        with nc.Block() as block:
            for n in ENGINE_NAMES:
                rec = self.engs[n]

                def run(eng, rec=rec):
                    for waits, method, args, kw, inc in rec.items:
                        for sem, val in waits:
                            eng.wait_ge(sem, val)
                        if method is None:
                            continue
                        ins = getattr(eng, method)(*args, **kw)
                        if inc is not None:
                            ins.then_inc(*inc)
                getattr(block, n)(run)



class Arena:
    def __init__(self, bld, name, dtype, cols):
        self.bld = bld
        self.t = bld.es.enter_context(
            bld.nc.sbuf_tensor(name, [128, cols], dtype))
        self.cols = cols
        self.off = 0
        self.marks = []

    def take(self, cols, name):
        self.off = (self.off + 15) & ~15
        assert self.off + cols <= self.cols, \
            f"arena {self.t.name} overflow: {self.off}+{cols}>{self.cols} ({name})"
        off = self.off
        self.off += cols
        return off, self.bld.buf(name)

    def view(self, off, cols, p0=0, p1=128):
        return self.t[p0:p1, off:off + cols]

    def push(self):
        self.marks.append(self.off)

    def pop(self):
        self.off = self.marks.pop()



def _hp(x):
    return np.ascontiguousarray(np.asarray(x, dtype=np.float32))


def _chunk_vec(v):
    return _hp(np.asarray(v, np.float32).reshape(NC, 128))


def host_prep_common(inp):
    d = {}
    d["c_id_bf"] = _hp(np.eye(128))
    d["c_id_f32"] = _hp(np.eye(128))
    d["c_J_f32"] = _hp(np.eye(128)[::-1])
    d["c_ones_row_f32"] = _hp(np.ones((1, 128)))
    d["c_ones_col_f32"] = _hp(np.ones((128, 1)))

    d["l0_wq_t"] = _hp(np.asarray(inp["L0_attn_Wq"]).T)
    d["l0_wk_t"] = _hp(np.asarray(inp["L0_attn_Wk"]).T)
    d["l0_wv_t"] = _hp(np.asarray(inp["L0_attn_Wv"]).T)
    d["l0_wo_t"] = _hp(np.asarray(inp["L0_attn_Wo"]).T)
    d["l0_bq"] = _chunk_vec(inp["L0_attn_bq"])
    d["l0_bk"] = _chunk_vec(inp["L0_attn_bk"])
    d["l0_bv"] = _chunk_vec(inp["L0_attn_bv"])
    d["l0_bo"] = _chunk_vec(inp["L0_attn_bo"])
    rel = np.asarray(inp["L0_attn_rel_bias"], np.float32)
    ext = np.pad(rel, ((0, 0), (639, 639)), mode="edge")
    d["l0_ext"] = _hp(ext)
    d["l0_ext_rev"] = _hp(ext[:, ::-1])
    d["l0_rel_raw"] = _hp(rel)

    d["l0f_rails"] = _hp(inp["L0_ffn_rail_maker"])
    d["l0f_up_t"] = _hp(np.asarray(inp["L0_ffn_rail_up_proj"]).transpose(0, 1, 3, 2))
    d["l0f_wc_t"] = _hp(np.asarray(inp["L0_ffn_w_cproj"]).T)

    d["l1_w_qkv"] = _hp(inp["L1_attn_w_qkv"])
    d["l1_w_out"] = _hp(inp["L1_attn_w_out"])
    d["l1_b_out"] = _chunk_vec(inp["L1_attn_b_out"])
    d["l1_g"] = _chunk_vec(inp["L1_attn_g"])
    d["l1_b"] = _chunk_vec(inp["L1_attn_b"])
    d["l1f_wr_t"] = _hp(np.asarray(inp["L1_ffn_W_r"]).T)
    d["l1f_wk_t"] = _hp(np.asarray(inp["L1_ffn_W_k"]).T)
    d["l1f_wv_t"] = _hp(np.asarray(inp["L1_ffn_W_v"]).T)
    d["l1f_mu_r"] = _chunk_vec(inp["L1_ffn_mu_r"])
    d["l1f_mu_k"] = _chunk_vec(inp["L1_ffn_mu_k"])

    d["l2_wq"] = _hp(inp["L2_attn_Wq"])
    d["l2_wkv"] = _hp(inp["L2_attn_Wkv"])
    d["l2_bq"] = _chunk_vec(inp["L2_attn_bq"])
    bkv = np.asarray(inp["L2_attn_bkv"], np.float32)
    d["l2_bk"] = _chunk_vec(bkv[0:1024])
    d["l2_bv"] = _hp(bkv[1024:2048])
    d["l2_wo"] = _hp(inp["L2_attn_Wo"])
    d["l2_bo"] = _chunk_vec(inp["L2_attn_bo"])
    pos = np.arange(T, dtype=np.float64)
    invf = 1.0 / 10000.0 ** (np.arange(0, 64, 2, dtype=np.float64) / 64.0)
    ang = pos[None, :] * invf[:, None]
    cs = np.cos(ang); sn = np.sin(ang)
    d["l2_cos"] = _hp(np.concatenate([cs, cs], axis=0))
    d["l2_sin"] = _hp(np.concatenate([sn, sn], axis=0))
    tri = np.where(np.arange(128)[:, None] <= np.arange(128)[None, :], 0.0, -1e30)
    d["l2_mtri"] = _hp(tri)
    d["l2_mtri_t"] = _hp(tri.T)
    d["l2f_kv_d_t"] = _hp(np.asarray(inp["L2_ffn_kv_d"]).T)
    d["l2f_q_d_t"] = _hp(np.asarray(inp["L2_ffn_q_d"]).T)
    d["l2f_k_u_t"] = _hp(np.asarray(inp["L2_ffn_k_u"]).T)
    d["l2f_q_u_t"] = _hp(np.asarray(inp["L2_ffn_q_u"]).T)
    d["l2f_v_u_t"] = _hp(np.asarray(inp["L2_ffn_v_u"]).T)
    d["l2f_rope_q_t"] = _hp(np.asarray(inp["L2_ffn_rope_q_w"]).T)
    d["l2f_rope_k_t"] = _hp(np.asarray(inp["L2_ffn_rope_k_w"]).T)
    d["l2f_o_w_t"] = _hp(np.asarray(inp["L2_ffn_o_w"]).T)
    d["l2f_ln1"] = _chunk_vec(inp["L2_ffn_ln1_w"])
    d["l2f_ln2"] = _chunk_vec(inp["L2_ffn_ln2_w"])
    d["l2f_router_t"] = _hp(np.asarray(inp["L2_ffn_router_w"]).T)
    d["l2f_rbias"] = _hp(np.tile(np.asarray(inp["L2_ffn_routing_bias"],
                                            np.float32)[None, :], (128, 1)))
    pos2 = np.arange(T, dtype=np.float64)
    invf2 = 1.0 / 10000.0 ** (np.arange(0, 32, 2, dtype=np.float64) / 32.0)
    idx2 = np.arange(128) % 32 % 16
    ang2 = pos2[None, :] * invf2[idx2][:, None]
    d["l2f_cos2"] = _hp(np.cos(ang2))
    d["l2f_sin2"] = _hp(np.sin(ang2))
    P = np.zeros((128, 128), np.float32)
    for blk in range(4):
        for i in range(16):
            P[blk * 32 + i, blk * 32 + i + 16] = -1.0
            P[blk * 32 + 16 + i, blk * 32 + i] = 1.0
    d["l2f_p32t"] = _hp(P.T)
    gate = np.concatenate([np.asarray(inp["L2_ffn_shared_gate"], np.float32),
                           np.asarray(inp["L2_ffn_routed_gate"], np.float32)])
    up = np.concatenate([np.asarray(inp["L2_ffn_shared_up"], np.float32),
                         np.asarray(inp["L2_ffn_routed_up"], np.float32)])
    down = np.concatenate([np.asarray(inp["L2_ffn_shared_down"], np.float32),
                           np.asarray(inp["L2_ffn_routed_down"], np.float32)])
    d["l2f_gate_t"] = _hp(gate.transpose(0, 2, 1))
    d["l2f_up_t"] = _hp(up.transpose(0, 2, 1))
    d["l2f_down_t"] = _hp(down.transpose(0, 2, 1))

    d["l3_wq"] = _hp(inp["L3_attn_Wq"])
    d["l3_wk"] = _hp(inp["L3_attn_Wk"])
    d["l3_wv"] = _hp(inp["L3_attn_Wv"])
    d["l3_bq"] = _chunk_vec(inp["L3_attn_bq"])
    d["l3_bk"] = _hp(inp["L3_attn_bk"])
    d["l3_bv"] = _hp(inp["L3_attn_bv"])
    d["l3_temp"] = _hp(np.tile(np.asarray(inp["L3_attn_temperature"],
                                          np.float32).reshape(16, 1), (1, 128)))
    d["l3f_w_t"] = _hp(np.asarray(inp["L3_ffn_W"]).T)
    d["l3f_vr"] = _hp(inp["L3_ffn_V_r"])
    d["l3f_c_t"] = _hp(np.asarray(inp["L3_ffn_C"]).T)
    d["l3f_b"] = _chunk_vec(inp["L3_ffn_b"])
    return d


def _merge(primary, filler):
    out = []
    si = 0
    n = max(1, len(primary))
    for i, f in enumerate(primary):
        out.append(f)
        want = (i + 1) * len(filler) // n
        while si < want:
            out.append(filler[si])
            si += 1
    out.extend(filler[si:])
    return out



class Ker:

    def __init__(self, nc, inputs_avail):
        self.nc = nc
        self.b = Bld(nc)
        self.dram = {}
        self.inputs_avail = inputs_avail
        es = self.b.es
        self.X = es.enter_context(nc.sbuf_tensor("X", [128, NC * T], F32))
        self.XBufs = [self.b.buf(f"X{c}") for c in range(NC)]
        self.XH = es.enter_context(nc.sbuf_tensor("XH", [128, NC * T], BF16))
        self.XHBufs = [self.b.buf(f"XH{c}") for c in range(NC)]
        self.OF = es.enter_context(nc.sbuf_tensor("OF", [128, NC * T], BF16))
        self.OFBufs = [self.b.buf(f"OF{c}") for c in range(NC)]
        self.af = Arena(self.b, "AF32", F32, 6144)
        self.ab = Arena(self.b, "ABF16", BF16, 21504)
        self.m_row = es.enter_context(nc.sbuf_tensor("m_row", [1, TB], F32))
        self.r_row = es.enter_context(nc.sbuf_tensor("r_row", [1, TB], F32))
        self.srow1 = es.enter_context(nc.sbuf_tensor("srow1", [1, TB], F32))
        self.srow2 = es.enter_context(nc.sbuf_tensor("srow2", [1, TB], F32))
        self.m_rowB = self.b.buf("m_row")
        self.r_rowB = self.b.buf("r_row")
        self.srow1B = self.b.buf("srow1")
        self.srow2B = self.b.buf("srow2")
        self.cols = es.enter_context(nc.sbuf_tensor("cols", [128, 224], F32))
        self.wsmall = es.enter_context(nc.sbuf_tensor("wsmall", [128, 16], BF16))
        self.wsmallB = self.b.buf("wsmall")
        self.augst = es.enter_context(nc.sbuf_tensor("augst", [16, 128], BF16))
        self.brow_hi = es.enter_context(nc.sbuf_tensor("brow_hi", [1, TB], BF16))
        self.brow_lo = es.enter_context(nc.sbuf_tensor("brow_lo", [1, TB], BF16))
        self.browB = self.b.buf("brow")
        self.ones_row_bf = es.enter_context(nc.sbuf_tensor("ones_row_bf", [1, 128], BF16))
        self.colsB = self.b.buf("cols")
        self.colsB2 = self.b.buf("cols2")
        self.col_alloc = 0
        self.id_bf = es.enter_context(nc.sbuf_tensor("id_bf", [128, 128], BF16))
        self.id_f32 = es.enter_context(nc.sbuf_tensor("id_f32", [128, 128], F32))
        self.J_bf = es.enter_context(nc.sbuf_tensor("J_bf", [128, 128], BF16))
        self.ones_row = es.enter_context(nc.sbuf_tensor("ones_row", [1, 128], F32))
        self.ones_col_bf = es.enter_context(nc.sbuf_tensor("ones_col_bf", [128, 1], BF16))
        self.constB = self.b.buf("consts")
        self.ps_big = es.enter_context(nc.psum_tensor("ps_big", [128, 2048], F32))
        self.ps_bigB = self.b.buf("ps_big")
        self.ps_bigQ = [self.b.buf(f"ps_bigq{i}") for i in range(4)]
        self.ps_a = es.enter_context(nc.psum_tensor("ps_a", [128, 512], F32))
        self.ps_aB = self.b.buf("ps_a")
        self.ps_b = es.enter_context(nc.psum_tensor("ps_b", [128, 512], F32))
        self.ps_bB = self.b.buf("ps_b")
        self.ps_c = es.enter_context(nc.psum_tensor("ps_c", [128, 512], F32))
        self.ps_cB = self.b.buf("ps_c")
        self.ps_d = es.enter_context(nc.psum_tensor("ps_d", [128, 512], F32))
        self.ps_dB = self.b.buf("ps_d")

    def din(self, name, shape):
        if name not in self.dram:
            self.dram[name] = self.nc.dram_tensor(
                name, list(shape), F32, kind="ExternalInput").ap()
            assert name in self.inputs_avail, name
        return self.dram[name]

    def col(self, n=1):
        off = self.col_alloc
        self.col_alloc += n
        assert self.col_alloc <= 224
        return off

    def zero_init(self):
        b = self.b
        b.emit("gpsimd", "memset", self.ab.view(0, self.ab.cols), 0.0,
               writes=[self.b.buf("z0")])
        b.emit("gpsimd", "memset", self.af.view(0, self.af.cols), 0.0,
               writes=[self.b.buf("z1")])
        b.emit("gpsimd", "memset", self.OF[:, :], 0.0, writes=self.OFBufs)
        b.emit("gpsimd", "memset", self.cols[:, :], 0.0, writes=[self.colsB])
        b.emit("vector", "memset", self.ps_big[:, :], 0.0,
               writes=[self.ps_bigB] + self.ps_bigQ)
        b.emit("vector", "memset", self.ps_a[:, :], 0.0, writes=[self.ps_aB])
        b.emit("vector", "memset", self.ps_b[:, :], 0.0, writes=[self.ps_bB])
        b.emit("vector", "memset", self.ps_c[:, :], 0.0, writes=[self.ps_cB])
        b.emit("vector", "memset", self.ps_d[:, :], 0.0, writes=[self.ps_dB])

    def load_consts(self):
        b = self.b
        idb = self.din("c_id_bf", (128, 128))
        idf = self.din("c_id_f32", (128, 128))
        jf = self.din("c_J_f32", (128, 128))
        onr = self.din("c_ones_row_f32", (1, 128))
        b.emit("sync", "dma_start", out=self.id_f32[:], in_=idf,
               writes=[self.constB])
        b.emit("sync", "dma_start", out=self.ones_row[:], in_=onr,
               writes=[self.constB])
        b.emit("gpsimd", "dma_start", out=self.id_bf[:], in_=idb,
               reads=[self.constB], writes=[self.constB])
        b.emit("gpsimd", "dma_start", out=self.J_bf[:], in_=jf,
               reads=[self.constB], writes=[self.constB])
        b.emit("gpsimd", "memset", self.ones_col_bf[:], 1.0,
               writes=[self.constB])
        b.emit("gpsimd", "memset", self.ones_row_bf[:], 1.0,
               writes=[self.constB])
        self.c_eps = self.col(1)
        b.emit("gpsimd", "memset", self.cols[:, self.c_eps:self.c_eps + 1], EPS,
               writes=[self.colsB])

    def prologue(self, xname="x_in"):
        b = self.b
        x = self.din(xname, (T, D))
        self.af.push()
        soff, sB = self.af.take(1024, "prolog_stage")
        for tt in range(16):
            b.emit("sync", "dma_start", out=self.af.view(soff, 1024),
                   in_=x[tt * 128:(tt + 1) * 128, :],
                   reads=[sB], writes=[sB])
            for c in range(NC):
                ps = self.ps_a if (c % 2 == 0) else self.ps_b
                psB = self.ps_aB if (c % 2 == 0) else self.ps_bB
                b.emit("tensor", "transpose", ps[:, 0:128],
                       self.af.view(soff, 1024)[:, c * 128:(c + 1) * 128],
                       self.id_f32[:],
                       reads=[sB, self.constB], writes=[psB])
                b.emit("vector", "tensor_copy",
                       out=self.X[:, c * T + tt * 128: c * T + (tt + 1) * 128],
                       in_=ps[:, 0:128], reads=[psB], writes=[self.XBufs[c]])
        self.af.pop()

    def epilogue(self, yname="y_out"):
        b = self.b
        y = self.nc.dram_tensor(yname, [T, D], F32, kind="ExternalOutput").ap()
        self.af.push()
        soff, sB = self.af.take(1024, "epilog_stage")
        for tt in range(16):
            for c in range(NC):
                ps = self.ps_a if (c % 2 == 0) else self.ps_b
                psB = self.ps_aB if (c % 2 == 0) else self.ps_bB
                b.emit("tensor", "transpose", ps[:, 0:128],
                       self.X[:, c * T + tt * 128: c * T + (tt + 1) * 128],
                       self.id_f32[:],
                       reads=[self.XBufs[c], self.constB], writes=[psB])
                b.emit("vector", "tensor_copy",
                       out=self.af.view(soff, 1024)[:, c * 128:(c + 1) * 128],
                       in_=ps[:, 0:128], reads=[psB], writes=[sB])
            b.emit("sync", "dma_start", out=y[tt * 128:(tt + 1) * 128, :],
                   in_=self.af.view(soff, 1024), reads=[sB], writes=[sB])
        self.af.pop()
        e = self.b.engs["sync"]
        e.items.append(([(e.dsem, e.dn)], None, None, None, None))

    def bcast_row(self, ps, psB, row_ap, rowB, width):
        b = self.b
        b.emit("vector", "tensor_copy", out=self.brow_hi[0:1, 0:width],
               in_=row_ap, reads=[rowB], writes=[self.browB])
        b.emit("vector", "tensor_sub", out=self.brow_lo[0:1, 0:width],
               in0=row_ap, in1=self.brow_hi[0:1, 0:width],
               reads=[rowB, self.browB], writes=[self.browB])
        b.emit("tensor", "matmul", ps[:, 0:width],
               self.ones_row_bf[0:1, :], self.brow_hi[0:1, 0:width],
               start=True, stop=False,
               reads=[self.browB, self.constB], writes=[psB])
        b.emit("tensor", "matmul", ps[:, 0:width],
               self.ones_row_bf[0:1, :], self.brow_lo[0:1, 0:width],
               start=False, stop=True,
               reads=[self.browB, self.constB], writes=[psB])

    def flush_aug(self, qoffh, qBh):
        b = self.b
        b.emit("tensor", "matmul", self.ps_c[0:16, 0:128],
               self.wsmall[:, 0:16], self.id_bf[:],
               start=True, stop=True,
               reads=[self.wsmallB, self.constB], writes=[self.ps_cB])
        b.emit("scalar", "copy", out=self.augst[:],
               in_=self.ps_c[0:16, 0:128],
               reads=[self.ps_cB], writes=[self.wsmallB])
        b.emit("gpsimd", "dma_start",
               out=self.ab.view(qoffh, T, 64, 65),
               in_=self.augst[:],
               reads=[self.wsmallB], writes=[qBh])

    def layernorm(self, scale_col=None, bias_col=None):
        b = self.b
        self.af.push()
        self.ab.push()
        sqoff, sqB = self.ab.take(TB, "ln_sq")
        tmpoff, tmpB = self.af.take(TB, "ln_tmp")
        for tb in range(NTB):
            for c in range(NC):
                b.emit("scalar", "copy",
                       out=self.XH[:, c * T + tb * TB: c * T + (tb + 1) * TB],
                       in_=self.X[:, c * T + tb * TB: c * T + (tb + 1) * TB],
                       reads=[self.XBufs[c]], writes=[self.XHBufs[c]])
                b.emit("scalar", "activation",
                       out=self.ab.view(sqoff, TB),
                       in_=self.XH[:, c * T + tb * TB: c * T + (tb + 1) * TB],
                       func=AF.Square,
                       reads=[self.XHBufs[c]], writes=[sqB])
                b.emit("tensor", "matmul", self.ps_c[0:1, :],
                       self.ones_col_bf[:],
                       self.XH[:, c * T + tb * TB: c * T + (tb + 1) * TB],
                       start=(c == 0), stop=(c == NC - 1),
                       reads=[self.XHBufs[c], self.constB], writes=[self.ps_cB])
                b.emit("tensor", "matmul", self.ps_d[0:1, :],
                       self.ones_col_bf[:], self.ab.view(sqoff, TB),
                       start=(c == 0), stop=(c == NC - 1),
                       reads=[sqB, self.constB], writes=[self.ps_dB])
            mr = self.m_row[0:1, :]
            rr = self.r_row[0:1, :]
            s1 = self.srow1[0:1, :]
            s2 = self.srow2[0:1, :]
            b.emit("vector", "tensor_scalar", out=mr, in0=self.ps_c[0:1, :],
                   scalar1=1.0 / D, scalar2=None, op0=ALU.mult,
                   reads=[self.ps_cB], writes=[self.m_rowB])
            b.emit("vector", "tensor_scalar", out=s1, in0=self.ps_d[0:1, :],
                   scalar1=1.0 / D, scalar2=None, op0=ALU.mult,
                   reads=[self.ps_dB], writes=[self.srow1B])
            b.emit("vector", "tensor_mul", out=s2, in0=mr, in1=mr,
                   reads=[self.m_rowB], writes=[self.srow2B])
            b.emit("vector", "tensor_sub", out=s1, in0=s1, in1=s2,
                   reads=[self.srow1B, self.srow2B], writes=[self.srow1B])
            b.emit("scalar", "activation", out=s1, in_=s1, func=AF.Sqrt,
                   bias=self.cols[0:1, self.c_eps:self.c_eps + 1],
                   reads=[self.srow1B, self.colsB], writes=[self.srow1B])
            b.emit("vector", "reciprocal", out=rr, in_=s1,
                   reads=[self.srow1B], writes=[self.r_rowB])
            self.bcast_row(self.ps_c, self.ps_cB, mr, self.m_rowB, TB)
            self.bcast_row(self.ps_d, self.ps_dB, rr, self.r_rowB, TB)
            for c in range(NC):
                b.emit("vector", "tensor_sub",
                       out=self.af.view(tmpoff, TB),
                       in0=self.X[:, c * T + tb * TB: c * T + (tb + 1) * TB],
                       in1=self.ps_c[:],
                       reads=[self.XBufs[c], self.ps_cB], writes=[tmpB])
                b.emit("vector", "tensor_mul",
                       out=self.XH[:, c * T + tb * TB: c * T + (tb + 1) * TB],
                       in0=self.af.view(tmpoff, TB), in1=self.ps_d[:],
                       reads=[tmpB, self.ps_dB], writes=[self.XHBufs[c]])
                if scale_col is not None:
                    sc_t, sc_B, sc_off = scale_col
                    bi_t, bi_B, bi_off = bias_col
                    b.emit("vector", "tensor_scalar",
                           out=self.XH[:, c * T + tb * TB: c * T + (tb + 1) * TB],
                           in0=self.XH[:, c * T + tb * TB: c * T + (tb + 1) * TB],
                           scalar1=sc_t[:, sc_off + c:sc_off + c + 1],
                           scalar2=bi_t[:, bi_off + c:bi_off + c + 1],
                           op0=ALU.mult, op1=ALU.add,
                           reads=[self.XHBufs[c], sc_B], writes=[self.XHBufs[c]])
        self.af.pop()
        self.ab.pop()
        self.b.barrier()

    def load_wt_bf16(self, dram_ap, rows, cols, ab_off, abB, st_off=None,
                     stB=None, cast_eng=None):
        b = self.b
        nk = rows // 128
        for kc in range(nk):
            b.emit("gpsimd", "dma_start",
                   out=self.ab.view(ab_off + kc * cols, cols),
                   in_=dram_ap[kc * 128:(kc + 1) * 128, :],
                   reads=[abB], writes=[abB])

    def load_cols(self, dram_name, shape, coff, n):
        b = self.b
        v = self.din(dram_name, shape)
        for c in range(n):
            b.emit("sync", "dma_start",
                   out=self.cols[:, coff + c:coff + c + 1],
                   in_=v[c:c + 1, :].rearrange("a b -> b a"),
                   reads=[self.colsB], writes=[self.colsB])

    def proj_fm(self, wt_ab_off, wtB, out_t, out_bufs, out_base_col, osize,
                bias_coff=None, scale=None, in_t=None, in_bufs=None,
                in_base=0, ksize=D, evac="scalar", accum_ps=None):
        b = self.b
        if in_t is None:
            in_t, in_bufs = self.XH, self.XHBufs
        nk = ksize // 128
        no = osize // 128
        for oc in range(no):
            for tb in range(NTB):
                ps = self.ps_a if ((oc * NTB + tb) % 2 == 0) else self.ps_b
                psB = self.ps_aB if ((oc * NTB + tb) % 2 == 0) else self.ps_bB
                for kc in range(nk):
                    b.emit("tensor", "matmul", ps[:],
                           self.ab.view(wt_ab_off + kc * osize, osize)[:, oc * 128:(oc + 1) * 128],
                           in_t[:, in_base + kc * T + tb * TB: in_base + kc * T + (tb + 1) * TB],
                           start=(kc == 0), stop=(kc == nk - 1),
                           reads=[wtB, in_bufs[kc]], writes=[psB])
                kwargs = {}
                if bias_coff is not None:
                    kwargs["bias"] = self.cols[:, bias_coff + oc:bias_coff + oc + 1]
                if scale is not None:
                    kwargs["scale"] = scale
                b.emit(evac, "activation",
                       out=out_t[:, out_base_col + oc * T + tb * TB: out_base_col + oc * T + (tb + 1) * TB],
                       in_=ps[:], func=AF.Copy,
                       reads=[psB, self.colsB], writes=[out_bufs[oc]], **kwargs)

    def l0_attn(self):
        b = self.b
        nc = self.nc
        self.layernorm()
        wq = self.din("l0_wq_t", (D, D))
        wk = self.din("l0_wk_t", (D, D))
        wv = self.din("l0_wv_t", (D, D))
        wo = self.din("l0_wo_t", (D, D))
        ext = self.din("l0_ext", (16, 1535))
        ext_rev = self.din("l0_ext_rev", (16, 1535))
        rel_raw = self.din("l0_rel_raw", (16, 257))
        cb_q = self.col(NC); self.load_cols("l0_bq", (NC, 128), cb_q, NC)
        cb_k = self.col(NC); self.load_cols("l0_bk", (NC, 128), cb_k, NC)
        cb_o = self.col(NC); self.load_cols("l0_bo", (NC, 128), cb_o, NC)
        bv_dram = self.din("l0_bv", (NC, 128))
        b.emit("vector", "tensor_scalar",
               out=self.cols[:, cb_q:cb_q + NC], in0=self.cols[:, cb_q:cb_q + NC],
               scalar1=0.125, scalar2=None, op0=ALU.mult,
               reads=[self.colsB], writes=[self.colsB])
        c_lo = self.col(2)

        self.ab.push()
        self.af.push()
        wsl_off, wslB = self.ab.take(3 * 1024, "l0_wslices")
        q1_off, q1B = self.ab.take(T, "Q1")
        q2_off, q2B = self.ab.take(T, "Q2")
        k1_off, k1B = self.ab.take(T, "K1")
        k2_off, k2B = self.ab.take(T, "K2")
        va_off, vaB = self.ab.take(16 * 130, "VA")
        pt_off, ptB = self.ab.take(2 * TB, "PT")
        u1_off, u1B = self.ab.take(1152, "U_h1")
        u2_off, u2B = self.ab.take(1152, "U_h2")
        w1_off, w1B = self.ab.take(1152, "U2_h1")
        w2_off, w2B = self.ab.take(1152, "U2_h2")
        st_off, stB = self.af.take(1024, "wstage")
        hk_off, hkB = self.af.take(512, "hankel")
        nr_off, nrB = self.af.take(1024, "near")
        vb_off, vbB = self.af.take(128, "vbias")
        c_m = self.col(4)

        def build_toeplitz(dram_row_t, h, base_idx, dst_off, dstB):
            for g0 in range(0, 1152, 512):
                w = min(512, 1152 - g0)
                b.emit("gpsimd", "dma_start",
                       out=self.ab.view(pt_off, TB)[:, 0:w],
                       in_=AP(dram_row_t.tensor, h * 1535 + base_idx + g0,
                              [[1, 128], [1, w]]),
                       reads=[ptB], writes=[ptB])
                b.emit("tensor", "matmul", self.ps_c[:, 0:w],
                       self.J_bf[:], self.ab.view(pt_off, TB)[:, 0:w],
                       start=True, stop=True,
                       reads=[ptB, self.constB], writes=[self.ps_cB])
                b.emit("scalar", "activation",
                       out=self.ab.view(dst_off + g0, w),
                       in_=self.ps_c[:, 0:w], func=AF.Copy,
                       reads=[self.ps_cB], writes=[dstB])

        b.emit("gpsimd", "memset", self.ab.view(k1_off, T, 64, 65), 1.0, writes=[k1B])
        b.emit("gpsimd", "memset", self.ab.view(k2_off, T, 64, 65), 1.0, writes=[k2B])
        for tt in range(16):
            b.emit("gpsimd", "memset",
                   self.ab.view(va_off + tt * 130 + 64, 1), 1.0, writes=[vaB])
            b.emit("gpsimd", "memset",
                   self.ab.view(va_off + tt * 130 + 129, 1), 1.0, writes=[vaB])

        for hp in range(8):
            h1, h2 = 2 * hp, 2 * hp + 1
            for wi, wdr in enumerate((wq, wk, wv)):
                for kc in range(NC):
                    b.emit("gpsimd", "dma_start",
                           out=self.ab.view(wsl_off + wi * 1024 + kc * 128, 128),
                           in_=wdr[kc * 128:(kc + 1) * 128, hp * 128:(hp + 1) * 128],
                           reads=[wslB], writes=[wslB])
            for (dst_off, dstB, head_half, wi, coff, scl) in (
                    (q1_off, q1B, 0, 0, cb_q, 0.125),
                    (q2_off, q2B, 1, 0, cb_q, 0.125),
                    (k1_off, k1B, 0, 1, cb_k, None),
                    (k2_off, k2B, 1, 1, cb_k, None)):
                for tb in range(NTB):
                    ps = self.ps_a if tb % 2 == 0 else self.ps_b
                    psB = self.ps_aB if tb % 2 == 0 else self.ps_bB
                    for kc in range(NC):
                        b.emit("tensor", "matmul", ps[:],
                               self.ab.view(wsl_off + wi * 1024 + kc * 128, 128),
                               self.XH[:, kc * T + tb * TB: kc * T + (tb + 1) * TB],
                               start=(kc == 0), stop=(kc == NC - 1),
                               reads=[wslB, self.XHBufs[kc]], writes=[psB])
                    bias_ap = self.cols[head_half * 64:(head_half + 1) * 64,
                                        coff + hp:coff + hp + 1]
                    if scl is not None:
                        b.emit("vector", "tensor_scalar",
                               out=self.ab.view(dst_off + tb * TB, TB, 0, 64),
                               in0=ps[head_half * 64:(head_half + 1) * 64, :],
                               scalar1=scl, scalar2=bias_ap,
                               op0=ALU.mult, op1=ALU.add,
                               reads=[psB, self.colsB], writes=[dstB])
                    else:
                        b.emit("vector", "tensor_scalar",
                               out=self.ab.view(dst_off + tb * TB, TB, 0, 64),
                               in0=ps[head_half * 64:(head_half + 1) * 64, :],
                               scalar1=bias_ap, scalar2=None, op0=ALU.add,
                               reads=[psB, self.colsB], writes=[dstB])
            b.emit("sync", "dma_start",
                   out=self.af.view(vb_off, 128),
                   in_=AP(bv_dram.tensor, hp * 128, [[0, 128], [1, 128]]),
                   reads=[vbB], writes=[vbB])
            def mk_vp(tt):
                def f():
                    ps = self.ps_a if tt % 2 == 0 else self.ps_b
                    psB = self.ps_aB if tt % 2 == 0 else self.ps_bB
                    for kc in range(NC):
                        b.emit("tensor", "matmul", ps[:, 0:128],
                               self.XH[:, kc * T + tt * 128: kc * T + (tt + 1) * 128],
                               self.ab.view(wsl_off + 2 * 1024 + kc * 128, 128),
                               start=(kc == 0), stop=(kc == NC - 1),
                               reads=[wslB, self.XHBufs[kc]], writes=[psB])
                    b.emit("vector", "tensor_add",
                           out=self.ab.view(va_off, 16 * 130)[:, tt * 130: tt * 130 + 64],
                           in0=ps[:, 0:64], in1=self.af.view(vb_off, 128)[:, 0:64],
                           reads=[psB, vbB], writes=[vaB])
                    b.emit("vector", "tensor_add",
                           out=self.ab.view(va_off, 16 * 130)[:, tt * 130 + 65: tt * 130 + 129],
                           in0=ps[:, 64:128], in1=self.af.view(vb_off, 128)[:, 64:128],
                           reads=[psB, vbB], writes=[vaB])
                return f
            build_toeplitz(ext_rev, h1, 128, u1_off, u1B)
            build_toeplitz(ext_rev, h2, 128, u2_off, u2B)
            build_toeplitz(ext, h1, 128, w1_off, w1B)
            build_toeplitz(ext, h2, 128, w2_off, w2B)
            for i, h in enumerate((h1, h2)):
                b.emit("sync", "dma_start",
                       out=self.cols[:, c_lo + i:c_lo + i + 1],
                       in_=AP(rel_raw.tensor, h * 257 + 0, [[0, 128], [1, 1]]),
                       reads=[self.colsB], writes=[self.colsB])
                b.emit("sync", "dma_start",
                       out=self.cols[:, c_m + i:c_m + i + 1],
                       in_=AP(rel_raw.tensor, h * 257 + 256, [[0, 128], [1, 1]]),
                       reads=[self.colsB], writes=[self.colsB])

            l0heads = ((q1_off, q1B, k1_off, k1B, u1_off, u1B, w1_off, w1B),
                       (q2_off, q2B, k2_off, k2B, u2_off, u2B, w2_off, w2B))

            def l0_stats_steps(hi):
                qoffh, qBh, koffh, kBh, uoff, uB, woff, wB = l0heads[hi]
                steps = []
                for qt in range(16):
                    def mk(qt=qt, qoffh=qoffh, qBh=qBh, koffh=koffh, kBh=kBh,
                           uoff=uoff, uB=uB, hi=hi):
                        for kb in range(NTB):
                            b.emit("tensor", "matmul",
                                   self.ps_big[:, kb * TB:(kb + 1) * TB],
                                   self.ab.view(qoffh + qt * 128, 128, 0, 64),
                                   self.ab.view(koffh + kb * TB, TB, 0, 64),
                                   start=True, stop=True,
                                   reads=[qBh, kBh], writes=[self.ps_bigQ[kb]])
                        near = [kb for kb in range(NTB)
                                if -128 <= 128 * qt - 512 * kb <= 512]
                        lows = [kb for kb in range(NTB) if 128 * qt - 512 * kb < -128]
                        highs = [kb for kb in range(NTB) if 128 * qt - 512 * kb > 512]
                        for j, kb in enumerate(near):
                            d1 = 128 * qt - 512 * kb
                            o_m = 512 - d1
                            b.emit("vector", "tensor_add",
                                   out=self.af.view(nr_off + j * TB, TB),
                                   in0=self.ps_big[:, kb * TB:(kb + 1) * TB],
                                   in1=self.ab.view(uoff + o_m, TB),
                                   reads=[self.ps_bigQ[kb], uB], writes=[nrB])
                        b.emit("vector", "reduce_max",
                               out=self.cols[:, c_m + 2:c_m + 3],
                               in_=self.af.view(nr_off, len(near) * TB),
                               axis=AX.X, reads=[nrB], writes=[self.colsB2])
                        for kbs, ccol in ((lows, c_lo + hi), (highs, c_m + hi)):
                            if not kbs:
                                continue
                            kb0, kb1 = min(kbs), max(kbs) + 1
                            b.emit("vector", "reduce_max",
                                   out=self.cols[:, c_m + 3:c_m + 4],
                                   in_=self.ps_big[:, kb0 * TB:kb1 * TB],
                                   axis=AX.X, reads=self.ps_bigQ[kb0:kb1],
                                   writes=[self.colsB2])
                            b.emit("vector", "tensor_add",
                                   out=self.cols[:, c_m + 3:c_m + 4],
                                   in0=self.cols[:, c_m + 3:c_m + 4],
                                   in1=self.cols[:, ccol:ccol + 1],
                                   reads=[self.colsB, self.colsB2],
                                   writes=[self.colsB2])
                            b.emit("vector", "tensor_max",
                                   out=self.cols[:, c_m + 2:c_m + 3],
                                   in0=self.cols[:, c_m + 2:c_m + 3],
                                   in1=self.cols[:, c_m + 3:c_m + 4],
                                   reads=[self.colsB2], writes=[self.colsB2])
                        b.emit("vector", "tensor_scalar",
                               out=self.wsmall[:, qt:qt + 1],
                               in0=self.cols[:, c_m + 2:c_m + 3],
                               scalar1=-1.0, scalar2=None, op0=ALU.mult,
                               reads=[self.colsB2], writes=[self.wsmallB])
                    steps.append(mk)
                return steps

            def l0_pass2_steps(hi):
                qoffh, qBh, koffh, kBh, uoff, uB, woff, wB = l0heads[hi]
                steps = []

                def mk_s(qb, kc):
                    def f():
                        d2 = 512 * qb - 128 * kc
                        ps = self.ps_a if kc % 2 == 0 else self.ps_b
                        psB = self.ps_aB if kc % 2 == 0 else self.ps_bB
                        b.emit("tensor", "matmul", ps[:],
                               self.ab.view(koffh + kc * 128, 128, 0, 65),
                               self.ab.view(qoffh + qb * TB, TB, 0, 65),
                               start=True, stop=True,
                               reads=[kBh, qBh], writes=[psB])
                        kw = {}
                        if d2 <= -639:
                            kw["bias"] = self.cols[:, c_lo + hi:c_lo + hi + 1]
                        elif d2 >= 255:
                            kw["bias"] = self.cols[:, c_m + hi:c_m + hi + 1]
                        else:
                            o2 = d2 + 512
                            b.emit("vector", "tensor_add",
                                   out=ps[:], in0=ps[:],
                                   in1=self.ab.view(woff + o2, TB),
                                   reads=[psB, wB], writes=[psB])
                        pslot = pt_off if kc % 2 == 0 else pt_off + TB
                        b.emit("scalar", "activation",
                               out=self.ab.view(pslot, TB),
                               in_=ps[:], func=AF.Exp,
                               reads=[psB, self.colsB], writes=[ptB], **kw)
                    return f

                def mk_av(qb, kc):
                    def f():
                        pslot = pt_off if kc % 2 == 0 else pt_off + TB
                        b.emit("tensor", "matmul", self.ps_c[0:65, :],
                               self.ab.view(va_off, 16 * 130)[:, kc * 130 + hi * 65: kc * 130 + hi * 65 + 65],
                               self.ab.view(pslot, TB),
                               start=(kc == 0), stop=(kc == 15),
                               reads=[vaB, ptB], writes=[self.ps_cB])
                    return f

                def mk_norm(qb):
                    def f():
                        b.emit("vector", "reciprocal",
                               out=self.srow1[0:1, 0:TB],
                               in_=self.ps_c[64:65, :],
                               reads=[self.ps_cB], writes=[self.srow1B])
                        self.bcast_row(self.ps_d, self.ps_dB,
                                       self.srow1[0:1, 0:TB], self.srow1B, TB)
                        oc = hp
                        b.emit("scalar", "activation",
                               out=self.ab.view(pt_off, TB, 0, 64),
                               in_=self.ps_c[0:64, :], func=AF.Copy,
                               reads=[self.ps_cB], writes=[ptB])
                        b.emit("vector", "tensor_mul",
                               out=self.OF[hi * 64:(hi + 1) * 64,
                                           oc * T + qb * TB: oc * T + (qb + 1) * TB],
                               in0=self.ab.view(pt_off, TB, 0, 64),
                               in1=self.ps_d[0:64, :],
                               reads=[ptB, self.ps_dB], writes=[self.OFBufs[oc]])
                    return f

                for qb in range(NTB):
                    for kc in range(16):
                        steps.append(mk_s(qb, kc))
                        if kc > 0:
                            steps.append(mk_av(qb, kc - 1))
                    steps.append(mk_av(qb, 15))
                    steps.append(mk_norm(qb))
                return steps

            for f in _merge([mk_vp(tt) for tt in range(16)], l0_stats_steps(0)):
                f()
            self.flush_aug(q1_off, q1B)
            for f in _merge(l0_pass2_steps(0), l0_stats_steps(1)):
                f()
            self.flush_aug(q2_off, q2B)
            for f in l0_pass2_steps(1):
                f()
        self.ab.pop()
        self.af.pop()
        self.b.barrier()
        self.ab.push()
        self.af.push()
        st_off, stB = self.af.take(1024, "wstage2")
        wo_off, woB = self.ab.take(8 * 1024, "l0_wo")
        self.load_wt_bf16(wo, D, D, wo_off, woB, st_off, stB)
        for oc in range(NC):
            for tb in range(NTB):
                ps = self.ps_a if (oc + tb) % 2 == 0 else self.ps_b
                psB = self.ps_aB if (oc + tb) % 2 == 0 else self.ps_bB
                for kc in range(NC):
                    b.emit("tensor", "matmul", ps[:],
                           self.ab.view(wo_off + kc * 1024, 1024)[:, oc * 128:(oc + 1) * 128],
                           self.OF[:, kc * T + tb * TB: kc * T + (tb + 1) * TB],
                           start=(kc == 0), stop=(kc == NC - 1),
                           reads=[woB, self.OFBufs[kc]], writes=[psB])
                b.emit("vector", "scalar_tensor_tensor",
                       out=self.X[:, oc * T + tb * TB: oc * T + (tb + 1) * TB],
                       in0=ps[:], scalar=self.cols[:, cb_o + oc:cb_o + oc + 1],
                       in1=self.X[:, oc * T + tb * TB: oc * T + (tb + 1) * TB],
                       op0=ALU.add, op1=ALU.add,
                       reads=[psB, self.colsB, self.XBufs[oc]],
                       writes=[self.XBufs[oc]])
        self.ab.pop()
        self.af.pop()


    def l0_ffn(self):
        b = self.b
        self.layernorm()
        rails = self.din("l0f_rails", (4, 64, 16, 16))
        up_t = self.din("l0f_up_t", (4, 64, 16, 64))
        wc_t = self.din("l0f_wc_t", (16384, 1024))
        self.ab.push()
        self.af.push()
        OFall = self.b.buf("OFall")
        h_off, hB = self.ab.take(16 * TB, "l0f_h")
        b1_off, b1B = self.ab.take(4 * 128, "l0f_b1")
        b2_off, b2B = self.ab.take(16 * 128, "l0f_b2")
        y_off, yB = self.ab.take(4 * TB, "l0f_y")
        sg_off, sgB = self.ab.take(64, "l0f_stg_bf")
        st_off, stB = self.af.take(1024, "l0f_stage")
        r_off, rB = self.af.take(64, "l0f_railstage")

        for ig in range(8):
            h = ig // 2
            half = ig % 2
            for kc in range(16):
                b.emit("gpsimd", "dma_start",
                       out=self.OF[:, kc * 1024:(kc + 1) * 1024],
                       in_=wc_t[ig * 2048 + kc * 128: ig * 2048 + (kc + 1) * 128, :],
                       reads=[OFall], writes=[OFall])
            for i in range(4):
                yc = 4 * half + i
                b.emit("gpsimd", "memset", self.ab.view(b1_off + i * 128, 128),
                       0.0, writes=[b1B])
                b.emit("sync", "dma_start",
                       out=self.af.view(r_off, 16),
                       in_=AP(rails.tensor, (h * 64 + 8 * yc) * 256,
                              [[16, 128], [1, 16]]),
                       reads=[rB], writes=[rB])
                b.emit("scalar", "copy",
                       out=self.ab.view(sg_off, 64)[:, 0:16],
                       in_=self.af.view(r_off, 64)[:, 0:16],
                       reads=[rB], writes=[sgB])
                for rr in range(8):
                    b.emit("gpsimd", "dma_start",
                           out=self.ab.view(b1_off + i * 128 + rr * 16, 16,
                                            rr * 16, rr * 16 + 16),
                           in_=self.ab.view(sg_off, 16, rr * 16, rr * 16 + 16),
                           reads=[sgB], writes=[b1B])
            for zi in range(16):
                zc = ig * 16 + zi
                oc_h = zc % 32
                r0 = 2 * oc_h
                b.emit("gpsimd", "memset", self.ab.view(b2_off + zi * 128, 128),
                       0.0, writes=[b2B])
                b.emit("sync", "dma_start",
                       out=self.af.view(r_off, 64)[0:32, :],
                       in_=AP(up_t.tensor, (h * 64 + r0) * 1024,
                              [[64, 32], [1, 64]]),
                       reads=[rB], writes=[rB])
                b.emit("scalar", "copy",
                       out=self.ab.view(sg_off, 64)[0:32, :],
                       in_=self.af.view(r_off, 64)[0:32, :],
                       reads=[rB], writes=[sgB])
                for rr in range(2):
                    b.emit("gpsimd", "dma_start",
                           out=self.ab.view(b2_off + zi * 128 + rr * 64, 64,
                                            ((r0 + rr) % 8) * 16,
                                            ((r0 + rr) % 8) * 16 + 16),
                           in_=self.ab.view(sg_off, 64, rr * 16, rr * 16 + 16),
                           reads=[sgB], writes=[b2B])
            for tb in range(NTB):
                for i in range(4):
                    yc = 4 * half + i
                    psy, psyB = ((self.ps_a, self.ps_aB) if i % 2 == 0
                                 else (self.ps_c, self.ps_cB))
                    b.emit("tensor", "matmul", psy[:],
                           self.ab.view(b1_off + i * 128, 128),
                           self.XH[:, yc * T + tb * TB: yc * T + (tb + 1) * TB],
                           start=True, stop=True,
                           reads=[b1B, self.XHBufs[yc]], writes=[psyB])
                    b.emit("scalar", "copy",
                           out=self.ab.view(y_off + i * TB, TB),
                           in_=psy[:], reads=[psyB], writes=[yB])
                for zi in range(16):
                    zc = ig * 16 + zi
                    i = (zc % 32) // 4 - 4 * half
                    psz, pszB = ((self.ps_b, self.ps_bB) if zi % 2 == 0
                                 else (self.ps_d, self.ps_dB))
                    b.emit("tensor", "matmul", psz[:],
                           self.ab.view(b2_off + zi * 128, 128),
                           self.ab.view(y_off + i * TB, TB),
                           start=True, stop=True,
                           reads=[b2B, yB], writes=[pszB])
                    b.emit("scalar", "activation",
                           out=self.ab.view(h_off + zi * TB, TB),
                           in_=psz[:], func=AF.Gelu,
                           reads=[pszB], writes=[hB])
                for oc in range(8):
                    ps = self.ps_c if oc % 2 == 0 else self.ps_d
                    psB = self.ps_cB if oc % 2 == 0 else self.ps_dB
                    for zi in range(16):
                        b.emit("tensor", "matmul", ps[:],
                               self.OF[:, zi * 1024 + oc * 128: zi * 1024 + (oc + 1) * 128],
                               self.ab.view(h_off + zi * TB, TB),
                               start=(zi == 0), stop=(zi == 15),
                               reads=[OFall, hB], writes=[psB])
                    b.emit("vector", "tensor_add",
                           out=self.X[:, oc * T + tb * TB: oc * T + (tb + 1) * TB],
                           in0=self.X[:, oc * T + tb * TB: oc * T + (tb + 1) * TB],
                           in1=ps[:],
                           reads=[self.XBufs[oc], psB], writes=[self.XBufs[oc]])
        self.ab.pop()
        self.af.pop()

    def l1_attn(self):
        b = self.b
        self.layernorm()
        self.layernorm_of_xh()
        wqkv = self.din("l1_w_qkv", (D, 3 * D))
        wout = self.din("l1_w_out", (D, D))
        cb_o = self.col(NC); self.load_cols("l1_b_out", (NC, 128), cb_o, NC)

        self.ab.push()
        self.af.push()
        wsl_off, wslB = self.ab.take(3 * 1024, "l1_wsl")
        q1_off, q1B = self.ab.take(T, "Q1")
        q2_off, q2B = self.ab.take(T, "Q2")
        k1_off, k1B = self.ab.take(T, "K1")
        k2_off, k2B = self.ab.take(T, "K2")
        va_off, vaB = self.ab.take(16 * 130, "VA")
        pt_off, ptB = self.ab.take(2 * TB, "PT")
        st_off, stB = self.af.take(1024, "wstage")
        c_m = self.col(2)

        b.emit("gpsimd", "memset", self.ab.view(k1_off, T, 64, 65), 1.0, writes=[k1B])
        b.emit("gpsimd", "memset", self.ab.view(k2_off, T, 64, 65), 1.0, writes=[k2B])
        for tt in range(16):
            b.emit("gpsimd", "memset",
                   self.ab.view(va_off + tt * 130 + 64, 1), 1.0, writes=[vaB])
            b.emit("gpsimd", "memset",
                   self.ab.view(va_off + tt * 130 + 129, 1), 1.0, writes=[vaB])

        for hp in range(8):
            for wi, colbase in enumerate((hp * 128, D + hp * 128, 2 * D + hp * 128)):
                for kc in range(NC):
                    b.emit("gpsimd", "dma_start",
                           out=self.ab.view(wsl_off + wi * 1024 + kc * 128, 128),
                           in_=wqkv[kc * 128:(kc + 1) * 128, colbase:colbase + 128],
                           reads=[wslB], writes=[wslB])
            for (dst_off, dstB, head_half, wi, scl) in (
                    (q1_off, q1B, 0, 0, 0.125), (q2_off, q2B, 1, 0, 0.125),
                    (k1_off, k1B, 0, 1, None), (k2_off, k2B, 1, 1, None)):
                for tb in range(NTB):
                    ps = self.ps_a if tb % 2 == 0 else self.ps_b
                    psB = self.ps_aB if tb % 2 == 0 else self.ps_bB
                    for kc in range(NC):
                        b.emit("tensor", "matmul", ps[:],
                               self.ab.view(wsl_off + wi * 1024 + kc * 128, 128),
                               self.XH[:, kc * T + tb * TB: kc * T + (tb + 1) * TB],
                               start=(kc == 0), stop=(kc == NC - 1),
                               reads=[wslB, self.XHBufs[kc]], writes=[psB])
                    if scl is not None:
                        b.emit("vector", "tensor_scalar",
                               out=self.ab.view(dst_off + tb * TB, TB, 0, 64),
                               in0=ps[head_half * 64:(head_half + 1) * 64, :],
                               scalar1=scl, scalar2=None, op0=ALU.mult,
                               reads=[psB], writes=[dstB])
                    else:
                        b.emit("scalar", "copy",
                               out=self.ab.view(dst_off + tb * TB, TB, 0, 64),
                               in_=ps[head_half * 64:(head_half + 1) * 64, :],
                               reads=[psB], writes=[dstB])
            def mk_vp(tt):
                def f():
                    ps = self.ps_a if tt % 2 == 0 else self.ps_b
                    psB = self.ps_aB if tt % 2 == 0 else self.ps_bB
                    for kc in range(NC):
                        b.emit("tensor", "matmul", ps[:, 0:128],
                               self.XH[:, kc * T + tt * 128: kc * T + (tt + 1) * 128],
                               self.ab.view(wsl_off + 2 * 1024 + kc * 128, 128),
                               start=(kc == 0), stop=(kc == NC - 1),
                               reads=[wslB, self.XHBufs[kc]], writes=[psB])
                    b.emit("scalar", "copy",
                           out=self.ab.view(va_off, 16 * 130)[:, tt * 130: tt * 130 + 64],
                           in_=ps[:, 0:64], reads=[psB], writes=[vaB])
                    b.emit("scalar", "copy",
                           out=self.ab.view(va_off, 16 * 130)[:, tt * 130 + 65: tt * 130 + 129],
                           in_=ps[:, 64:128], reads=[psB], writes=[vaB])
                return f
            heads = ((q1_off, q1B, k1_off, k1B), (q2_off, q2B, k2_off, k2B))

            def l1_stats_steps(hi):
                qoffh, qBh, koffh, kBh = heads[hi]
                steps = []
                for qt in range(16):
                    def mk(qt=qt, qoffh=qoffh, qBh=qBh, koffh=koffh, kBh=kBh):
                        wq_allow = (qt // 2 + 1) * 256
                        nkb = (wq_allow + TB - 1) // TB
                        for kb in range(nkb):
                            w = min(TB, wq_allow - kb * TB)
                            b.emit("tensor", "matmul",
                                   self.ps_big[:, kb * TB: kb * TB + w],
                                   self.ab.view(qoffh + qt * 128, 128, 0, 64),
                                   self.ab.view(koffh + kb * TB, w, 0, 64),
                                   start=True, stop=True,
                                   reads=[qBh, kBh], writes=[self.ps_bigQ[kb]])
                        b.emit("vector", "reduce_max",
                               out=self.wsmall[:, qt:qt + 1],
                               in_=self.ps_big[:, 0:wq_allow],
                               axis=AX.X, negate=True,
                               reads=self.ps_bigQ[0:nkb], writes=[self.wsmallB])
                    steps.append(mk)
                return steps

            def l1_pass2_steps(hi):
                qoffh, qBh, koffh, kBh = heads[hi]
                steps = []

                def mk_s(qb, kc):
                    def f():
                        half_chunk = (kc >= 4 * qb + 2)
                        ps = self.ps_a if kc % 2 == 0 else self.ps_b
                        psB = self.ps_aB if kc % 2 == 0 else self.ps_bB
                        pslot = pt_off if kc % 2 == 0 else pt_off + TB
                        if half_chunk:
                            b.emit("gpsimd", "memset",
                                   self.ab.view(pslot, 256), 0.0, writes=[ptB])
                            b.emit("tensor", "matmul", ps[:, 0:256],
                                   self.ab.view(koffh + kc * 128, 128, 0, 65),
                                   self.ab.view(qoffh + qb * TB + 256, 256, 0, 65),
                                   start=True, stop=True,
                                   reads=[kBh, qBh], writes=[psB])
                            b.emit("scalar", "activation",
                                   out=self.ab.view(pslot + 256, 256),
                                   in_=ps[:, 0:256], func=AF.Exp,
                                   reads=[psB], writes=[ptB])
                        else:
                            b.emit("tensor", "matmul", ps[:],
                                   self.ab.view(koffh + kc * 128, 128, 0, 65),
                                   self.ab.view(qoffh + qb * TB, TB, 0, 65),
                                   start=True, stop=True,
                                   reads=[kBh, qBh], writes=[psB])
                            b.emit("scalar", "activation",
                                   out=self.ab.view(pslot, TB),
                                   in_=ps[:], func=AF.Exp,
                                   reads=[psB], writes=[ptB])
                    return f

                def mk_av(qb, kc, nkc):
                    def f():
                        pslot = pt_off if kc % 2 == 0 else pt_off + TB
                        b.emit("tensor", "matmul", self.ps_c[0:65, :],
                               self.ab.view(va_off, 16 * 130)[:, kc * 130 + hi * 65: kc * 130 + hi * 65 + 65],
                               self.ab.view(pslot, TB),
                               start=(kc == 0), stop=(kc == nkc - 1),
                               reads=[vaB, ptB], writes=[self.ps_cB])
                    return f

                def mk_norm(qb):
                    def f():
                        b.emit("vector", "reciprocal",
                               out=self.srow1[0:1, 0:TB], in_=self.ps_c[64:65, :],
                               reads=[self.ps_cB], writes=[self.srow1B])
                        self.bcast_row(self.ps_d, self.ps_dB,
                                       self.srow1[0:1, 0:TB], self.srow1B, TB)
                        b.emit("scalar", "activation",
                               out=self.ab.view(pt_off, TB, 0, 64),
                               in_=self.ps_c[0:64, :], func=AF.Copy,
                               reads=[self.ps_cB], writes=[ptB])
                        b.emit("vector", "tensor_mul",
                               out=self.OF[hi * 64:(hi + 1) * 64,
                                           hp * T + qb * TB: hp * T + (qb + 1) * TB],
                               in0=self.ab.view(pt_off, TB, 0, 64),
                               in1=self.ps_d[0:64, :],
                               reads=[ptB, self.ps_dB], writes=[self.OFBufs[hp]])
                    return f

                for qb in range(NTB):
                    nkc = 4 * qb + 4
                    for kc in range(nkc):
                        steps.append(mk_s(qb, kc))
                        if kc > 0:
                            steps.append(mk_av(qb, kc - 1, nkc))
                    steps.append(mk_av(qb, nkc - 1, nkc))
                    steps.append(mk_norm(qb))
                return steps

            for f in _merge([mk_vp(tt) for tt in range(16)], l1_stats_steps(0)):
                f()
            self.flush_aug(q1_off, q1B)
            for f in _merge(l1_pass2_steps(0), l1_stats_steps(1)):
                f()
            self.flush_aug(q2_off, q2B)
            for f in l1_pass2_steps(1):
                f()
        self.ab.pop()
        self.af.pop()
        self.b.barrier()
        self.ab.push()
        self.af.push()
        st_off, stB = self.af.take(1024, "wstage2")
        wo_off, woB = self.ab.take(8 * 1024, "l1_wo")
        self.load_wt_bf16(wout, D, D, wo_off, woB, st_off, stB)
        for oc in range(NC):
            for tb in range(NTB):
                ps = self.ps_a if (oc + tb) % 2 == 0 else self.ps_b
                psB = self.ps_aB if (oc + tb) % 2 == 0 else self.ps_bB
                for kc in range(NC):
                    b.emit("tensor", "matmul", ps[:],
                           self.ab.view(wo_off + kc * 1024, 1024)[:, oc * 128:(oc + 1) * 128],
                           self.OF[:, kc * T + tb * TB: kc * T + (tb + 1) * TB],
                           start=(kc == 0), stop=(kc == NC - 1),
                           reads=[woB, self.OFBufs[kc]], writes=[psB])
                b.emit("vector", "scalar_tensor_tensor",
                       out=self.X[:, oc * T + tb * TB: oc * T + (tb + 1) * TB],
                       in0=ps[:], scalar=self.cols[:, cb_o + oc:cb_o + oc + 1],
                       in1=self.X[:, oc * T + tb * TB: oc * T + (tb + 1) * TB],
                       op0=ALU.add, op1=ALU.add,
                       reads=[psB, self.colsB, self.XBufs[oc]],
                       writes=[self.XBufs[oc]])
        self.ab.pop()
        self.af.pop()

    def layernorm_of_xh(self):
        cg = self.col(NC); self.load_cols("l1_g", (NC, 128), cg, NC)
        cbb = self.col(NC); self.load_cols("l1_b", (NC, 128), cbb, NC)
        self.ln_inner(self.XH, self.XHBufs, cg, cbb)

    def ln_inner(self, dst_t, dst_bufs, cg=None, cbb=None):
        b = self.b
        self.af.push()
        self.ab.push()
        sqoff, sqB = self.ab.take(TB, "ln2_sq")
        for tb in range(NTB):
            for c in range(NC):
                b.emit("scalar", "activation",
                       out=self.ab.view(sqoff, TB),
                       in_=self.XH[:, c * T + tb * TB: c * T + (tb + 1) * TB],
                       func=AF.Square,
                       reads=[self.XHBufs[c]], writes=[sqB])
                b.emit("tensor", "matmul", self.ps_c[0:1, :],
                       self.ones_col_bf[:],
                       self.XH[:, c * T + tb * TB: c * T + (tb + 1) * TB],
                       start=(c == 0), stop=(c == NC - 1),
                       reads=[self.XHBufs[c], self.constB], writes=[self.ps_cB])
                b.emit("tensor", "matmul", self.ps_d[0:1, :],
                       self.ones_col_bf[:], self.ab.view(sqoff, TB),
                       start=(c == 0), stop=(c == NC - 1),
                       reads=[sqB, self.constB], writes=[self.ps_dB])
            mr = self.m_row[0:1, :]
            rr = self.r_row[0:1, :]
            s1 = self.srow1[0:1, :]
            s2 = self.srow2[0:1, :]
            b.emit("vector", "tensor_scalar", out=mr, in0=self.ps_c[0:1, :],
                   scalar1=1.0 / D, scalar2=None, op0=ALU.mult,
                   reads=[self.ps_cB], writes=[self.m_rowB])
            b.emit("vector", "tensor_scalar", out=s1, in0=self.ps_d[0:1, :],
                   scalar1=1.0 / D, scalar2=None, op0=ALU.mult,
                   reads=[self.ps_dB], writes=[self.srow1B])
            b.emit("vector", "tensor_mul", out=s2, in0=mr, in1=mr,
                   reads=[self.m_rowB], writes=[self.srow2B])
            b.emit("vector", "tensor_sub", out=s1, in0=s1, in1=s2,
                   reads=[self.srow1B, self.srow2B], writes=[self.srow1B])
            b.emit("scalar", "activation", out=s1, in_=s1, func=AF.Sqrt,
                   bias=self.cols[0:1, self.c_eps:self.c_eps + 1],
                   reads=[self.srow1B, self.colsB], writes=[self.srow1B])
            b.emit("vector", "reciprocal", out=rr, in_=s1,
                   reads=[self.srow1B], writes=[self.r_rowB])
            self.bcast_row(self.ps_c, self.ps_cB, mr, self.m_rowB, TB)
            self.bcast_row(self.ps_d, self.ps_dB, rr, self.r_rowB, TB)
            for c in range(NC):
                sl = slice(c * T + tb * TB, c * T + (tb + 1) * TB)
                b.emit("vector", "tensor_sub",
                       out=self.ab.view(sqoff, TB),
                       in0=self.XH[:, sl], in1=self.ps_c[:],
                       reads=[self.XHBufs[c], self.ps_cB], writes=[sqB])
                b.emit("vector", "tensor_mul",
                       out=dst_t[:, sl],
                       in0=self.ab.view(sqoff, TB), in1=self.ps_d[:],
                       reads=[sqB, self.ps_dB], writes=[dst_bufs[c]])
                if cg is not None:
                    if cbb is not None:
                        b.emit("vector", "tensor_scalar",
                               out=dst_t[:, sl], in0=dst_t[:, sl],
                               scalar1=self.cols[:, cg + c:cg + c + 1],
                               scalar2=self.cols[:, cbb + c:cbb + c + 1],
                               op0=ALU.mult, op1=ALU.add,
                               reads=[dst_bufs[c], self.colsB], writes=[dst_bufs[c]])
                    else:
                        b.emit("vector", "tensor_scalar",
                               out=dst_t[:, sl], in0=dst_t[:, sl],
                               scalar1=self.cols[:, cg + c:cg + c + 1],
                               scalar2=None, op0=ALU.mult,
                               reads=[dst_bufs[c], self.colsB], writes=[dst_bufs[c]])
        self.af.pop()
        self.ab.pop()
        self.b.barrier()

    def l1_ffn(self):
        b = self.b
        self.layernorm()
        wr_t = self.din("l1f_wr_t", (D, D))
        wk_t = self.din("l1f_wk_t", (D, D))
        wv_t = self.din("l1f_wv_t", (D, D))
        cmu_r = self.col(NC); self.load_cols("l1f_mu_r", (NC, 128), cmu_r, NC)
        cmu_k = self.col(NC); self.load_cols("l1f_mu_k", (NC, 128), cmu_k, NC)
        c1m_r = self.col(NC)
        c1m_k = self.col(NC)
        for src, dst in ((cmu_r, c1m_r), (cmu_k, c1m_k)):
            b.emit("vector", "tensor_scalar",
                   out=self.cols[:, dst:dst + NC], in0=self.cols[:, src:src + NC],
                   scalar1=-1.0, scalar2=1.0, op0=ALU.mult, op1=ALU.add,
                   reads=[self.colsB], writes=[self.colsB])
        self.ab.push()
        self.af.push()
        xr_off, xrB = self.ab.take(NC * T, "l1f_xr")
        wsl_off, wslB = self.ab.take(2048, "l1f_wsl")
        st_off, stB = self.af.take(1024, "wstage")
        tm_off, tmB = self.af.take(T, "shift_tmp")
        OFall = self.b.buf("l1f_OFall")

        def token_shift(dst_off, dstB, cmu, c1m):
            for c in range(NC):
                b.emit("vector", "tensor_scalar",
                       out=self.af.view(tm_off, T),
                       in0=self.XH[:, c * T:(c + 1) * T],
                       scalar1=self.cols[:, cmu + c:cmu + c + 1], scalar2=None,
                       op0=ALU.mult,
                       reads=[self.XHBufs[c], self.colsB], writes=[tmB])
                b.emit("vector", "scalar_tensor_tensor",
                       out=self.ab.view(dst_off + c * T + 1, T - 1),
                       in0=self.XH[:, c * T: c * T + T - 1],
                       scalar=self.cols[:, c1m + c:c1m + c + 1],
                       in1=self.af.view(tm_off + 1, T - 1),
                       op0=ALU.mult, op1=ALU.add,
                       reads=[self.XHBufs[c], self.colsB, tmB], writes=[dstB])
                b.emit("vector", "tensor_copy",
                       out=self.ab.view(dst_off + c * T, 1),
                       in_=self.af.view(tm_off, 1),
                       reads=[tmB], writes=[dstB])

        def proj_from(in_off, inB, wdr, dst_t, dst_wr, dst_rd, act):
            for oc in range(NC):
                dbuf = oc % 2
                for kc in range(NC):
                    b.emit("sync", "dma_start",
                           out=self.af.view(st_off, 1024)[:, dbuf * 128:(dbuf + 1) * 128],
                           in_=wdr[kc * 128:(kc + 1) * 128, oc * 128:(oc + 1) * 128],
                           reads=[stB], writes=[stB])
                    b.emit("gpsimd", "tensor_copy",
                           out=self.ab.view(wsl_off + dbuf * 1024 + kc * 128, 128),
                           in_=self.af.view(st_off, 1024)[:, dbuf * 128:(dbuf + 1) * 128],
                           reads=[stB], writes=[wslB])
                for tb in range(NTB):
                    ps = self.ps_a if tb % 2 == 0 else self.ps_b
                    psB = self.ps_aB if tb % 2 == 0 else self.ps_bB
                    for kc in range(NC):
                        b.emit("tensor", "matmul", ps[:],
                               self.ab.view(wsl_off + dbuf * 1024 + kc * 128, 128),
                               self.ab.view(in_off + kc * T + tb * TB, TB),
                               start=(kc == 0), stop=(kc == NC - 1),
                               reads=[wslB, inB], writes=[psB])
                    b.emit("scalar", "activation",
                           out=dst_t[:, oc * T + tb * TB: oc * T + (tb + 1) * TB],
                           in_=ps[:], func=(AF.Sigmoid if act == "sig" else AF.Relu),
                           reads=[psB] + dst_rd, writes=[dst_wr[oc]])

        token_shift(xr_off, xrB, cmu_r, c1m_r)
        proj_from(xr_off, xrB, wr_t, self.OF, [OFall] * NC, [], "sig")
        token_shift(xr_off, xrB, cmu_k, c1m_k)
        proj_from(xr_off, xrB, wk_t, self.XH, self.XHBufs, [], "relu")
        for c in range(NC):
            sl = slice(c * T, (c + 1) * T)
            b.emit("scalar", "activation",
                   out=self.XH[:, sl], in_=self.XH[:, sl], func=AF.Square,
                   reads=[self.XHBufs[c]], writes=[self.XHBufs[c]])
        for oc in range(NC):
            dbuf = oc % 2
            for kc in range(NC):
                b.emit("gpsimd", "dma_start",
                       out=self.ab.view(wsl_off + dbuf * 1024 + kc * 128, 128),
                       in_=wv_t[kc * 128:(kc + 1) * 128, oc * 128:(oc + 1) * 128],
                       reads=[wslB], writes=[wslB])
            for tb in range(NTB):
                ps = self.ps_a if tb % 2 == 0 else self.ps_b
                psB = self.ps_aB if tb % 2 == 0 else self.ps_bB
                for kc in range(NC):
                    b.emit("tensor", "matmul", ps[:],
                           self.ab.view(wsl_off + dbuf * 1024 + kc * 128, 128),
                           self.XH[:, kc * T + tb * TB: kc * T + (tb + 1) * TB],
                           start=(kc == 0), stop=(kc == NC - 1),
                           reads=[wslB, self.XHBufs[kc]], writes=[psB])
                b.emit("vector", "tensor_mul",
                       out=self.af.view(tm_off, TB),
                       in0=ps[:],
                       in1=self.OF[:, oc * T + tb * TB: oc * T + (tb + 1) * TB],
                       reads=[psB, OFall], writes=[tmB])
                b.emit("vector", "tensor_add",
                       out=self.X[:, oc * T + tb * TB: oc * T + (tb + 1) * TB],
                       in0=self.X[:, oc * T + tb * TB: oc * T + (tb + 1) * TB],
                       in1=self.af.view(tm_off, TB),
                       reads=[self.XBufs[oc], tmB], writes=[self.XBufs[oc]])
        self.ab.pop()
        self.af.pop()

    def _rope64(self, dst_off, dstB, cs_off, sn_off, tabB, sc_off, scB, scale):
        b = self.b
        b.emit("vector", "tensor_mul",
               out=self.ab.view(sc_off, T, 0, 64),
               in0=self.ab.view(dst_off, T, 0, 64),
               in1=self.af.view(cs_off, T, 0, 64),
               reads=[dstB, tabB], writes=[scB])
        b.emit("vector", "tensor_mul",
               out=self.ab.view(sc_off + T, T, 0, 32),
               in0=self.ab.view(dst_off, T, 32, 64),
               in1=self.af.view(sn_off, T, 32, 64),
               reads=[dstB, tabB], writes=[scB])
        b.emit("vector", "tensor_mul",
               out=self.ab.view(sc_off + T, T, 32, 64),
               in0=self.ab.view(dst_off, T, 0, 32),
               in1=self.af.view(sn_off, T, 0, 32),
               reads=[dstB, tabB], writes=[scB])
        b.emit("vector", "tensor_sub",
               out=self.ab.view(dst_off, T, 0, 32),
               in0=self.ab.view(sc_off, T, 0, 32),
               in1=self.ab.view(sc_off + T, T, 0, 32),
               reads=[scB], writes=[dstB])
        b.emit("vector", "tensor_add",
               out=self.ab.view(dst_off, T, 32, 64),
               in0=self.ab.view(sc_off, T, 32, 64),
               in1=self.ab.view(sc_off + T, T, 32, 64),
               reads=[scB], writes=[dstB])
        if scale is not None:
            b.emit("vector", "tensor_scalar",
                   out=self.ab.view(dst_off, T, 0, 64),
                   in0=self.ab.view(dst_off, T, 0, 64),
                   scalar1=scale, scalar2=None, op0=ALU.mult,
                   reads=[dstB], writes=[dstB])

    def _attn_causal_core(self, hp, q1_off, q1B, q2_off, q2B, k1_off, k1B,
                          k2_off, k2B, va_off, vaB, pt_off, ptB,
                          mtri_off, mtrit_off, mtB, c_m, of_base=None,
                          pre_steps=None):
        b = self.b
        heads = ((q1_off, q1B, k1_off, k1B), (q2_off, q2B, k2_off, k2B))

        def stats_steps(hi):
            qoffh, qBh, koffh, kBh = heads[hi]
            steps = []
            for qt in range(16):
                def mk(qt=qt, qoffh=qoffh, qBh=qBh, koffh=koffh, kBh=kBh):
                    w_allow = 128 * qt + 128
                    nkb = (w_allow + TB - 1) // TB
                    for kb in range(nkb):
                        w = min(TB, w_allow - kb * TB)
                        b.emit("tensor", "matmul",
                               self.ps_big[:, kb * TB: kb * TB + w],
                               self.ab.view(qoffh + qt * 128, 128, 0, 64),
                               self.ab.view(koffh + kb * TB, w, 0, 64),
                               start=True, stop=True,
                               reads=[qBh, kBh], writes=[self.ps_bigQ[kb]])
                    b.emit("vector", "tensor_add",
                           out=self.ps_big[:, w_allow - 128:w_allow],
                           in0=self.ps_big[:, w_allow - 128:w_allow],
                           in1=self.af.view(mtrit_off, 128),
                           reads=[self.ps_bigQ[nkb - 1], mtB],
                           writes=[self.ps_bigQ[nkb - 1]])
                    b.emit("vector", "reduce_max",
                           out=self.wsmall[:, qt:qt + 1],
                           in_=self.ps_big[:, 0:w_allow],
                           axis=AX.X, negate=True,
                           reads=self.ps_bigQ[0:nkb], writes=[self.wsmallB])
                steps.append(mk)
            return steps

        def pass2_steps(hi):
            qoffh, qBh, koffh, kBh = heads[hi]
            steps = []

            def mk_s(qb, kc):
                def f():
                    diag = kc - 4 * qb
                    ps = self.ps_a if kc % 2 == 0 else self.ps_b
                    psB = self.ps_aB if kc % 2 == 0 else self.ps_bB
                    pslot = pt_off if kc % 2 == 0 else pt_off + TB
                    if diag < 0:
                        b.emit("tensor", "matmul", ps[:],
                               self.ab.view(koffh + kc * 128, 128, 0, 65),
                               self.ab.view(qoffh + qb * TB, TB, 0, 65),
                               start=True, stop=True,
                               reads=[kBh, qBh], writes=[psB])
                        b.emit("scalar", "activation",
                               out=self.ab.view(pslot, TB),
                               in_=ps[:], func=AF.Exp,
                               reads=[psB], writes=[ptB])
                    else:
                        u0 = 128 * diag
                        if u0 > 0:
                            b.emit("gpsimd", "memset",
                                   self.ab.view(pslot, u0), 0.0, writes=[ptB])
                        b.emit("tensor", "matmul", ps[:, u0:TB],
                               self.ab.view(koffh + kc * 128, 128, 0, 65),
                               self.ab.view(qoffh + qb * TB + u0, TB - u0, 0, 65),
                               start=True, stop=True,
                               reads=[kBh, qBh], writes=[psB])
                        b.emit("vector", "tensor_add",
                               out=ps[:, u0:u0 + 128],
                               in0=ps[:, u0:u0 + 128],
                               in1=self.af.view(mtri_off, 128),
                               reads=[psB, mtB], writes=[psB])
                        b.emit("scalar", "activation",
                               out=self.ab.view(pslot + u0, TB - u0),
                               in_=ps[:, u0:TB], func=AF.Exp,
                               reads=[psB], writes=[ptB])
                return f

            def mk_av(qb, kc, nkc):
                def f():
                    pslot = pt_off if kc % 2 == 0 else pt_off + TB
                    b.emit("tensor", "matmul", self.ps_c[0:65, :],
                           self.ab.view(va_off, 16 * 130)[:, kc * 130 + hi * 65: kc * 130 + hi * 65 + 65],
                           self.ab.view(pslot, TB),
                           start=(kc == 0), stop=(kc == nkc - 1),
                           reads=[vaB, ptB], writes=[self.ps_cB])
                return f

            def mk_norm(qb):
                def f():
                    b.emit("vector", "reciprocal",
                           out=self.srow1[0:1, 0:TB], in_=self.ps_c[64:65, :],
                           reads=[self.ps_cB], writes=[self.srow1B])
                    self.bcast_row(self.ps_d, self.ps_dB,
                                   self.srow1[0:1, 0:TB], self.srow1B, TB)
                    b.emit("scalar", "activation",
                           out=self.ab.view(pt_off, TB, 0, 64),
                           in_=self.ps_c[0:64, :], func=AF.Copy,
                           reads=[self.ps_cB], writes=[ptB])
                    b.emit("vector", "tensor_mul",
                           out=self.OF[hi * 64:(hi + 1) * 64,
                                       hp * T + qb * TB: hp * T + (qb + 1) * TB],
                           in0=self.ab.view(pt_off, TB, 0, 64),
                           in1=self.ps_d[0:64, :],
                           reads=[ptB, self.ps_dB], writes=[self.OFBufs[hp]])
                return f

            for qb in range(NTB):
                nkc = 4 * qb + 4
                for kc in range(nkc):
                    steps.append(mk_s(qb, kc))
                    if kc > 0:
                        steps.append(mk_av(qb, kc - 1, nkc))
                steps.append(mk_av(qb, nkc - 1, nkc))
                steps.append(mk_norm(qb))
            return steps

        def merge(primary, filler):
            out = []
            si = 0
            n = max(1, len(primary))
            for i, f in enumerate(primary):
                out.append(f)
                want = (i + 1) * len(filler) // n
                while si < want:
                    out.append(filler[si])
                    si += 1
            out.extend(filler[si:])
            return out

        s0 = stats_steps(0)
        if pre_steps:
            for f in merge(list(pre_steps), s0):
                f()
        else:
            for f in s0:
                f()
        self.flush_aug(q1_off, q1B)
        for f in merge(pass2_steps(0), stats_steps(1)):
            f()
        self.flush_aug(q2_off, q2B)
        for f in pass2_steps(1):
            f()

    def l2_attn(self):
        b = self.b
        self.layernorm()
        wq = self.din("l2_wq", (D, D))
        wkv = self.din("l2_wkv", (D, 2 * D))
        wo = self.din("l2_wo", (D, D))
        cb_q = self.col(NC); self.load_cols("l2_bq", (NC, 128), cb_q, NC)
        cb_k = self.col(NC); self.load_cols("l2_bk", (NC, 128), cb_k, NC)
        cb_o = self.col(NC); self.load_cols("l2_bo", (NC, 128), cb_o, NC)
        bv_dram = self.din("l2_bv", (D,))
        cos_dr = self.din("l2_cos", (64, T))
        sin_dr = self.din("l2_sin", (64, T))
        mtri_dr = self.din("l2_mtri", (128, 128))
        mtrit_dr = self.din("l2_mtri_t", (128, 128))
        b.emit("vector", "tensor_scalar",
               out=self.cols[:, cb_q:cb_q + NC], in0=self.cols[:, cb_q:cb_q + NC],
               scalar1=0.125, scalar2=None, op0=ALU.mult,
               reads=[self.colsB], writes=[self.colsB])

        self.ab.push()
        self.af.push()
        wsl_off, wslB = self.ab.take(3 * 1024, "l2_wsl")
        q1_off, q1B = self.ab.take(T, "Q1")
        q2_off, q2B = self.ab.take(T, "Q2")
        k1_off, k1B = self.ab.take(T, "K1")
        k2_off, k2B = self.ab.take(T, "K2")
        va_off, vaB = self.ab.take(16 * 130, "VA")
        pt_off, ptB = self.ab.take(2 * TB, "PT")
        rs_off, rsB = self.ab.take(2 * T, "rope_scr")
        st_off, stB = self.af.take(1024, "wstage")
        cs_off, tabB = self.af.take(T, "cos")
        sn_off, _tab2 = self.af.take(T, "sin")
        mtri_off, mtB = self.af.take(128, "mtri")
        mtrit_off, _mt2 = self.af.take(128, "mtrit")
        vb_off, vbB = self.af.take(128, "vbias")
        c_m = self.col(2)
        b.emit("sync", "dma_start", out=self.af.view(cs_off, T)[0:64, :],
               in_=cos_dr, reads=[tabB], writes=[tabB])
        b.emit("sync", "dma_start", out=self.af.view(sn_off, T)[0:64, :],
               in_=sin_dr, reads=[tabB], writes=[tabB])
        b.emit("sync", "dma_start", out=self.af.view(mtri_off, 128),
               in_=mtri_dr, reads=[mtB], writes=[mtB])
        b.emit("sync", "dma_start", out=self.af.view(mtrit_off, 128),
               in_=mtrit_dr, reads=[mtB], writes=[mtB])

        b.emit("gpsimd", "memset", self.ab.view(k1_off, T, 64, 65), 1.0, writes=[k1B])
        b.emit("gpsimd", "memset", self.ab.view(k2_off, T, 64, 65), 1.0, writes=[k2B])
        for tt in range(16):
            b.emit("gpsimd", "memset",
                   self.ab.view(va_off + tt * 130 + 64, 1), 1.0, writes=[vaB])
            b.emit("gpsimd", "memset",
                   self.ab.view(va_off + tt * 130 + 129, 1), 1.0, writes=[vaB])

        for hp in range(8):
            for wi, (wdr, colbase) in enumerate(
                    ((wq, hp * 128), (wkv, hp * 128), (wkv, D + hp * 128))):
                for kc in range(NC):
                    b.emit("gpsimd", "dma_start",
                           out=self.ab.view(wsl_off + wi * 1024 + kc * 128, 128),
                           in_=wdr[kc * 128:(kc + 1) * 128, colbase:colbase + 128],
                           reads=[wslB], writes=[wslB])
            b.emit("sync", "dma_start",
                   out=self.af.view(vb_off, 128),
                   in_=AP(bv_dram.tensor, hp * 128, [[0, 128], [1, 128]]),
                   reads=[vbB], writes=[vbB])
            for (dst_off, dstB, head_half, wi, coff) in (
                    (q1_off, q1B, 0, 0, cb_q), (q2_off, q2B, 1, 0, cb_q),
                    (k1_off, k1B, 0, 1, cb_k), (k2_off, k2B, 1, 1, cb_k)):
                for tb in range(NTB):
                    ps = self.ps_a if tb % 2 == 0 else self.ps_b
                    psB = self.ps_aB if tb % 2 == 0 else self.ps_bB
                    for kc in range(NC):
                        b.emit("tensor", "matmul", ps[:],
                               self.ab.view(wsl_off + wi * 1024 + kc * 128, 128),
                               self.XH[:, kc * T + tb * TB: kc * T + (tb + 1) * TB],
                               start=(kc == 0), stop=(kc == NC - 1),
                               reads=[wslB, self.XHBufs[kc]], writes=[psB])
                    scl = 0.125 if coff == cb_q else None
                    if scl is not None:
                        b.emit("vector", "tensor_scalar",
                               out=self.ab.view(dst_off + tb * TB, TB, 0, 64),
                               in0=ps[head_half * 64:(head_half + 1) * 64, :],
                               scalar1=scl,
                               scalar2=self.cols[head_half * 64:(head_half + 1) * 64,
                                                 coff + hp:coff + hp + 1],
                               op0=ALU.mult, op1=ALU.add,
                               reads=[psB, self.colsB], writes=[dstB])
                    else:
                        b.emit("vector", "tensor_scalar",
                               out=self.ab.view(dst_off + tb * TB, TB, 0, 64),
                               in0=ps[head_half * 64:(head_half + 1) * 64, :],
                               scalar1=self.cols[head_half * 64:(head_half + 1) * 64,
                                                 coff + hp:coff + hp + 1],
                               scalar2=None, op0=ALU.add,
                               reads=[psB, self.colsB], writes=[dstB])
            for doff, dB in ((q1_off, q1B), (q2_off, q2B), (k1_off, k1B), (k2_off, k2B)):
                self._rope64(doff, dB, cs_off, sn_off, tabB, rs_off, rsB, None)
            def mk_vp(tt):
                def f():
                    ps = self.ps_a if tt % 2 == 0 else self.ps_b
                    psB = self.ps_aB if tt % 2 == 0 else self.ps_bB
                    for kc in range(NC):
                        b.emit("tensor", "matmul", ps[:, 0:128],
                               self.XH[:, kc * T + tt * 128: kc * T + (tt + 1) * 128],
                               self.ab.view(wsl_off + 2 * 1024 + kc * 128, 128),
                               start=(kc == 0), stop=(kc == NC - 1),
                               reads=[wslB, self.XHBufs[kc]], writes=[psB])
                    b.emit("vector", "tensor_add",
                           out=self.ab.view(va_off, 16 * 130)[:, tt * 130: tt * 130 + 64],
                           in0=ps[:, 0:64], in1=self.af.view(vb_off, 128)[:, 0:64],
                           reads=[psB, vbB], writes=[vaB])
                    b.emit("vector", "tensor_add",
                           out=self.ab.view(va_off, 16 * 130)[:, tt * 130 + 65: tt * 130 + 129],
                           in0=ps[:, 64:128], in1=self.af.view(vb_off, 128)[:, 64:128],
                           reads=[psB, vbB], writes=[vaB])
                return f
            self._attn_causal_core(hp, q1_off, q1B, q2_off, q2B, k1_off, k1B,
                                   k2_off, k2B, va_off, vaB, pt_off, ptB,
                                   mtri_off, mtrit_off, mtB, c_m,
                                   pre_steps=[mk_vp(tt) for tt in range(16)])
        self.ab.pop()
        self.af.pop()
        self.b.barrier()
        self.ab.push()
        self.af.push()
        st_off, stB = self.af.take(1024, "wstage2")
        wo_off, woB = self.ab.take(8 * 1024, "l2_wo")
        self.load_wt_bf16(wo, D, D, wo_off, woB, st_off, stB)
        for oc in range(NC):
            for tb in range(NTB):
                ps = self.ps_a if (oc + tb) % 2 == 0 else self.ps_b
                psB = self.ps_aB if (oc + tb) % 2 == 0 else self.ps_bB
                for kc in range(NC):
                    b.emit("tensor", "matmul", ps[:],
                           self.ab.view(wo_off + kc * 1024, 1024)[:, oc * 128:(oc + 1) * 128],
                           self.OF[:, kc * T + tb * TB: kc * T + (tb + 1) * TB],
                           start=(kc == 0), stop=(kc == NC - 1),
                           reads=[woB, self.OFBufs[kc]], writes=[psB])
                b.emit("vector", "scalar_tensor_tensor",
                       out=self.X[:, oc * T + tb * TB: oc * T + (tb + 1) * TB],
                       in0=ps[:], scalar=self.cols[:, cb_o + oc:cb_o + oc + 1],
                       in1=self.X[:, oc * T + tb * TB: oc * T + (tb + 1) * TB],
                       op0=ALU.add, op1=ALU.add,
                       reads=[psB, self.colsB, self.XBufs[oc]],
                       writes=[self.XBufs[oc]])
        self.ab.pop()
        self.af.pop()

    def _rope32(self, dst_off, dstB, rows, cs2_off, sn2_off, tabB,
                p32_off, p32B, sc_off, scB):
        b = self.b
        for tb in range(NTB):
            b.emit("vector", "tensor_mul",
                   out=self.ab.view(sc_off, TB, 0, rows),
                   in0=self.ab.view(dst_off + tb * TB, TB, 0, rows),
                   in1=self.af.view(sn2_off + tb * TB, TB, 0, rows),
                   reads=[dstB, tabB], writes=[scB])
            b.emit("vector", "tensor_mul",
                   out=self.ab.view(sc_off + TB, TB, 0, rows),
                   in0=self.ab.view(dst_off + tb * TB, TB, 0, rows),
                   in1=self.af.view(cs2_off + tb * TB, TB, 0, rows),
                   reads=[dstB, tabB], writes=[scB])
            ps = self.ps_a if tb % 2 == 0 else self.ps_b
            psB = self.ps_aB if tb % 2 == 0 else self.ps_bB
            b.emit("tensor", "matmul", ps[0:rows, :],
                   self.ab.view(p32_off, 128, 0, rows)[:, 0:rows],
                   self.ab.view(sc_off, TB, 0, rows),
                   start=True, stop=True,
                   reads=[p32B, scB], writes=[psB])
            b.emit("vector", "scalar_tensor_tensor",
                   out=self.ab.view(dst_off + tb * TB, TB, 0, rows),
                   in0=ps[0:rows, :], scalar=1.0,
                   in1=self.ab.view(sc_off + TB, TB, 0, rows),
                   op0=ALU.bypass, op1=ALU.add,
                   reads=[psB, scB], writes=[dstB])

    def l2_ffn(self):
        b = self.b
        nc = self.nc
        self.layernorm()
        for c in range(NC):
            b.emit("vector", "tensor_add",
                   out=self.X[:, c * T:(c + 1) * T],
                   in0=self.X[:, c * T:(c + 1) * T],
                   in1=self.XH[:, c * T:(c + 1) * T],
                   reads=[self.XBufs[c], self.XHBufs[c]], writes=[self.XBufs[c]])
        cln1 = self.col(NC); self.load_cols("l2f_ln1", (NC, 128), cln1, NC)
        cln2 = self.col(NC); self.load_cols("l2f_ln2", (NC, 128), cln2, NC)
        OFall = self.b.buf("l2f_xn1")
        self.ln_inner(self.OF, [OFall] * NC, cln1, None)

        kv_d_t = self.din("l2f_kv_d_t", (D, 256))
        q_d_t = self.din("l2f_q_d_t", (D, 256))
        k_u_t = self.din("l2f_k_u_t", (256, D))
        q_u_t = self.din("l2f_q_u_t", (256, D))
        v_u_t = self.din("l2f_v_u_t", (256, D))
        rope_q_t = self.din("l2f_rope_q_t", (256, 512))
        rope_k_t = self.din("l2f_rope_k_t", (D, 512))
        o_w_t = self.din("l2f_o_w_t", (D, D))
        mtri_dr = self.din("l2_mtri", (128, 128))
        mtrit_dr = self.din("l2_mtri_t", (128, 128))
        cos2_dr = self.din("l2f_cos2", (128, T))
        sin2_dr = self.din("l2f_sin2", (128, T))
        p32_dr = self.din("l2f_p32t", (128, 128))
        krot_dr = nc.dram_tensor("l2f_krot_scr", [512, T], BF16, kind="Internal").ap()
        krB = self.b.buf("l2f_krotdr")

        self.ab.push()
        self.af.push()
        kvl_off, kvlB = self.ab.take(2 * T, "kvl")
        ql_off, qlB = self.ab.take(2 * T, "ql")
        p32_off, p32B = self.ab.take(128, "p32")
        cs2_off, tab2B = self.af.take(T, "cos2")
        sn2_off, _tb2 = self.af.take(T, "sin2")
        mtri_off, mtB = self.af.take(128, "mtri")
        mtrit_off, _m2 = self.af.take(128, "mtrit")
        b.emit("gpsimd", "dma_start", out=self.ab.view(p32_off, 128),
               in_=p32_dr, reads=[p32B], writes=[p32B])
        b.emit("sync", "dma_start", out=self.af.view(cs2_off, T),
               in_=cos2_dr, reads=[tab2B], writes=[tab2B])
        b.emit("sync", "dma_start", out=self.af.view(sn2_off, T),
               in_=sin2_dr, reads=[tab2B], writes=[tab2B])
        b.emit("sync", "dma_start", out=self.af.view(mtri_off, 128),
               in_=mtri_dr, reads=[mtB], writes=[mtB])
        b.emit("sync", "dma_start", out=self.af.view(mtrit_off, 128),
               in_=mtrit_dr, reads=[mtB], writes=[mtB])

        self.ab.push()
        wsl_off, wslB = self.ab.take(1024, "wsl")
        for wdr, doff, dB in ((kv_d_t, kvl_off, kvlB), (q_d_t, ql_off, qlB)):
            for oc in range(2):
                for kc in range(NC):
                    b.emit("gpsimd", "dma_start",
                           out=self.ab.view(wsl_off + kc * 128, 128),
                           in_=wdr[kc * 128:(kc + 1) * 128, oc * 128:(oc + 1) * 128],
                           reads=[wslB], writes=[wslB])
                for tb in range(NTB):
                    ps = self.ps_a if tb % 2 == 0 else self.ps_b
                    psB = self.ps_aB if tb % 2 == 0 else self.ps_bB
                    for kc in range(NC):
                        b.emit("tensor", "matmul", ps[:],
                               self.ab.view(wsl_off + kc * 128, 128),
                               self.OF[:, kc * T + tb * TB: kc * T + (tb + 1) * TB],
                               start=(kc == 0), stop=(kc == NC - 1),
                               reads=[wslB, OFall], writes=[psB])
                    b.emit("scalar", "copy",
                           out=self.ab.view(doff + oc * T + tb * TB, TB),
                           in_=ps[:], reads=[psB], writes=[dB])
        kr_off, krsB = self.ab.take(T, "krot")
        rsc_off, rscB = self.ab.take(2 * TB, "rope_scr")
        for oc4 in range(4):
            for kc in range(NC):
                b.emit("gpsimd", "dma_start",
                       out=self.ab.view(wsl_off + kc * 128, 128),
                       in_=rope_k_t[kc * 128:(kc + 1) * 128, oc4 * 128:(oc4 + 1) * 128],
                       reads=[wslB], writes=[wslB])
            for tb in range(NTB):
                ps = self.ps_a if tb % 2 == 0 else self.ps_b
                psB = self.ps_aB if tb % 2 == 0 else self.ps_bB
                for kc in range(NC):
                    b.emit("tensor", "matmul", ps[:],
                           self.ab.view(wsl_off + kc * 128, 128),
                           self.OF[:, kc * T + tb * TB: kc * T + (tb + 1) * TB],
                           start=(kc == 0), stop=(kc == NC - 1),
                           reads=[wslB, OFall], writes=[psB])
                b.emit("scalar", "copy",
                       out=self.ab.view(kr_off + tb * TB, TB),
                       in_=ps[:], reads=[psB], writes=[krsB])
            self._rope32(kr_off, krsB, 128, cs2_off, sn2_off, tab2B,
                         p32_off, p32B, rsc_off, rscB)
            b.emit("gpsimd", "dma_start",
                   out=krot_dr[oc4 * 128:(oc4 + 1) * 128, :],
                   in_=self.ab.view(kr_off, T), reads=[krsB], writes=[krB])
        self.ab.pop()
        self.b.barrier()
        wsl_off, wslB = self.ab.take(512, "wslB")
        rsc_off, rscB = self.ab.take(2 * TB, "rope_scrB")
        q1_off, q1B = self.ab.take(T, "Q1")
        q2_off, q2B = self.ab.take(T, "Q2")
        k1_off, k1B = self.ab.take(T, "K1")
        k2_off, k2B = self.ab.take(T, "K2")
        va_off, vaB = self.ab.take(16 * 130, "VA")
        pt_off, ptB = self.ab.take(2 * TB, "PT")
        c_m = self.col(2)
        b.emit("gpsimd", "memset", self.ab.view(k1_off, T, 64, 65), 1.0, writes=[k1B])
        b.emit("gpsimd", "memset", self.ab.view(k2_off, T, 64, 65), 1.0, writes=[k2B])
        for tt in range(16):
            b.emit("gpsimd", "memset",
                   self.ab.view(va_off + tt * 130 + 64, 1), 1.0, writes=[vaB])
            b.emit("gpsimd", "memset",
                   self.ab.view(va_off + tt * 130 + 129, 1), 1.0, writes=[vaB])
        for hp in range(8):
            for kc2 in range(2):
                b.emit("gpsimd", "dma_start",
                       out=self.ab.view(wsl_off + kc2 * 128, 64),
                       in_=rope_q_t[kc2 * 128:(kc2 + 1) * 128, hp * 64:(hp + 1) * 64],
                       reads=[wslB], writes=[wslB])
            for tb in range(NTB):
                ps = self.ps_a if tb % 2 == 0 else self.ps_b
                psB = self.ps_aB if tb % 2 == 0 else self.ps_bB
                for kc2 in range(2):
                    b.emit("tensor", "matmul", ps[0:64, :],
                           self.ab.view(wsl_off + kc2 * 128, 64),
                           self.ab.view(ql_off + kc2 * T + tb * TB, TB),
                           start=(kc2 == 0), stop=(kc2 == 1),
                           reads=[wslB, qlB], writes=[psB])
                b.emit("scalar", "copy",
                       out=self.ab.view(q1_off + tb * TB, TB, 0, 32),
                       in_=ps[0:32, :], reads=[psB], writes=[q1B])
                b.emit("scalar", "copy",
                       out=self.ab.view(q2_off + tb * TB, TB, 0, 32),
                       in_=ps[32:64, :], reads=[psB], writes=[q2B])
            self._rope32(q1_off, q1B, 32, cs2_off, sn2_off, tab2B,
                         p32_off, p32B, rsc_off, rscB)
            self._rope32(q2_off, q2B, 32, cs2_off, sn2_off, tab2B,
                         p32_off, p32B, rsc_off, rscB)
            b.emit("sync", "dma_start",
                   out=self.ab.view(k1_off, T, 0, 32),
                   in_=AP(krot_dr.tensor, (64 * hp) * T, [[T, 32], [1, T]]),
                   reads=[krB], writes=[k1B])
            b.emit("sync", "dma_start",
                   out=self.ab.view(k2_off, T, 0, 32),
                   in_=AP(krot_dr.tensor, (64 * hp + 32) * T, [[T, 32], [1, T]]),
                   reads=[krB], writes=[k2B])
            for wdr, d1o, d1B, d2o, d2B, scl in (
                    (q_u_t, q1_off, q1B, q2_off, q2B, None),
                    (k_u_t, k1_off, k1B, k2_off, k2B, None)):
                for kc2 in range(2):
                    b.emit("gpsimd", "dma_start",
                           out=self.ab.view(wsl_off + kc2 * 128, 128),
                           in_=wdr[kc2 * 128:(kc2 + 1) * 128, hp * 128:(hp + 1) * 128],
                           reads=[wslB], writes=[wslB])
                for tb in range(NTB):
                    ps = self.ps_a if tb % 2 == 0 else self.ps_b
                    psB = self.ps_aB if tb % 2 == 0 else self.ps_bB
                    for kc2 in range(2):
                        b.emit("tensor", "matmul", ps[:],
                               self.ab.view(wsl_off + kc2 * 128, 128),
                               self.ab.view(kvl_off + kc2 * T + tb * TB, TB) if wdr is k_u_t
                               else self.ab.view(ql_off + kc2 * T + tb * TB, TB),
                               start=(kc2 == 0), stop=(kc2 == 1),
                               reads=[wslB, kvlB if wdr is k_u_t else qlB],
                               writes=[psB])
                    for half, (do_, dB_) in enumerate(((d1o, d1B), (d2o, d2B))):
                        if scl is not None:
                            b.emit("vector", "tensor_scalar",
                                   out=self.ab.view(do_ + tb * TB, TB, 32, 64),
                                   in0=ps[half * 64 + 32:(half + 1) * 64, :],
                                   scalar1=scl, scalar2=None, op0=ALU.mult,
                                   reads=[psB], writes=[dB_])
                        else:
                            b.emit("scalar", "copy",
                                   out=self.ab.view(do_ + tb * TB, TB, 32, 64),
                                   in_=ps[half * 64 + 32:(half + 1) * 64, :],
                                   reads=[psB], writes=[dB_])
            for qo, qB_ in ((q1_off, q1B), (q2_off, q2B)):
                b.emit("vector", "tensor_scalar",
                       out=self.ab.view(qo, T, 0, 64),
                       in0=self.ab.view(qo, T, 0, 64),
                       scalar1=0.125, scalar2=None, op0=ALU.mult,
                       reads=[qB_], writes=[qB_])
            for kc2 in range(2):
                b.emit("gpsimd", "dma_start",
                       out=self.ab.view(wsl_off + kc2 * 128, 128),
                       in_=v_u_t[kc2 * 128:(kc2 + 1) * 128, hp * 128:(hp + 1) * 128],
                       reads=[wslB], writes=[wslB])

            def mk_vp(tt):
                def f():
                    ps = self.ps_a if tt % 2 == 0 else self.ps_b
                    psB = self.ps_aB if tt % 2 == 0 else self.ps_bB
                    for kc2 in range(2):
                        b.emit("tensor", "matmul", ps[:, 0:128],
                               self.ab.view(kvl_off + kc2 * T + tt * 128, 128),
                               self.ab.view(wsl_off + kc2 * 128, 128),
                               start=(kc2 == 0), stop=(kc2 == 1),
                               reads=[wslB, kvlB], writes=[psB])
                    b.emit("scalar", "copy",
                           out=self.ab.view(va_off, 16 * 130)[:, tt * 130: tt * 130 + 64],
                           in_=ps[:, 0:64], reads=[psB], writes=[vaB])
                    b.emit("scalar", "copy",
                           out=self.ab.view(va_off, 16 * 130)[:, tt * 130 + 65: tt * 130 + 129],
                           in_=ps[:, 64:128], reads=[psB], writes=[vaB])
                return f
            self._attn_causal_core(hp, q1_off, q1B, q2_off, q2B, k1_off, k1B,
                                   k2_off, k2B, va_off, vaB, pt_off, ptB,
                                   mtri_off, mtrit_off, mtB, c_m,
                                   pre_steps=[mk_vp(tt) for tt in range(16)])
        self.ab.pop()
        self.af.pop()
        self.b.barrier()
        self.ab.push()
        self.af.push()
        wo_off, woB = self.ab.take(8 * 1024, "l2f_wo")
        for kc in range(NC):
            b.emit("gpsimd", "dma_start",
                   out=self.ab.view(wo_off + kc * 1024, 1024),
                   in_=o_w_t[kc * 128:(kc + 1) * 128, :],
                   reads=[woB], writes=[woB])
        for oc in range(NC):
            for tb in range(NTB):
                ps = self.ps_a if (oc + tb) % 2 == 0 else self.ps_b
                psB = self.ps_aB if (oc + tb) % 2 == 0 else self.ps_bB
                for kc in range(NC):
                    b.emit("tensor", "matmul", ps[:],
                           self.ab.view(wo_off + kc * 1024, 1024)[:, oc * 128:(oc + 1) * 128],
                           self.OF[:, kc * T + tb * TB: kc * T + (tb + 1) * TB],
                           start=(kc == 0), stop=(kc == NC - 1),
                           reads=[woB, self.OFBufs[kc]], writes=[psB])
                sl = slice(oc * T + tb * TB, oc * T + (tb + 1) * TB)
                b.emit("vector", "tensor_add",
                       out=self.X[:, sl], in0=self.X[:, sl], in1=ps[:],
                       reads=[self.XBufs[oc], psB], writes=[self.XBufs[oc]])
                b.emit("vector", "tensor_add",
                       out=self.XH[:, sl], in0=self.XH[:, sl], in1=ps[:],
                       reads=[self.XHBufs[oc], psB], writes=[self.XHBufs[oc]])
        self.ab.pop()
        self.af.pop()
        self.b.barrier()
        if L2F_PARTS == "attn":
            return
        self.ln_inner(self.XH, self.XHBufs, cln2, None)
        router_t = self.din("l2f_router_t", (D, 7))
        rbias_dr = self.din("l2f_rbias", (128, 7))
        gate_dr = self.din("l2f_gate_t", (8, D, 2048))
        up_dr = self.din("l2f_up_t", (8, D, 2048))
        down_dr = self.din("l2f_down_t", (8, 2048, D))
        self.ab.push()
        self.af.push()
        rw_off, rwB = self.ab.take(56, "routerw")
        pr_off, prB = self.af.take(7, "probs")

        rb_off, rbB = self.af.take(7, "rbias")
        wg_off, wgB = self.af.take(128, "WGT_stage")
        wb_off, wbB = self.af.take(4 * TB, "wgt_bcast")
        wgt_dr = nc.dram_tensor("l2f_wgt_scr", [7, T], F32, kind="Internal").ap()
        wgdB = self.b.buf("l2f_wgtdr")
        c_t2 = self.col(4)
        b.emit("gpsimd", "memset", self.wsmall[:, 7:16], 0.0,
               writes=[self.wsmallB])
        for kc in range(NC):
            b.emit("gpsimd", "dma_start",
                   out=self.ab.view(rw_off + kc * 7, 7),
                   in_=router_t[kc * 128:(kc + 1) * 128, :],
                   reads=[rwB], writes=[rwB])
        b.emit("sync", "dma_start", out=self.af.view(rb_off, 7),
               in_=rbias_dr, reads=[rbB], writes=[rbB])
        for tt in range(16):
            for kc in range(NC):
                b.emit("tensor", "matmul", self.ps_a[:, 0:7],
                       self.XH[:, kc * T + tt * 128: kc * T + (tt + 1) * 128],
                       self.ab.view(rw_off + kc * 7, 7),
                       start=(kc == 0), stop=(kc == NC - 1),
                       reads=[self.XHBufs[kc], rwB], writes=[self.ps_aB])
            b.emit("vector", "tensor_add",
                   out=self.af.view(pr_off, 7), in0=self.ps_a[:, 0:7],
                   in1=self.af.view(rb_off, 7),
                   reads=[self.ps_aB, rbB], writes=[prB])
            b.emit("scalar", "activation", out=self.af.view(pr_off, 7),
                   in_=self.af.view(pr_off, 7), func=AF.Sigmoid,
                   reads=[prB], writes=[prB])
            b.emit("vector", "reduce_max", out=self.cols[:, c_t2:c_t2 + 1],
                   in_=self.af.view(pr_off, 7), axis=AX.X,
                   reads=[prB], writes=[self.colsB])
            b.emit("vector", "tensor_scalar", out=self.wsmall[:, 0:7],
                   in0=self.af.view(pr_off, 7),
                   scalar1=self.cols[:, c_t2:c_t2 + 1], scalar2=None,
                   op0=ALU.is_ge, reads=[prB, self.colsB], writes=[self.wsmallB])
            b.emit("vector", "scalar_tensor_tensor",
                   out=self.af.view(pr_off, 7),
                   in0=self.wsmall[:, 0:7], scalar=-2.0,
                   in1=self.af.view(pr_off, 7),
                   op0=ALU.mult, op1=ALU.add,
                   reads=[prB, self.wsmallB], writes=[prB])
            b.emit("vector", "reduce_max", out=self.cols[:, c_t2 + 1:c_t2 + 2],
                   in_=self.af.view(pr_off, 7), axis=AX.X,
                   reads=[prB], writes=[self.colsB])
            b.emit("vector", "tensor_scalar", out=self.af.view(pr_off, 7),
                   in0=self.af.view(pr_off, 7),
                   scalar1=self.cols[:, c_t2 + 1:c_t2 + 2], scalar2=None,
                   op0=ALU.is_ge, reads=[prB, self.colsB], writes=[prB])
            b.emit("vector", "tensor_add", out=self.wsmall[:, 0:7],
                   in0=self.wsmall[:, 0:7],
                   in1=self.af.view(pr_off, 7),
                   reads=[self.wsmallB, prB], writes=[self.wsmallB])
            b.emit("vector", "tensor_add",
                   out=self.af.view(pr_off, 7), in0=self.ps_a[:, 0:7],
                   in1=self.af.view(rb_off, 7),
                   reads=[self.ps_aB, rbB], writes=[prB])
            b.emit("scalar", "activation", out=self.af.view(pr_off, 7),
                   in_=self.af.view(pr_off, 7), func=AF.Sigmoid,
                   reads=[prB], writes=[prB])
            b.emit("vector", "tensor_add", out=self.cols[:, c_t2 + 2:c_t2 + 3],
                   in0=self.cols[:, c_t2:c_t2 + 1],
                   in1=self.cols[:, c_t2 + 1:c_t2 + 2],
                   reads=[self.colsB], writes=[self.colsB])
            b.emit("vector", "reciprocal", out=self.cols[:, c_t2 + 3:c_t2 + 4],
                   in_=self.cols[:, c_t2 + 2:c_t2 + 3],
                   reads=[self.colsB], writes=[self.colsB])
            b.emit("vector", "tensor_mul", out=self.af.view(pr_off, 7),
                   in0=self.af.view(pr_off, 7),
                   in1=self.wsmall[:, 0:7],
                   reads=[prB, self.wsmallB], writes=[prB])
            b.emit("vector", "tensor_scalar", out=self.wsmall[:, 0:7],
                   in0=self.af.view(pr_off, 7),
                   scalar1=self.cols[:, c_t2 + 3:c_t2 + 4], scalar2=None,
                   op0=ALU.mult, reads=[prB, self.colsB], writes=[self.wsmallB])
            b.emit("tensor", "matmul", self.ps_b[0:16, 0:128],
                   self.wsmall[:, 0:16], self.id_bf[:],
                   start=True, stop=True,
                   reads=[self.wsmallB, self.constB], writes=[self.ps_bB])
            if L2F_DEBUG and tt == 15:
                did = self.nc.dram_tensor("dbg_id", [128, 128], F32,
                                          kind="ExternalOutput").ap()
                o4, s4B = self.af.take(128, "dbg4")
                b.emit("vector", "tensor_copy",
                       out=self.af.view(o4, 128),
                       in_=self.id_bf[:], reads=[self.constB], writes=[s4B])
                b.emit("gpsimd", "dma_start", out=did,
                       in_=self.af.view(o4, 128),
                       reads=[s4B], writes=[s4B])
                dps = self.nc.dram_tensor("dbg_ps", [7, 128], F32,
                                          kind="ExternalOutput").ap()
                o3, s3B = self.af.take(128, "dbg3")
                b.emit("vector", "tensor_copy",
                       out=self.af.view(o3, 128)[0:7, :],
                       in_=self.ps_b[0:7, 0:128],
                       reads=[self.ps_bB], writes=[s3B])
                b.emit("gpsimd", "dma_start", out=dps,
                       in_=self.af.view(o3, 128)[0:7, :],
                       reads=[s3B], writes=[s3B])
            b.emit("scalar", "copy",
                   out=self.af.view(wg_off, 128)[0:7, :],
                   in_=self.ps_b[0:7, 0:128],
                   reads=[self.ps_bB], writes=[wgB])
            b.emit("gpsimd", "dma_start",
                   out=AP(wgt_dr.tensor, tt * 128, [[T, 7], [1, 128]]),
                   in_=self.af.view(wg_off, 128)[0:7, :],
                   reads=[wgB], writes=[wgdB])
        if L2F_DEBUG:
            dst_stage = self.nc.dram_tensor("dbg_stage", [7, 128], F32,
                                            kind="ExternalOutput").ap()
            self.b.barrier()
            b.emit("gpsimd", "dma_start", out=dst_stage,
                   in_=self.af.view(wg_off, 128)[0:7, :],
                   reads=[wgB], writes=[wgB])
            dwg = self.nc.dram_tensor("dbg_wgt", [7, T], F32,
                                      kind="ExternalOutput").ap()
            self.b.barrier()
            b.emit("gpsimd", "dma_start", out=self.af.view(wg_off, 128)[0:7, :],
                   in_=AP(wgt_dr.tensor, 0, [[T, 7], [1, 128]]),
                   reads=[wgdB], writes=[wgB])
            b.emit("gpsimd", "dma_start",
                   out=AP(dwg.tensor, 0, [[T, 7], [1, 128]]),
                   in_=self.af.view(wg_off, 128)[0:7, :],
                   reads=[wgB], writes=[wgB])
            dpb = self.nc.dram_tensor("dbg_wgtb", [128, 7], F32,
                                      kind="ExternalOutput").ap()
            off2, s2B = self.af.take(7, "dbg2")
            b.emit("vector", "tensor_copy", out=self.af.view(off2, 7),
                   in_=self.wsmall[:, 0:7], reads=[self.wsmallB], writes=[s2B])
            b.emit("gpsimd", "dma_start", out=dpb, in_=self.af.view(off2, 7),
                   reads=[s2B], writes=[s2B])
            self.b.barrier()
        gu_off, _gu = self.ab.take(4096, "gu")
        guBs = [self.b.buf("gu0"), self.b.buf("gu1")]
        dn_off, _dn = self.ab.take(16384, "down")
        dnBs = [self.b.buf("dn0"), self.b.buf("dn1")]
        hs_off, hsB = self.af.take(2 * TB, "silu")
        h2_off, h2B = self.af.take(TB, "hid_tmp")

        def gu_load(n):
            e, ic = n // 16, n % 16
            slot = n % 2
            base = e * D * 2048 + ic * 128
            b.emit("gpsimd", "dma_start",
                   out=self.ab.view(gu_off + slot * 2048, 1024),
                   in_=AP(gate_dr.tensor, base,
                          [[2048, 128], [128 * 2048, 8], [1, 128]]),
                   reads=[guBs[slot]], writes=[guBs[slot]])
            b.emit("gpsimd", "dma_start",
                   out=self.ab.view(gu_off + slot * 2048 + 1024, 1024),
                   in_=AP(up_dr.tensor, base,
                          [[2048, 128], [128 * 2048, 8], [1, 128]]),
                   reads=[guBs[slot]], writes=[guBs[slot]])

        def dn_load(m):
            e, ig = m // 2, m % 2
            slot = m % 2
            base = e * 2048 * D + ig * 8 * 128 * D
            b.emit("gpsimd", "dma_start",
                   out=self.ab.view(dn_off + slot * 8192, 8192),
                   in_=AP(down_dr.tensor, base,
                          [[D, 128], [128 * D, 8], [1, D]]),
                   reads=[dnBs[slot]], writes=[dnBs[slot]])

        gu_load(0)
        gu_load(1)
        dn_load(0)
        dn_load(1)
        for e in range(8):
            if e > 0:
                for tb in range(NTB):
                    b.emit("scalar", "dma_start",
                           out=self.af.view(wb_off + tb * TB, TB),
                           in_=AP(wgt_dr.tensor, (e - 1) * T + tb * TB,
                                  [[0, 128], [1, TB]]),
                           reads=[wgdB], writes=[wbB])
            for ig in range(2):
                m = e * 2 + ig
                dslot = m % 2
                for il in range(8):
                    n = e * 16 + ig * 8 + il
                    slot = n % 2
                    for tb in range(NTB):
                        psg, psgB = ((self.ps_a, self.ps_aB) if tb % 2 == 0
                                     else (self.ps_c, self.ps_cB))
                        psu, psuB = ((self.ps_b, self.ps_bB) if tb % 2 == 0
                                     else (self.ps_d, self.ps_dB))
                        for kc in range(NC):
                            b.emit("tensor", "matmul", psg[:],
                                   self.ab.view(gu_off + slot * 2048 + kc * 128, 128),
                                   self.XH[:, kc * T + tb * TB: kc * T + (tb + 1) * TB],
                                   start=(kc == 0), stop=(kc == NC - 1),
                                   reads=[guBs[slot], self.XHBufs[kc]], writes=[psgB])
                            b.emit("tensor", "matmul", psu[:],
                                   self.ab.view(gu_off + slot * 2048 + 1024 + kc * 128, 128),
                                   self.XH[:, kc * T + tb * TB: kc * T + (tb + 1) * TB],
                                   start=(kc == 0), stop=(kc == NC - 1),
                                   reads=[guBs[slot], self.XHBufs[kc]], writes=[psuB])
                        hsl = hs_off + (tb % 2) * TB
                        b.emit("scalar", "activation",
                               out=self.af.view(hsl, TB), in_=psg[:],
                               func=AF.Silu, reads=[psgB], writes=[hsB])
                        ofslot = il * T + tb * TB
                        if e == 0:
                            b.emit("vector", "tensor_mul",
                                   out=self.OF[:, ofslot:ofslot + TB],
                                   in0=psu[:], in1=self.af.view(hsl, TB),
                                   reads=[psuB, hsB], writes=[self.OFBufs[il]])
                        else:
                            b.emit("vector", "tensor_mul",
                                   out=self.af.view(h2_off, TB),
                                   in0=psu[:], in1=self.af.view(hsl, TB),
                                   reads=[psuB, hsB], writes=[h2B])
                            b.emit("vector", "tensor_mul",
                                   out=self.OF[:, ofslot:ofslot + TB],
                                   in0=self.af.view(h2_off, TB),
                                   in1=self.af.view(wb_off + tb * TB, TB),
                                   reads=[wbB, h2B], writes=[self.OFBufs[il]])
                    if n + 2 < 128:
                        gu_load(n + 2)
                for tb in range(NTB):
                    for oc in range(NC):
                        ps = self.ps_big[:, (oc % 4) * TB:(oc % 4 + 1) * TB]
                        for il in range(8):
                            b.emit("tensor", "matmul", ps,
                                   self.ab.view(dn_off + dslot * 8192 + il * 1024, 1024)[:, oc * 128:(oc + 1) * 128],
                                   self.OF[:, il * T + tb * TB: il * T + (tb + 1) * TB],
                                   start=(il == 0), stop=(il == 7),
                                   reads=[dnBs[dslot], self.OFBufs[il]],
                                   writes=[self.ps_bigQ[oc % 4]])
                        sl = slice(oc * T + tb * TB, oc * T + (tb + 1) * TB)
                        b.emit("vector", "tensor_add",
                               out=self.X[:, sl], in0=self.X[:, sl], in1=ps,
                               reads=[self.XBufs[oc], self.ps_bigQ[oc % 4]],
                               writes=[self.XBufs[oc]])
                if m + 2 < 16:
                    dn_load(m + 2)
        self.ab.pop()
        self.af.pop()

    def l3_attn(self):
        b = self.b
        self.layernorm()
        wq = self.din("l3_wq", (D, D))
        wk = self.din("l3_wk", (D, D))
        wv = self.din("l3_wv", (D, D))
        cb_q = self.col(NC); self.load_cols("l3_bq", (NC, 128), cb_q, NC)
        c_tmp = self.col(16); self.load_cols("l3_temp", (16, 128), c_tmp, 16)
        bk_b = self.din("l3_bk", (D,))
        bv_b = self.din("l3_bv", (D,))
        yk_dr = self.nc.dram_tensor("l3_yk_scr", [T, D], BF16, kind="Internal").ap()
        yv_dr = self.nc.dram_tensor("l3_yv_scr", [T, D], BF16, kind="Internal").ap()
        ykB = self.b.buf("l3_ykdr")
        yvB = self.b.buf("l3_yvdr")
        OFall = self.b.buf("l3_OFall")

        self.ab.push()
        self.af.push()
        st_off, stB = self.af.take(1024, "wstage")
        wsl_off, wslB = self.ab.take(2048, "wsl")
        for oc in range(NC):
            dbuf = oc % 2
            for kc in range(NC):
                b.emit("gpsimd", "dma_start",
                       out=self.ab.view(wsl_off + dbuf * 1024 + kc * 128, 128),
                       in_=wq[kc * 128:(kc + 1) * 128, oc * 128:(oc + 1) * 128],
                       reads=[wslB], writes=[wslB])
            for tb in range(NTB):
                ps = self.ps_a if tb % 2 == 0 else self.ps_b
                psB = self.ps_aB if tb % 2 == 0 else self.ps_bB
                for kc in range(NC):
                    b.emit("tensor", "matmul", ps[:],
                           self.ab.view(wsl_off + dbuf * 1024 + kc * 128, 128),
                           self.XH[:, kc * T + tb * TB: kc * T + (tb + 1) * TB],
                           start=(kc == 0), stop=(kc == NC - 1),
                           reads=[wslB, self.XHBufs[kc]], writes=[psB])
                b.emit("vector", "tensor_scalar",
                       out=self.OF[:, oc * T + tb * TB: oc * T + (tb + 1) * TB],
                       in0=ps[:], scalar1=self.cols[:, cb_q + oc:cb_q + oc + 1],
                       scalar2=None, op0=ALU.add,
                       reads=[psB, self.colsB], writes=[OFall])
        bk_off, bkbB = self.af.take(512, "bkb")
        ev_off, evB = self.ab.take(512, "yevac")
        wh_off, whB = self.ab.take(8192, "whalf")
        for wdr, bias_dr, dst, dstB, temp_scaled in (
                (wk, bk_b, yk_dr, ykB, True), (wv, bv_b, yv_dr, yvB, False)):
            for oh in range(2):
                for kc in range(NC):
                    b.emit("gpsimd", "dma_start",
                           out=self.ab.view(wh_off + kc * 512, 512),
                           in_=wdr[kc * 128:(kc + 1) * 128, oh * 512:(oh + 1) * 512],
                           reads=[whB], writes=[whB])
                b.emit("gpsimd", "dma_start",
                       out=self.af.view(bk_off, 512),
                       in_=AP(bias_dr.tensor, oh * 512, [[0, 128], [1, 512]]),
                       reads=[bkbB], writes=[bkbB])
                for tt in range(16):
                    h = tt
                    ps = self.ps_a if tt % 2 == 0 else self.ps_b
                    psB = self.ps_aB if tt % 2 == 0 else self.ps_bB
                    for kc in range(NC):
                        b.emit("tensor", "matmul", ps[:],
                               self.XH[:, kc * T + tt * 128: kc * T + (tt + 1) * 128],
                               self.ab.view(wh_off + kc * 512, 512),
                               start=(kc == 0), stop=(kc == NC - 1),
                               reads=[whB, self.XHBufs[kc]], writes=[psB])
                    if temp_scaled:
                        b.emit("vector", "scalar_tensor_tensor",
                               out=self.ab.view(ev_off, 512),
                               in0=ps[:], scalar=self.cols[:, c_tmp + h:c_tmp + h + 1],
                               in1=self.af.view(bk_off, 512),
                               op0=ALU.bypass, op1=ALU.add,
                               reads=[psB, bkbB, self.colsB], writes=[evB])
                        b.emit("vector", "tensor_scalar",
                               out=self.ab.view(ev_off, 512),
                               in0=self.ab.view(ev_off, 512),
                               scalar1=self.cols[:, c_tmp + h:c_tmp + h + 1],
                               scalar2=None, op0=ALU.mult,
                               reads=[evB, self.colsB], writes=[evB])
                    else:
                        b.emit("vector", "tensor_add",
                               out=self.ab.view(ev_off, 512),
                               in0=ps[:], in1=self.af.view(bk_off, 512),
                               reads=[psB, bkbB], writes=[evB])
                    b.emit("gpsimd", "dma_start",
                           out=dst[tt * 128:(tt + 1) * 128, oh * 512:(oh + 1) * 512],
                           in_=self.ab.view(ev_off, 512),
                           reads=[evB], writes=[dstB])
        self.ab.pop()
        self.af.pop()
        self.b.barrier()
        self.ab.push()
        self.af.push()
        km_off, kmB = self.ab.take(T, "Kmat")
        vm_off, vmB = self.ab.take(T, "Vmat")
        p_off, pB = self.ab.take(T, "Ptile")
        qs_off, qsB = self.ab.take(64, "QS")
        os_off, osB = self.af.take(T, "out_f32")
        rc_off = self.col(16)
        cm2 = self.col(2)
        for h in range(16):
            for src_dr, srcB, moff, mB in ((yk_dr, ykB, km_off, kmB),
                                           (yv_dr, yvB, vm_off, vmB)):
                for flag in range(2):
                    b.emit("sync", "dma_start",
                           out=self.ab.view(moff + flag * 1024, 1024, 0, 64),
                           in_=AP(src_dr.tensor, (128 * h + flag) * D,
                                  [[2 * D, 64], [1, D]]),
                           reads=[srcB], writes=[mB])
            psacc = (self.ps_a, self.ps_b, self.ps_c, self.ps_d)
            psaccB = (self.ps_aB, self.ps_bB, self.ps_cB, self.ps_dB)
            for tc in range(16):
                for sb in range(NTB):
                    b.emit("tensor", "matmul",
                           self.ps_big[:, sb * TB:(sb + 1) * TB],
                           self.ab.view(km_off + tc * 128, 128, 0, 64),
                           self.ab.view(vm_off + sb * TB, TB, 0, 64),
                           start=True, stop=True,
                           reads=[kmB, vmB], writes=[self.ps_bigQ[sb]])
                b.emit("vector", "reduce_max",
                       out=self.cols[:, cm2:cm2 + 1], in_=self.ps_big[:, :],
                       axis=AX.X, negate=True,
                       reads=self.ps_bigQ, writes=[self.colsB])
                for sb in range(NTB):
                    b.emit("scalar", "activation",
                           out=self.ab.view(p_off + sb * TB, TB),
                           in_=self.ps_big[:, sb * TB:(sb + 1) * TB],
                           func=AF.Exp, bias=self.cols[:, cm2:cm2 + 1],
                           reads=[self.ps_bigQ[sb], self.colsB], writes=[pB])
                b.emit("vector", "reduce_sum",
                       out=self.cols[:, cm2 + 1:cm2 + 2], in_=self.ab.view(p_off, T),
                       axis=AX.X, reads=[pB], writes=[self.colsB])
                b.emit("vector", "reciprocal",
                       out=self.cols[:, rc_off:rc_off + 1],
                       in_=self.cols[:, cm2 + 1:cm2 + 2],
                       reads=[self.colsB], writes=[self.colsB])
                qview = AP(self.OF, (tc % 8) * T + 128 * h + (tc // 8),
                           [[NC * T, 128], [2, 64]])
                b.emit("vector", "tensor_scalar",
                       out=self.ab.view(qs_off, 64), in0=qview,
                       scalar1=self.cols[:, rc_off:rc_off + 1], scalar2=None,
                       op0=ALU.mult,
                       reads=[OFall, self.colsB], writes=[qsB])
                for sb in range(NTB):
                    b.emit("tensor", "matmul", psacc[sb][0:64, :],
                           self.ab.view(qs_off, 64),
                           self.ab.view(p_off + sb * TB, TB),
                           start=(tc == 0), stop=(tc == 15),
                           reads=[qsB, pB], writes=[psaccB[sb]])
            for sb in range(NTB):
                b.emit("scalar", "copy",
                       out=self.af.view(os_off + sb * TB, TB)[0:64, :],
                       in_=psacc[sb][0:64, :],
                       reads=[psaccB[sb]], writes=[osB])
            for sc in range(16):
                cf = sc % 8
                flag = sc // 8
                ps = self.ps_a if sc % 2 == 0 else self.ps_b
                psB = self.ps_aB if sc % 2 == 0 else self.ps_bB
                b.emit("tensor", "transpose", ps[:, 0:64],
                       self.af.view(os_off + sc * 128, 128)[0:64, :],
                       self.id_f32[0:64, 0:64],
                       reads=[osB, self.constB], writes=[psB])
                xview = AP(self.X, cf * T + 128 * h + flag, [[NC * T, 128], [2, 64]])
                b.emit("vector", "tensor_add",
                       out=xview, in0=xview, in1=ps[:, 0:64],
                       reads=[self.XBufs[cf], psB], writes=[self.XBufs[cf]])
        self.ab.pop()
        self.af.pop()

    def l3_ffn(self):
        b = self.b
        self.layernorm()
        w_t = self.din("l3f_w_t", (D, D))
        vr = self.din("l3f_vr", (D, 32))
        c_t = self.din("l3f_c_t", (32, D))
        cb = self.col(NC); self.load_cols("l3f_b", (NC, 128), cb, NC)
        self.ab.push()
        self.af.push()
        st_off, stB = self.af.take(1024, "wstage")
        w_off, wB = self.ab.take(8 * 1024, "l3f_w")
        xr_off, xrB = self.ab.take(T, "l3f_xr")
        ct_off, ctB = self.ab.take(1024, "l3f_ct")
        self.load_wt_bf16(w_t, D, D, w_off, wB, st_off, stB)
        b.emit("gpsimd", "dma_start", out=self.ab.view(ct_off, 1024)[0:32, :],
               in_=c_t, reads=[ctB], writes=[ctB])
        vr_off, vrB = self.ab.take(256, "l3f_vr")
        for kc in range(NC):
            b.emit("gpsimd", "dma_start",
                   out=self.ab.view(vr_off + kc * 32, 32),
                   in_=vr[kc * 128:(kc + 1) * 128, :], reads=[vrB], writes=[vrB])
        for tb in range(NTB):
            ps = self.ps_c
            for kc in range(NC):
                b.emit("tensor", "matmul", ps[0:32, :],
                       self.ab.view(vr_off + kc * 32, 32),
                       self.XH[:, kc * T + tb * TB: kc * T + (tb + 1) * TB],
                       start=(kc == 0), stop=(kc == NC - 1),
                       reads=[vrB, self.XHBufs[kc]], writes=[self.ps_cB])
            b.emit("scalar", "copy",
                   out=self.ab.view(xr_off + tb * TB, TB)[0:32, :],
                   in_=ps[0:32, :], reads=[self.ps_cB], writes=[xrB])
        for oc in range(NC):
            for tb in range(NTB):
                ps = self.ps_a if (oc + tb) % 2 == 0 else self.ps_b
                psB = self.ps_aB if (oc + tb) % 2 == 0 else self.ps_bB
                for kc in range(NC):
                    b.emit("tensor", "matmul", ps[:],
                           self.ab.view(w_off + kc * 1024, 1024)[:, oc * 128:(oc + 1) * 128],
                           self.XH[:, kc * T + tb * TB: kc * T + (tb + 1) * TB],
                           start=(kc == 0), stop=False,
                           reads=[wB, self.XHBufs[kc]], writes=[psB])
                b.emit("tensor", "matmul", ps[:],
                       self.ab.view(ct_off, 1024)[0:32, oc * 128:(oc + 1) * 128],
                       self.ab.view(xr_off + tb * TB, TB)[0:32, :],
                       start=False, stop=True,
                       reads=[ctB, xrB], writes=[psB])
                b.emit("vector", "scalar_tensor_tensor",
                       out=self.X[:, oc * T + tb * TB: oc * T + (tb + 1) * TB],
                       in0=ps[:], scalar=self.cols[:, cb + oc:cb + oc + 1],
                       in1=self.X[:, oc * T + tb * TB: oc * T + (tb + 1) * TB],
                       op0=ALU.add, op1=ALU.add,
                       reads=[psB, self.colsB, self.XBufs[oc]],
                       writes=[self.XBufs[oc]])
        self.ab.pop()
        self.af.pop()



BLOCK_INPUT_KEYS = {
    "l0_attn": ["l0_wq_t", "l0_wk_t", "l0_wv_t", "l0_wo_t", "l0_bq", "l0_bk",
                "l0_bv", "l0_bo", "l0_ext", "l0_ext_rev", "l0_rel_raw"],
}
CONST_KEYS = ["c_id_bf", "c_id_f32", "c_J_f32", "c_ones_row_f32",
              "c_ones_col_f32"]


def build_program(blocks, avail):
    nc = bass.Bass("TRN2", target_bir_lowering=False, debug=False)
    k = Ker(nc, avail)
    k.zero_init()
    k.b.barrier()
    k.load_consts()
    k.prologue()
    for blk in blocks:
        getattr(k, blk)()
        k.b.barrier()
    k.epilogue()
    k.b.finalize()
    k.b.es.close()
    return nc, sorted(k.dram.keys())


def run_blocks(inputs, blocks, n_cores=8):
    common = host_prep_common(inputs)
    x = np.ascontiguousarray(np.asarray(inputs["L0_attn_x"], np.float32))
    xs = [x[i] for i in range(n_cores)]
    avail = set(common.keys()) | {"x_in"}
    nc, used = build_program(blocks, avail)
    in_maps = []
    for i in range(n_cores):
        m = {n: common[n] for n in used if n != "x_in"}
        m["x_in"] = xs[i]
        in_maps.append(m)
    res = run_bass_kernel_spmd(nc, in_maps, core_ids=list(range(n_cores)))
    out = np.stack([np.asarray(r["y_out"]) for r in res.results], axis=0)
    return out


ALL_BLOCKS = ["l0_attn", "l0_ffn", "l1_attn", "l1_ffn", "l2_attn", "l2_ffn", "l3_attn", "l3_ffn"]


def kernel(**inputs):
    return run_blocks(inputs, ALL_BLOCKS).astype(np.float32)


# version 08
# speedup vs baseline: 1.4605x; 1.0434x over previous
import numpy as np
import ml_dtypes
from contextlib import ExitStack

import concourse.bass as bass
import concourse.mybir as mybir
from concourse.bass import AP
from concourse.bass_utils import run_bass_kernel_spmd

F32 = mybir.dt.float32
BF16 = mybir.dt.bfloat16
AF = mybir.ActivationFunctionType
ALU = mybir.AluOpType
AX = mybir.AxisListType

T = 2048
L2F_PARTS = "all"
L2F_DEBUG = False
QUIET_GROUPS = False
D = 1024
NC = 8
NTB = 4
TB = 512
NEG = -1.0e30
EPS = 1e-5

ENGINE_NAMES = ("sync", "tensor", "vector", "scalar", "gpsimd")



class Buf:
    __slots__ = ("name", "writer", "readers")

    def __init__(self, name):
        self.name = name
        self.writer = None
        self.readers = {}


class EngRec:
    def __init__(self, name, csem, dsem):
        self.name = name
        self.csem = csem
        self.dsem = dsem
        self.cn = 0
        self.dn = 0
        self.items = []
        self.watermark = {}
        self.pend_r = []
        self.pend_w = []


class Bld:
    def __init__(self, nc):
        self.nc = nc
        self.es = ExitStack()
        self.engs = {}
        for n in ENGINE_NAMES:
            csem = self.es.enter_context(nc.semaphore(f"c_{n}"))
            dsem = self.es.enter_context(nc.semaphore(f"d_{n}"))
            self.engs[n] = EngRec(n, csem, dsem)
        self.bufs = []

    def buf(self, name):
        b = Buf(name)
        self.bufs.append(b)
        return b

    def emit(self, ename, method, *args, reads=(), writes=(), **kw):
        e = self.engs[ename]
        waits = []

        def need(tok):
            if tok is None:
                return
            sem, val = tok
            if sem is e.csem and ename == "tensor":
                return
            if e.watermark.get(id(sem), 0) >= val:
                return
            waits.append((sem, val))
            e.watermark[id(sem)] = val

        for b in reads:
            need(b.writer)
        for b in writes:
            need(b.writer)
            for rt in b.readers.values():
                need(rt)
        is_dma = method in ("dma_start", "dma_start_transpose")
        quiet = QUIET_GROUPS and (method == "matmul" and kw.get("stop") is False)
        if quiet:
            e.items.append((waits, method, args, kw, None))
            e.pend_r.extend(reads)
            e.pend_w.extend(writes)
            return None
        if is_dma:
            e.dn += 16
            tok = (e.dsem, e.dn)
            inc = (e.dsem, 16)
        else:
            e.cn += 1
            tok = (e.csem, e.cn)
            inc = (e.csem, 1)
        e.items.append((waits, method, args, kw, inc))
        for b in e.pend_r:
            b.readers[ename] = tok
        for b in e.pend_w:
            b.writer = tok
            b.readers = {}
        e.pend_r = []
        e.pend_w = []
        for b in reads:
            b.readers[ename] = tok
        for b in writes:
            b.writer = tok
            b.readers = {}
        return tok

    def barrier(self):
        toks = []
        for n in ENGINE_NAMES:
            e = self.engs[n]
            toks.append((e.csem, e.cn))
            toks.append((e.dsem, e.dn))
        for n in ENGINE_NAMES:
            e = self.engs[n]
            for sem, val in toks:
                if sem is e.csem or val == 0:
                    continue
                if e.watermark.get(id(sem), 0) >= val:
                    continue
                e.items.append(([(sem, val)], None, None, None, None))
                e.watermark[id(sem)] = val

    def finalize(self):
        nc = self.nc
        with nc.Block() as block:
            for n in ENGINE_NAMES:
                rec = self.engs[n]

                def run(eng, rec=rec):
                    for waits, method, args, kw, inc in rec.items:
                        for sem, val in waits:
                            eng.wait_ge(sem, val)
                        if method is None:
                            continue
                        ins = getattr(eng, method)(*args, **kw)
                        if inc is not None:
                            ins.then_inc(*inc)
                getattr(block, n)(run)



class Arena:
    def __init__(self, bld, name, dtype, cols):
        self.bld = bld
        self.t = bld.es.enter_context(
            bld.nc.sbuf_tensor(name, [128, cols], dtype))
        self.cols = cols
        self.off = 0
        self.marks = []

    def take(self, cols, name):
        self.off = (self.off + 15) & ~15
        assert self.off + cols <= self.cols, \
            f"arena {self.t.name} overflow: {self.off}+{cols}>{self.cols} ({name})"
        off = self.off
        self.off += cols
        return off, self.bld.buf(name)

    def view(self, off, cols, p0=0, p1=128):
        return self.t[p0:p1, off:off + cols]

    def push(self):
        self.marks.append(self.off)

    def pop(self):
        self.off = self.marks.pop()



def _hp(x):
    return np.ascontiguousarray(np.asarray(x, dtype=np.float32))


def _chunk_vec(v):
    return _hp(np.asarray(v, np.float32).reshape(NC, 128))


def host_prep_common(inp):
    d = {}
    d["c_id_bf"] = _hp(np.eye(128))
    d["c_id_f32"] = _hp(np.eye(128))
    d["c_J_f32"] = _hp(np.eye(128)[::-1])
    d["c_ones_row_f32"] = _hp(np.ones((1, 128)))
    d["c_ones_col_f32"] = _hp(np.ones((128, 1)))

    d["l0_wq_t"] = _hp(np.asarray(inp["L0_attn_Wq"]).T)
    d["l0_wk_t"] = _hp(np.asarray(inp["L0_attn_Wk"]).T)
    d["l0_wv_t"] = _hp(np.asarray(inp["L0_attn_Wv"]).T)
    d["l0_wo_t"] = _hp(np.asarray(inp["L0_attn_Wo"]).T)
    d["l0_bq"] = _chunk_vec(inp["L0_attn_bq"])
    d["l0_bk"] = _chunk_vec(inp["L0_attn_bk"])
    d["l0_bv"] = _chunk_vec(inp["L0_attn_bv"])
    d["l0_bo"] = _chunk_vec(inp["L0_attn_bo"])
    rel = np.asarray(inp["L0_attn_rel_bias"], np.float32)
    ext = np.pad(rel, ((0, 0), (639, 639)), mode="edge")
    d["l0_ext"] = _hp(ext)
    d["l0_ext_rev"] = _hp(ext[:, ::-1])
    d["l0_rel_raw"] = _hp(rel)

    d["l0f_rails"] = _hp(inp["L0_ffn_rail_maker"])
    d["l0f_up_t"] = _hp(np.asarray(inp["L0_ffn_rail_up_proj"]).transpose(0, 1, 3, 2))
    d["l0f_wc_t"] = _hp(np.asarray(inp["L0_ffn_w_cproj"]).T)

    d["l1_w_qkv"] = _hp(inp["L1_attn_w_qkv"])
    d["l1_w_out"] = _hp(inp["L1_attn_w_out"])
    d["l1_b_out"] = _chunk_vec(inp["L1_attn_b_out"])
    d["l1_g"] = _chunk_vec(inp["L1_attn_g"])
    d["l1_b"] = _chunk_vec(inp["L1_attn_b"])
    d["l1f_wr_t"] = _hp(np.asarray(inp["L1_ffn_W_r"]).T)
    d["l1f_wk_t"] = _hp(np.asarray(inp["L1_ffn_W_k"]).T)
    d["l1f_wv_t"] = _hp(np.asarray(inp["L1_ffn_W_v"]).T)
    d["l1f_mu_r"] = _chunk_vec(inp["L1_ffn_mu_r"])
    d["l1f_mu_k"] = _chunk_vec(inp["L1_ffn_mu_k"])

    d["l2_wq"] = _hp(inp["L2_attn_Wq"])
    d["l2_wkv"] = _hp(inp["L2_attn_Wkv"])
    d["l2_bq"] = _chunk_vec(inp["L2_attn_bq"])
    bkv = np.asarray(inp["L2_attn_bkv"], np.float32)
    d["l2_bk"] = _chunk_vec(bkv[0:1024])
    d["l2_bv"] = _hp(bkv[1024:2048])
    d["l2_wo"] = _hp(inp["L2_attn_Wo"])
    d["l2_bo"] = _chunk_vec(inp["L2_attn_bo"])
    pos = np.arange(T, dtype=np.float64)
    invf = 1.0 / 10000.0 ** (np.arange(0, 64, 2, dtype=np.float64) / 64.0)
    ang = pos[None, :] * invf[:, None]
    cs = np.cos(ang); sn = np.sin(ang)
    d["l2_cos"] = _hp(np.concatenate([cs, cs], axis=0))
    d["l2_sin"] = _hp(np.concatenate([sn, sn], axis=0))
    tri = np.where(np.arange(128)[:, None] <= np.arange(128)[None, :], 0.0, -1e30)
    d["l2_mtri"] = _hp(tri)
    d["l2_mtri_t"] = _hp(tri.T)
    d["l2f_kv_d_t"] = _hp(np.asarray(inp["L2_ffn_kv_d"]).T)
    d["l2f_q_d_t"] = _hp(np.asarray(inp["L2_ffn_q_d"]).T)
    d["l2f_k_u_t"] = _hp(np.asarray(inp["L2_ffn_k_u"]).T)
    d["l2f_q_u_t"] = _hp(np.asarray(inp["L2_ffn_q_u"]).T)
    d["l2f_v_u_t"] = _hp(np.asarray(inp["L2_ffn_v_u"]).T)
    d["l2f_rope_q_t"] = _hp(np.asarray(inp["L2_ffn_rope_q_w"]).T)
    d["l2f_rope_k_t"] = _hp(np.asarray(inp["L2_ffn_rope_k_w"]).T)
    d["l2f_o_w_t"] = _hp(np.asarray(inp["L2_ffn_o_w"]).T)
    d["l2f_ln1"] = _chunk_vec(inp["L2_ffn_ln1_w"])
    d["l2f_ln2"] = _chunk_vec(inp["L2_ffn_ln2_w"])
    d["l2f_router_t"] = _hp(np.asarray(inp["L2_ffn_router_w"]).T)
    d["l2f_rbias"] = _hp(np.tile(np.asarray(inp["L2_ffn_routing_bias"],
                                            np.float32)[None, :], (128, 1)))
    pos2 = np.arange(T, dtype=np.float64)
    invf2 = 1.0 / 10000.0 ** (np.arange(0, 32, 2, dtype=np.float64) / 32.0)
    idx2 = np.arange(128) % 32 % 16
    ang2 = pos2[None, :] * invf2[idx2][:, None]
    d["l2f_cos2"] = _hp(np.cos(ang2))
    d["l2f_sin2"] = _hp(np.sin(ang2))
    P = np.zeros((128, 128), np.float32)
    for blk in range(4):
        for i in range(16):
            P[blk * 32 + i, blk * 32 + i + 16] = -1.0
            P[blk * 32 + 16 + i, blk * 32 + i] = 1.0
    d["l2f_p32t"] = _hp(P.T)
    gate = np.concatenate([np.asarray(inp["L2_ffn_shared_gate"], np.float32),
                           np.asarray(inp["L2_ffn_routed_gate"], np.float32)])
    up = np.concatenate([np.asarray(inp["L2_ffn_shared_up"], np.float32),
                         np.asarray(inp["L2_ffn_routed_up"], np.float32)])
    down = np.concatenate([np.asarray(inp["L2_ffn_shared_down"], np.float32),
                           np.asarray(inp["L2_ffn_routed_down"], np.float32)])
    d["l2f_gate_t"] = _hp(gate.transpose(0, 2, 1))
    d["l2f_up_t"] = _hp(up.transpose(0, 2, 1))
    d["l2f_down_t"] = _hp(down.transpose(0, 2, 1))

    d["l3_wq"] = _hp(inp["L3_attn_Wq"])
    d["l3_wk"] = _hp(inp["L3_attn_Wk"])
    d["l3_wv"] = _hp(inp["L3_attn_Wv"])
    d["l3_bq"] = _chunk_vec(inp["L3_attn_bq"])
    d["l3_bk"] = _hp(inp["L3_attn_bk"])
    d["l3_bv"] = _hp(inp["L3_attn_bv"])
    d["l3_temp"] = _hp(np.tile(np.asarray(inp["L3_attn_temperature"],
                                          np.float32).reshape(16, 1), (1, 128)))
    d["l3f_w_t"] = _hp(np.asarray(inp["L3_ffn_W"]).T)
    d["l3f_vr"] = _hp(inp["L3_ffn_V_r"])
    d["l3f_c_t"] = _hp(np.asarray(inp["L3_ffn_C"]).T)
    d["l3f_b"] = _chunk_vec(inp["L3_ffn_b"])
    return d


def _merge(primary, filler):
    out = []
    si = 0
    n = max(1, len(primary))
    for i, f in enumerate(primary):
        out.append(f)
        want = (i + 1) * len(filler) // n
        while si < want:
            out.append(filler[si])
            si += 1
    out.extend(filler[si:])
    return out



class Ker:

    def __init__(self, nc, inputs_avail):
        self.nc = nc
        self.b = Bld(nc)
        self.dram = {}
        self.inputs_avail = inputs_avail
        es = self.b.es
        self.X = es.enter_context(nc.sbuf_tensor("X", [128, NC * T], F32))
        self.XBufs = [self.b.buf(f"X{c}") for c in range(NC)]
        self.XH = es.enter_context(nc.sbuf_tensor("XH", [128, NC * T], BF16))
        self.XHBufs = [self.b.buf(f"XH{c}") for c in range(NC)]
        self.OF = es.enter_context(nc.sbuf_tensor("OF", [128, NC * T], BF16))
        self.OFBufs = [self.b.buf(f"OF{c}") for c in range(NC)]
        self.af = Arena(self.b, "AF32", F32, 6144)
        self.ab = Arena(self.b, "ABF16", BF16, 21504)
        self.m_row = es.enter_context(nc.sbuf_tensor("m_row", [1, TB], F32))
        self.r_row = es.enter_context(nc.sbuf_tensor("r_row", [1, TB], F32))
        self.srow1 = es.enter_context(nc.sbuf_tensor("srow1", [1, TB], F32))
        self.srow2 = es.enter_context(nc.sbuf_tensor("srow2", [1, TB], F32))
        self.m_rowB = self.b.buf("m_row")
        self.r_rowB = self.b.buf("r_row")
        self.srow1B = self.b.buf("srow1")
        self.srow2B = self.b.buf("srow2")
        self.cols = es.enter_context(nc.sbuf_tensor("cols", [128, 224], F32))
        self.wsmall = es.enter_context(nc.sbuf_tensor("wsmall", [128, 16], BF16))
        self.wsmallB = self.b.buf("wsmall")
        self.augst = es.enter_context(nc.sbuf_tensor("augst", [16, 128], BF16))
        self.brow_hi = es.enter_context(nc.sbuf_tensor("brow_hi", [1, TB], BF16))
        self.brow_lo = es.enter_context(nc.sbuf_tensor("brow_lo", [1, TB], BF16))
        self.browB = self.b.buf("brow")
        self.ones_row_bf = es.enter_context(nc.sbuf_tensor("ones_row_bf", [1, 128], BF16))
        self.colsB = self.b.buf("cols")
        self.colsB2 = self.b.buf("cols2")
        self.col_alloc = 0
        self.id_bf = es.enter_context(nc.sbuf_tensor("id_bf", [128, 128], BF16))
        self.id_f32 = es.enter_context(nc.sbuf_tensor("id_f32", [128, 128], F32))
        self.J_bf = es.enter_context(nc.sbuf_tensor("J_bf", [128, 128], BF16))
        self.ones_row = es.enter_context(nc.sbuf_tensor("ones_row", [1, 128], F32))
        self.ones_col_bf = es.enter_context(nc.sbuf_tensor("ones_col_bf", [128, 1], BF16))
        self.constB = self.b.buf("consts")
        self.ps_big = es.enter_context(nc.psum_tensor("ps_big", [128, 2048], F32))
        self.ps_bigB = self.b.buf("ps_big")
        self.ps_bigQ = [self.b.buf(f"ps_bigq{i}") for i in range(4)]
        self.ps_a = es.enter_context(nc.psum_tensor("ps_a", [128, 512], F32))
        self.ps_aB = self.b.buf("ps_a")
        self.ps_b = es.enter_context(nc.psum_tensor("ps_b", [128, 512], F32))
        self.ps_bB = self.b.buf("ps_b")
        self.ps_c = es.enter_context(nc.psum_tensor("ps_c", [128, 512], F32))
        self.ps_cB = self.b.buf("ps_c")
        self.ps_d = es.enter_context(nc.psum_tensor("ps_d", [128, 512], F32))
        self.ps_dB = self.b.buf("ps_d")

    def din(self, name, shape):
        if name not in self.dram:
            self.dram[name] = self.nc.dram_tensor(
                name, list(shape), F32, kind="ExternalInput").ap()
            assert name in self.inputs_avail, name
        return self.dram[name]

    def col(self, n=1):
        off = self.col_alloc
        self.col_alloc += n
        assert self.col_alloc <= 224
        return off

    def zero_init(self):
        b = self.b
        b.emit("gpsimd", "memset", self.ab.view(0, self.ab.cols), 0.0,
               writes=[self.b.buf("z0")])
        b.emit("gpsimd", "memset", self.af.view(0, self.af.cols), 0.0,
               writes=[self.b.buf("z1")])
        b.emit("gpsimd", "memset", self.OF[:, :], 0.0, writes=self.OFBufs)
        b.emit("gpsimd", "memset", self.cols[:, :], 0.0, writes=[self.colsB])
        b.emit("vector", "memset", self.ps_big[:, :], 0.0,
               writes=[self.ps_bigB] + self.ps_bigQ)
        b.emit("vector", "memset", self.ps_a[:, :], 0.0, writes=[self.ps_aB])
        b.emit("vector", "memset", self.ps_b[:, :], 0.0, writes=[self.ps_bB])
        b.emit("vector", "memset", self.ps_c[:, :], 0.0, writes=[self.ps_cB])
        b.emit("vector", "memset", self.ps_d[:, :], 0.0, writes=[self.ps_dB])

    def load_consts(self):
        b = self.b
        idb = self.din("c_id_bf", (128, 128))
        idf = self.din("c_id_f32", (128, 128))
        jf = self.din("c_J_f32", (128, 128))
        onr = self.din("c_ones_row_f32", (1, 128))
        b.emit("sync", "dma_start", out=self.id_f32[:], in_=idf,
               writes=[self.constB])
        b.emit("sync", "dma_start", out=self.ones_row[:], in_=onr,
               writes=[self.constB])
        b.emit("gpsimd", "dma_start", out=self.id_bf[:], in_=idb,
               reads=[self.constB], writes=[self.constB])
        b.emit("gpsimd", "dma_start", out=self.J_bf[:], in_=jf,
               reads=[self.constB], writes=[self.constB])
        b.emit("gpsimd", "memset", self.ones_col_bf[:], 1.0,
               writes=[self.constB])
        b.emit("gpsimd", "memset", self.ones_row_bf[:], 1.0,
               writes=[self.constB])
        self.c_eps = self.col(1)
        b.emit("gpsimd", "memset", self.cols[:, self.c_eps:self.c_eps + 1], EPS,
               writes=[self.colsB])

    def prologue(self, xname="x_in"):
        b = self.b
        x = self.din(xname, (T, D))
        self.af.push()
        soff, sB = self.af.take(1024, "prolog_stage")
        for tt in range(16):
            b.emit("sync", "dma_start", out=self.af.view(soff, 1024),
                   in_=x[tt * 128:(tt + 1) * 128, :],
                   reads=[sB], writes=[sB])
            for c in range(NC):
                ps = self.ps_a if (c % 2 == 0) else self.ps_b
                psB = self.ps_aB if (c % 2 == 0) else self.ps_bB
                b.emit("tensor", "transpose", ps[:, 0:128],
                       self.af.view(soff, 1024)[:, c * 128:(c + 1) * 128],
                       self.id_f32[:],
                       reads=[sB, self.constB], writes=[psB])
                b.emit("vector", "tensor_copy",
                       out=self.X[:, c * T + tt * 128: c * T + (tt + 1) * 128],
                       in_=ps[:, 0:128], reads=[psB], writes=[self.XBufs[c]])
        self.af.pop()

    def epilogue(self, yname="y_out"):
        b = self.b
        y = self.nc.dram_tensor(yname, [T, D], F32, kind="ExternalOutput").ap()
        self.af.push()
        soff, sB = self.af.take(1024, "epilog_stage")
        for tt in range(16):
            for c in range(NC):
                ps = self.ps_a if (c % 2 == 0) else self.ps_b
                psB = self.ps_aB if (c % 2 == 0) else self.ps_bB
                b.emit("tensor", "transpose", ps[:, 0:128],
                       self.X[:, c * T + tt * 128: c * T + (tt + 1) * 128],
                       self.id_f32[:],
                       reads=[self.XBufs[c], self.constB], writes=[psB])
                b.emit("vector", "tensor_copy",
                       out=self.af.view(soff, 1024)[:, c * 128:(c + 1) * 128],
                       in_=ps[:, 0:128], reads=[psB], writes=[sB])
            b.emit("sync", "dma_start", out=y[tt * 128:(tt + 1) * 128, :],
                   in_=self.af.view(soff, 1024), reads=[sB], writes=[sB])
        self.af.pop()
        e = self.b.engs["sync"]
        e.items.append(([(e.dsem, e.dn)], None, None, None, None))

    def bcast_row(self, ps, psB, row_ap, rowB, width):
        b = self.b
        b.emit("vector", "tensor_copy", out=self.brow_hi[0:1, 0:width],
               in_=row_ap, reads=[rowB], writes=[self.browB])
        b.emit("vector", "tensor_sub", out=self.brow_lo[0:1, 0:width],
               in0=row_ap, in1=self.brow_hi[0:1, 0:width],
               reads=[rowB, self.browB], writes=[self.browB])
        b.emit("tensor", "matmul", ps[:, 0:width],
               self.ones_row_bf[0:1, :], self.brow_hi[0:1, 0:width],
               start=True, stop=False,
               reads=[self.browB, self.constB], writes=[psB])
        b.emit("tensor", "matmul", ps[:, 0:width],
               self.ones_row_bf[0:1, :], self.brow_lo[0:1, 0:width],
               start=False, stop=True,
               reads=[self.browB, self.constB], writes=[psB])

    def bcast_row_fast(self, ps, psB, row_ap, rowB, width):
        b = self.b
        b.emit("scalar", "copy", out=self.brow_hi[0:1, 0:width],
               in_=row_ap, reads=[rowB], writes=[self.browB])
        b.emit("tensor", "matmul", ps[:, 0:width],
               self.ones_row_bf[0:1, :], self.brow_hi[0:1, 0:width],
               start=True, stop=True,
               reads=[self.browB, self.constB], writes=[psB])

    def flush_aug(self, qoffh, qBh):
        b = self.b
        b.emit("tensor", "matmul", self.ps_c[0:16, 0:128],
               self.wsmall[:, 0:16], self.id_bf[:],
               start=True, stop=True,
               reads=[self.wsmallB, self.constB], writes=[self.ps_cB])
        b.emit("scalar", "copy", out=self.augst[:],
               in_=self.ps_c[0:16, 0:128],
               reads=[self.ps_cB], writes=[self.wsmallB])
        b.emit("gpsimd", "dma_start",
               out=self.ab.view(qoffh, T, 64, 65),
               in_=self.augst[:],
               reads=[self.wsmallB], writes=[qBh])

    def layernorm(self, scale_col=None, bias_col=None):
        b = self.b
        self.af.push()
        self.ab.push()
        sqoff, sqB = self.ab.take(TB, "ln_sq")
        tmpoff, tmpB = self.af.take(TB, "ln_tmp")
        for tb in range(NTB):
            for c in range(NC):
                b.emit("scalar", "copy",
                       out=self.XH[:, c * T + tb * TB: c * T + (tb + 1) * TB],
                       in_=self.X[:, c * T + tb * TB: c * T + (tb + 1) * TB],
                       reads=[self.XBufs[c]], writes=[self.XHBufs[c]])
                b.emit("scalar", "activation",
                       out=self.ab.view(sqoff, TB),
                       in_=self.XH[:, c * T + tb * TB: c * T + (tb + 1) * TB],
                       func=AF.Square,
                       reads=[self.XHBufs[c]], writes=[sqB])
                b.emit("tensor", "matmul", self.ps_c[0:1, :],
                       self.ones_col_bf[:],
                       self.XH[:, c * T + tb * TB: c * T + (tb + 1) * TB],
                       start=(c == 0), stop=(c == NC - 1),
                       reads=[self.XHBufs[c], self.constB], writes=[self.ps_cB])
                b.emit("tensor", "matmul", self.ps_d[0:1, :],
                       self.ones_col_bf[:], self.ab.view(sqoff, TB),
                       start=(c == 0), stop=(c == NC - 1),
                       reads=[sqB, self.constB], writes=[self.ps_dB])
            mr = self.m_row[0:1, :]
            rr = self.r_row[0:1, :]
            s1 = self.srow1[0:1, :]
            s2 = self.srow2[0:1, :]
            b.emit("vector", "tensor_scalar", out=mr, in0=self.ps_c[0:1, :],
                   scalar1=1.0 / D, scalar2=None, op0=ALU.mult,
                   reads=[self.ps_cB], writes=[self.m_rowB])
            b.emit("vector", "tensor_scalar", out=s1, in0=self.ps_d[0:1, :],
                   scalar1=1.0 / D, scalar2=None, op0=ALU.mult,
                   reads=[self.ps_dB], writes=[self.srow1B])
            b.emit("vector", "tensor_mul", out=s2, in0=mr, in1=mr,
                   reads=[self.m_rowB], writes=[self.srow2B])
            b.emit("vector", "tensor_sub", out=s1, in0=s1, in1=s2,
                   reads=[self.srow1B, self.srow2B], writes=[self.srow1B])
            b.emit("scalar", "activation", out=s1, in_=s1, func=AF.Sqrt,
                   bias=self.cols[0:1, self.c_eps:self.c_eps + 1],
                   reads=[self.srow1B, self.colsB], writes=[self.srow1B])
            b.emit("vector", "reciprocal", out=rr, in_=s1,
                   reads=[self.srow1B], writes=[self.r_rowB])
            self.bcast_row(self.ps_c, self.ps_cB, mr, self.m_rowB, TB)
            self.bcast_row(self.ps_d, self.ps_dB, rr, self.r_rowB, TB)
            for c in range(NC):
                b.emit("vector", "tensor_sub",
                       out=self.af.view(tmpoff, TB),
                       in0=self.X[:, c * T + tb * TB: c * T + (tb + 1) * TB],
                       in1=self.ps_c[:],
                       reads=[self.XBufs[c], self.ps_cB], writes=[tmpB])
                b.emit("vector", "tensor_mul",
                       out=self.XH[:, c * T + tb * TB: c * T + (tb + 1) * TB],
                       in0=self.af.view(tmpoff, TB), in1=self.ps_d[:],
                       reads=[tmpB, self.ps_dB], writes=[self.XHBufs[c]])
                if scale_col is not None:
                    sc_t, sc_B, sc_off = scale_col
                    bi_t, bi_B, bi_off = bias_col
                    b.emit("vector", "tensor_scalar",
                           out=self.XH[:, c * T + tb * TB: c * T + (tb + 1) * TB],
                           in0=self.XH[:, c * T + tb * TB: c * T + (tb + 1) * TB],
                           scalar1=sc_t[:, sc_off + c:sc_off + c + 1],
                           scalar2=bi_t[:, bi_off + c:bi_off + c + 1],
                           op0=ALU.mult, op1=ALU.add,
                           reads=[self.XHBufs[c], sc_B], writes=[self.XHBufs[c]])
        self.af.pop()
        self.ab.pop()
        self.b.barrier()

    def load_wt_bf16(self, dram_ap, rows, cols, ab_off, abB, st_off=None,
                     stB=None, cast_eng=None):
        b = self.b
        nk = rows // 128
        for kc in range(nk):
            b.emit("gpsimd", "dma_start",
                   out=self.ab.view(ab_off + kc * cols, cols),
                   in_=dram_ap[kc * 128:(kc + 1) * 128, :],
                   reads=[abB], writes=[abB])

    def load_cols(self, dram_name, shape, coff, n):
        b = self.b
        v = self.din(dram_name, shape)
        for c in range(n):
            b.emit("sync", "dma_start",
                   out=self.cols[:, coff + c:coff + c + 1],
                   in_=v[c:c + 1, :].rearrange("a b -> b a"),
                   reads=[self.colsB], writes=[self.colsB])

    def proj_fm(self, wt_ab_off, wtB, out_t, out_bufs, out_base_col, osize,
                bias_coff=None, scale=None, in_t=None, in_bufs=None,
                in_base=0, ksize=D, evac="scalar", accum_ps=None):
        b = self.b
        if in_t is None:
            in_t, in_bufs = self.XH, self.XHBufs
        nk = ksize // 128
        no = osize // 128
        for oc in range(no):
            for tb in range(NTB):
                ps = self.ps_a if ((oc * NTB + tb) % 2 == 0) else self.ps_b
                psB = self.ps_aB if ((oc * NTB + tb) % 2 == 0) else self.ps_bB
                for kc in range(nk):
                    b.emit("tensor", "matmul", ps[:],
                           self.ab.view(wt_ab_off + kc * osize, osize)[:, oc * 128:(oc + 1) * 128],
                           in_t[:, in_base + kc * T + tb * TB: in_base + kc * T + (tb + 1) * TB],
                           start=(kc == 0), stop=(kc == nk - 1),
                           reads=[wtB, in_bufs[kc]], writes=[psB])
                kwargs = {}
                if bias_coff is not None:
                    kwargs["bias"] = self.cols[:, bias_coff + oc:bias_coff + oc + 1]
                if scale is not None:
                    kwargs["scale"] = scale
                b.emit(evac, "activation",
                       out=out_t[:, out_base_col + oc * T + tb * TB: out_base_col + oc * T + (tb + 1) * TB],
                       in_=ps[:], func=AF.Copy,
                       reads=[psB, self.colsB], writes=[out_bufs[oc]], **kwargs)

    def l0_attn(self):
        b = self.b
        nc = self.nc
        self.layernorm()
        wq = self.din("l0_wq_t", (D, D))
        wk = self.din("l0_wk_t", (D, D))
        wv = self.din("l0_wv_t", (D, D))
        wo = self.din("l0_wo_t", (D, D))
        ext = self.din("l0_ext", (16, 1535))
        ext_rev = self.din("l0_ext_rev", (16, 1535))
        rel_raw = self.din("l0_rel_raw", (16, 257))
        cb_q = self.col(NC); self.load_cols("l0_bq", (NC, 128), cb_q, NC)
        cb_k = self.col(NC); self.load_cols("l0_bk", (NC, 128), cb_k, NC)
        cb_o = self.col(NC); self.load_cols("l0_bo", (NC, 128), cb_o, NC)
        bv_dram = self.din("l0_bv", (NC, 128))
        b.emit("vector", "tensor_scalar",
               out=self.cols[:, cb_q:cb_q + NC], in0=self.cols[:, cb_q:cb_q + NC],
               scalar1=0.125, scalar2=None, op0=ALU.mult,
               reads=[self.colsB], writes=[self.colsB])
        c_lo = self.col(2)

        self.ab.push()
        self.af.push()
        wsl_off, wslB = self.ab.take(3 * 1024, "l0_wslices")
        q1_off, q1B = self.ab.take(T, "Q1")
        q2_off, q2B = self.ab.take(T, "Q2")
        k1_off, k1B = self.ab.take(T, "K1")
        k2_off, k2B = self.ab.take(T, "K2")
        va_off, vaB = self.ab.take(16 * 130, "VA")
        pt_off, ptB = self.ab.take(2 * TB, "PT")
        u1_off, u1B = self.ab.take(1152, "U_h1")
        u2_off, u2B = self.ab.take(1152, "U_h2")
        w1_off, w1B = self.ab.take(1152, "U2_h1")
        w2_off, w2B = self.ab.take(1152, "U2_h2")
        st_off, stB = self.af.take(1024, "wstage")
        hk_off, hkB = self.af.take(512, "hankel")
        nr_off, nrB = self.af.take(1024, "near")
        vb_off, vbB = self.af.take(128, "vbias")
        c_m = self.col(4)

        def build_toeplitz(dram_row_t, h, base_idx, dst_off, dstB):
            for g0 in range(0, 1152, 512):
                w = min(512, 1152 - g0)
                b.emit("gpsimd", "dma_start",
                       out=self.ab.view(pt_off, TB)[:, 0:w],
                       in_=AP(dram_row_t.tensor, h * 1535 + base_idx + g0,
                              [[1, 128], [1, w]]),
                       reads=[ptB], writes=[ptB])
                b.emit("tensor", "matmul", self.ps_c[:, 0:w],
                       self.J_bf[:], self.ab.view(pt_off, TB)[:, 0:w],
                       start=True, stop=True,
                       reads=[ptB, self.constB], writes=[self.ps_cB])
                b.emit("scalar", "activation",
                       out=self.ab.view(dst_off + g0, w),
                       in_=self.ps_c[:, 0:w], func=AF.Copy,
                       reads=[self.ps_cB], writes=[dstB])

        b.emit("gpsimd", "memset", self.ab.view(k1_off, T, 64, 65), 1.0, writes=[k1B])
        b.emit("gpsimd", "memset", self.ab.view(k2_off, T, 64, 65), 1.0, writes=[k2B])
        for tt in range(16):
            b.emit("gpsimd", "memset",
                   self.ab.view(va_off + tt * 130 + 64, 1), 1.0, writes=[vaB])
            b.emit("gpsimd", "memset",
                   self.ab.view(va_off + tt * 130 + 129, 1), 1.0, writes=[vaB])

        for hp in range(8):
            h1, h2 = 2 * hp, 2 * hp + 1
            for wi, wdr in enumerate((wq, wk, wv)):
                for kc in range(NC):
                    b.emit("gpsimd", "dma_start",
                           out=self.ab.view(wsl_off + wi * 1024 + kc * 128, 128),
                           in_=wdr[kc * 128:(kc + 1) * 128, hp * 128:(hp + 1) * 128],
                           reads=[wslB], writes=[wslB])
            for (dst_off, dstB, head_half, wi, coff, scl) in (
                    (q1_off, q1B, 0, 0, cb_q, 0.125),
                    (q2_off, q2B, 1, 0, cb_q, 0.125),
                    (k1_off, k1B, 0, 1, cb_k, None),
                    (k2_off, k2B, 1, 1, cb_k, None)):
                for tb in range(NTB):
                    ps = self.ps_a if tb % 2 == 0 else self.ps_b
                    psB = self.ps_aB if tb % 2 == 0 else self.ps_bB
                    for kc in range(NC):
                        b.emit("tensor", "matmul", ps[:],
                               self.ab.view(wsl_off + wi * 1024 + kc * 128, 128),
                               self.XH[:, kc * T + tb * TB: kc * T + (tb + 1) * TB],
                               start=(kc == 0), stop=(kc == NC - 1),
                               reads=[wslB, self.XHBufs[kc]], writes=[psB])
                    bias_ap = self.cols[head_half * 64:(head_half + 1) * 64,
                                        coff + hp:coff + hp + 1]
                    if scl is not None:
                        b.emit("vector", "tensor_scalar",
                               out=self.ab.view(dst_off + tb * TB, TB, 0, 64),
                               in0=ps[head_half * 64:(head_half + 1) * 64, :],
                               scalar1=scl, scalar2=bias_ap,
                               op0=ALU.mult, op1=ALU.add,
                               reads=[psB, self.colsB], writes=[dstB])
                    else:
                        b.emit("vector", "tensor_scalar",
                               out=self.ab.view(dst_off + tb * TB, TB, 0, 64),
                               in0=ps[head_half * 64:(head_half + 1) * 64, :],
                               scalar1=bias_ap, scalar2=None, op0=ALU.add,
                               reads=[psB, self.colsB], writes=[dstB])
            b.emit("sync", "dma_start",
                   out=self.af.view(vb_off, 128),
                   in_=AP(bv_dram.tensor, hp * 128, [[0, 128], [1, 128]]),
                   reads=[vbB], writes=[vbB])
            def mk_vp(tt):
                def f():
                    ps = self.ps_a if tt % 2 == 0 else self.ps_b
                    psB = self.ps_aB if tt % 2 == 0 else self.ps_bB
                    for kc in range(NC):
                        b.emit("tensor", "matmul", ps[:, 0:128],
                               self.XH[:, kc * T + tt * 128: kc * T + (tt + 1) * 128],
                               self.ab.view(wsl_off + 2 * 1024 + kc * 128, 128),
                               start=(kc == 0), stop=(kc == NC - 1),
                               reads=[wslB, self.XHBufs[kc]], writes=[psB])
                    b.emit("vector", "tensor_add",
                           out=self.ab.view(va_off, 16 * 130)[:, tt * 130: tt * 130 + 64],
                           in0=ps[:, 0:64], in1=self.af.view(vb_off, 128)[:, 0:64],
                           reads=[psB, vbB], writes=[vaB])
                    b.emit("vector", "tensor_add",
                           out=self.ab.view(va_off, 16 * 130)[:, tt * 130 + 65: tt * 130 + 129],
                           in0=ps[:, 64:128], in1=self.af.view(vb_off, 128)[:, 64:128],
                           reads=[psB, vbB], writes=[vaB])
                return f
            build_toeplitz(ext_rev, h1, 128, u1_off, u1B)
            build_toeplitz(ext_rev, h2, 128, u2_off, u2B)
            build_toeplitz(ext, h1, 128, w1_off, w1B)
            build_toeplitz(ext, h2, 128, w2_off, w2B)
            for i, h in enumerate((h1, h2)):
                b.emit("sync", "dma_start",
                       out=self.cols[:, c_lo + i:c_lo + i + 1],
                       in_=AP(rel_raw.tensor, h * 257 + 0, [[0, 128], [1, 1]]),
                       reads=[self.colsB], writes=[self.colsB])
                b.emit("sync", "dma_start",
                       out=self.cols[:, c_m + i:c_m + i + 1],
                       in_=AP(rel_raw.tensor, h * 257 + 256, [[0, 128], [1, 1]]),
                       reads=[self.colsB], writes=[self.colsB])

            l0heads = ((q1_off, q1B, k1_off, k1B, u1_off, u1B, w1_off, w1B),
                       (q2_off, q2B, k2_off, k2B, u2_off, u2B, w2_off, w2B))

            def l0_stats_steps(hi):
                qoffh, qBh, koffh, kBh, uoff, uB, woff, wB = l0heads[hi]
                steps = []
                for qt in range(16):
                    def mk(qt=qt, qoffh=qoffh, qBh=qBh, koffh=koffh, kBh=kBh,
                           uoff=uoff, uB=uB, hi=hi):
                        for kb in range(NTB):
                            b.emit("tensor", "matmul",
                                   self.ps_big[:, kb * TB:(kb + 1) * TB],
                                   self.ab.view(qoffh + qt * 128, 128, 0, 64),
                                   self.ab.view(koffh + kb * TB, TB, 0, 64),
                                   start=True, stop=True,
                                   reads=[qBh, kBh], writes=[self.ps_bigQ[kb]])
                        near = [kb for kb in range(NTB)
                                if -128 <= 128 * qt - 512 * kb <= 512]
                        lows = [kb for kb in range(NTB) if 128 * qt - 512 * kb < -128]
                        highs = [kb for kb in range(NTB) if 128 * qt - 512 * kb > 512]
                        for j, kb in enumerate(near):
                            d1 = 128 * qt - 512 * kb
                            o_m = 512 - d1
                            b.emit("vector", "tensor_add",
                                   out=self.af.view(nr_off + j * TB, TB),
                                   in0=self.ps_big[:, kb * TB:(kb + 1) * TB],
                                   in1=self.ab.view(uoff + o_m, TB),
                                   reads=[self.ps_bigQ[kb], uB], writes=[nrB])
                        b.emit("vector", "reduce_max",
                               out=self.cols[:, c_m + 2:c_m + 3],
                               in_=self.af.view(nr_off, len(near) * TB),
                               axis=AX.X, reads=[nrB], writes=[self.colsB2])
                        for kbs, ccol in ((lows, c_lo + hi), (highs, c_m + hi)):
                            if not kbs:
                                continue
                            kb0, kb1 = min(kbs), max(kbs) + 1
                            b.emit("vector", "reduce_max",
                                   out=self.cols[:, c_m + 3:c_m + 4],
                                   in_=self.ps_big[:, kb0 * TB:kb1 * TB],
                                   axis=AX.X, reads=self.ps_bigQ[kb0:kb1],
                                   writes=[self.colsB2])
                            b.emit("vector", "tensor_add",
                                   out=self.cols[:, c_m + 3:c_m + 4],
                                   in0=self.cols[:, c_m + 3:c_m + 4],
                                   in1=self.cols[:, ccol:ccol + 1],
                                   reads=[self.colsB, self.colsB2],
                                   writes=[self.colsB2])
                            b.emit("vector", "tensor_max",
                                   out=self.cols[:, c_m + 2:c_m + 3],
                                   in0=self.cols[:, c_m + 2:c_m + 3],
                                   in1=self.cols[:, c_m + 3:c_m + 4],
                                   reads=[self.colsB2], writes=[self.colsB2])
                        b.emit("vector", "tensor_scalar",
                               out=self.wsmall[:, qt:qt + 1],
                               in0=self.cols[:, c_m + 2:c_m + 3],
                               scalar1=-1.0, scalar2=None, op0=ALU.mult,
                               reads=[self.colsB2], writes=[self.wsmallB])
                    steps.append(mk)
                return steps

            def l0_pass2_steps(hi):
                qoffh, qBh, koffh, kBh, uoff, uB, woff, wB = l0heads[hi]
                steps = []

                def mk_s(qb, kc):
                    def f():
                        d2 = 512 * qb - 128 * kc
                        ps = self.ps_a if kc % 2 == 0 else self.ps_b
                        psB = self.ps_aB if kc % 2 == 0 else self.ps_bB
                        b.emit("tensor", "matmul", ps[:],
                               self.ab.view(koffh + kc * 128, 128, 0, 65),
                               self.ab.view(qoffh + qb * TB, TB, 0, 65),
                               start=True, stop=True,
                               reads=[kBh, qBh], writes=[psB])
                        kw = {}
                        if d2 <= -639:
                            kw["bias"] = self.cols[:, c_lo + hi:c_lo + hi + 1]
                        elif d2 >= 255:
                            kw["bias"] = self.cols[:, c_m + hi:c_m + hi + 1]
                        else:
                            o2 = d2 + 512
                            b.emit("vector", "tensor_add",
                                   out=ps[:], in0=ps[:],
                                   in1=self.ab.view(woff + o2, TB),
                                   reads=[psB, wB], writes=[psB])
                        pslot = pt_off if kc % 2 == 0 else pt_off + TB
                        b.emit("scalar", "activation",
                               out=self.ab.view(pslot, TB),
                               in_=ps[:], func=AF.Exp,
                               reads=[psB, self.colsB], writes=[ptB], **kw)
                    return f

                def mk_av(qb, kc):
                    def f():
                        pslot = pt_off if kc % 2 == 0 else pt_off + TB
                        b.emit("tensor", "matmul", self.ps_c[0:65, :],
                               self.ab.view(va_off, 16 * 130)[:, kc * 130 + hi * 65: kc * 130 + hi * 65 + 65],
                               self.ab.view(pslot, TB),
                               start=(kc == 0), stop=(kc == 15),
                               reads=[vaB, ptB], writes=[self.ps_cB])
                    return f

                def mk_norm(qb):
                    def f():
                        b.emit("vector", "reciprocal",
                               out=self.srow1[0:1, 0:TB],
                               in_=self.ps_c[64:65, :],
                               reads=[self.ps_cB], writes=[self.srow1B])
                        self.bcast_row_fast(self.ps_d, self.ps_dB,
                                            self.srow1[0:1, 0:TB], self.srow1B, TB)
                        oc = hp
                        b.emit("scalar", "activation",
                               out=self.ab.view(pt_off, TB, 0, 64),
                               in_=self.ps_c[0:64, :], func=AF.Copy,
                               reads=[self.ps_cB], writes=[ptB])
                        b.emit("vector", "tensor_mul",
                               out=self.OF[hi * 64:(hi + 1) * 64,
                                           oc * T + qb * TB: oc * T + (qb + 1) * TB],
                               in0=self.ab.view(pt_off, TB, 0, 64),
                               in1=self.ps_d[0:64, :],
                               reads=[ptB, self.ps_dB], writes=[self.OFBufs[oc]])
                    return f

                for qb in range(NTB):
                    for kc in range(16):
                        steps.append(mk_s(qb, kc))
                        if kc > 0:
                            steps.append(mk_av(qb, kc - 1))
                    steps.append(mk_av(qb, 15))
                    steps.append(mk_norm(qb))
                return steps

            for f in _merge([mk_vp(tt) for tt in range(16)], l0_stats_steps(0)):
                f()
            self.flush_aug(q1_off, q1B)
            for f in _merge(l0_pass2_steps(0), l0_stats_steps(1)):
                f()
            self.flush_aug(q2_off, q2B)
            for f in l0_pass2_steps(1):
                f()
        self.ab.pop()
        self.af.pop()
        self.b.barrier()
        self.ab.push()
        self.af.push()
        st_off, stB = self.af.take(1024, "wstage2")
        wo_off, woB = self.ab.take(8 * 1024, "l0_wo")
        self.load_wt_bf16(wo, D, D, wo_off, woB, st_off, stB)
        for oc in range(NC):
            for tb in range(NTB):
                ps = self.ps_a if (oc + tb) % 2 == 0 else self.ps_b
                psB = self.ps_aB if (oc + tb) % 2 == 0 else self.ps_bB
                for kc in range(NC):
                    b.emit("tensor", "matmul", ps[:],
                           self.ab.view(wo_off + kc * 1024, 1024)[:, oc * 128:(oc + 1) * 128],
                           self.OF[:, kc * T + tb * TB: kc * T + (tb + 1) * TB],
                           start=(kc == 0), stop=(kc == NC - 1),
                           reads=[woB, self.OFBufs[kc]], writes=[psB])
                b.emit("vector", "scalar_tensor_tensor",
                       out=self.X[:, oc * T + tb * TB: oc * T + (tb + 1) * TB],
                       in0=ps[:], scalar=self.cols[:, cb_o + oc:cb_o + oc + 1],
                       in1=self.X[:, oc * T + tb * TB: oc * T + (tb + 1) * TB],
                       op0=ALU.add, op1=ALU.add,
                       reads=[psB, self.colsB, self.XBufs[oc]],
                       writes=[self.XBufs[oc]])
        self.ab.pop()
        self.af.pop()


    def l0_ffn(self):
        b = self.b
        self.layernorm()
        rails = self.din("l0f_rails", (4, 64, 16, 16))
        up_t = self.din("l0f_up_t", (4, 64, 16, 64))
        wc_t = self.din("l0f_wc_t", (16384, 1024))
        self.ab.push()
        self.af.push()
        OFall = self.b.buf("OFall")
        h_off, hB = self.ab.take(16 * TB, "l0f_h")
        b1_off, b1B = self.ab.take(4 * 128, "l0f_b1")
        b2_off, b2B = self.ab.take(16 * 128, "l0f_b2")
        y_off, yB = self.ab.take(4 * TB, "l0f_y")
        sg_off, sgB = self.ab.take(64, "l0f_stg_bf")
        st_off, stB = self.af.take(1024, "l0f_stage")
        r_off, rB = self.af.take(64, "l0f_railstage")

        for ig in range(8):
            h = ig // 2
            half = ig % 2
            for kc in range(16):
                b.emit("gpsimd", "dma_start",
                       out=self.OF[:, kc * 1024:(kc + 1) * 1024],
                       in_=wc_t[ig * 2048 + kc * 128: ig * 2048 + (kc + 1) * 128, :],
                       reads=[OFall], writes=[OFall])
            for i in range(4):
                yc = 4 * half + i
                b.emit("gpsimd", "memset", self.ab.view(b1_off + i * 128, 128),
                       0.0, writes=[b1B])
                b.emit("sync", "dma_start",
                       out=self.af.view(r_off, 16),
                       in_=AP(rails.tensor, (h * 64 + 8 * yc) * 256,
                              [[16, 128], [1, 16]]),
                       reads=[rB], writes=[rB])
                b.emit("scalar", "copy",
                       out=self.ab.view(sg_off, 64)[:, 0:16],
                       in_=self.af.view(r_off, 64)[:, 0:16],
                       reads=[rB], writes=[sgB])
                for rr in range(8):
                    b.emit("gpsimd", "dma_start",
                           out=self.ab.view(b1_off + i * 128 + rr * 16, 16,
                                            rr * 16, rr * 16 + 16),
                           in_=self.ab.view(sg_off, 16, rr * 16, rr * 16 + 16),
                           reads=[sgB], writes=[b1B])
            for zi in range(16):
                zc = ig * 16 + zi
                oc_h = zc % 32
                r0 = 2 * oc_h
                b.emit("gpsimd", "memset", self.ab.view(b2_off + zi * 128, 128),
                       0.0, writes=[b2B])
                b.emit("sync", "dma_start",
                       out=self.af.view(r_off, 64)[0:32, :],
                       in_=AP(up_t.tensor, (h * 64 + r0) * 1024,
                              [[64, 32], [1, 64]]),
                       reads=[rB], writes=[rB])
                b.emit("scalar", "copy",
                       out=self.ab.view(sg_off, 64)[0:32, :],
                       in_=self.af.view(r_off, 64)[0:32, :],
                       reads=[rB], writes=[sgB])
                for rr in range(2):
                    b.emit("gpsimd", "dma_start",
                           out=self.ab.view(b2_off + zi * 128 + rr * 64, 64,
                                            ((r0 + rr) % 8) * 16,
                                            ((r0 + rr) % 8) * 16 + 16),
                           in_=self.ab.view(sg_off, 64, rr * 16, rr * 16 + 16),
                           reads=[sgB], writes=[b2B])
            for tb in range(NTB):
                for i in range(4):
                    yc = 4 * half + i
                    psy, psyB = ((self.ps_a, self.ps_aB) if i % 2 == 0
                                 else (self.ps_c, self.ps_cB))
                    b.emit("tensor", "matmul", psy[:],
                           self.ab.view(b1_off + i * 128, 128),
                           self.XH[:, yc * T + tb * TB: yc * T + (tb + 1) * TB],
                           start=True, stop=True,
                           reads=[b1B, self.XHBufs[yc]], writes=[psyB])
                    b.emit("scalar", "copy",
                           out=self.ab.view(y_off + i * TB, TB),
                           in_=psy[:], reads=[psyB], writes=[yB])
                for zi in range(16):
                    zc = ig * 16 + zi
                    i = (zc % 32) // 4 - 4 * half
                    psz, pszB = ((self.ps_b, self.ps_bB) if zi % 2 == 0
                                 else (self.ps_d, self.ps_dB))
                    b.emit("tensor", "matmul", psz[:],
                           self.ab.view(b2_off + zi * 128, 128),
                           self.ab.view(y_off + i * TB, TB),
                           start=True, stop=True,
                           reads=[b2B, yB], writes=[pszB])
                    b.emit("scalar", "activation",
                           out=self.ab.view(h_off + zi * TB, TB),
                           in_=psz[:], func=AF.Gelu,
                           reads=[pszB], writes=[hB])
                for oc in range(8):
                    ps = self.ps_c if oc % 2 == 0 else self.ps_d
                    psB = self.ps_cB if oc % 2 == 0 else self.ps_dB
                    for zi in range(16):
                        b.emit("tensor", "matmul", ps[:],
                               self.OF[:, zi * 1024 + oc * 128: zi * 1024 + (oc + 1) * 128],
                               self.ab.view(h_off + zi * TB, TB),
                               start=(zi == 0), stop=(zi == 15),
                               reads=[OFall, hB], writes=[psB])
                    b.emit("vector", "tensor_add",
                           out=self.X[:, oc * T + tb * TB: oc * T + (tb + 1) * TB],
                           in0=self.X[:, oc * T + tb * TB: oc * T + (tb + 1) * TB],
                           in1=ps[:],
                           reads=[self.XBufs[oc], psB], writes=[self.XBufs[oc]])
        self.ab.pop()
        self.af.pop()

    def l1_attn(self):
        b = self.b
        self.layernorm()
        self.layernorm_of_xh()
        wqkv = self.din("l1_w_qkv", (D, 3 * D))
        wout = self.din("l1_w_out", (D, D))
        cb_o = self.col(NC); self.load_cols("l1_b_out", (NC, 128), cb_o, NC)

        self.ab.push()
        self.af.push()
        wsl_off, wslB = self.ab.take(3 * 1024, "l1_wsl")
        q1_off, q1B = self.ab.take(T, "Q1")
        q2_off, q2B = self.ab.take(T, "Q2")
        k1_off, k1B = self.ab.take(T, "K1")
        k2_off, k2B = self.ab.take(T, "K2")
        va_off, vaB = self.ab.take(16 * 130, "VA")
        pt_off, ptB = self.ab.take(2 * TB, "PT")
        st_off, stB = self.af.take(1024, "wstage")
        c_m = self.col(2)

        b.emit("gpsimd", "memset", self.ab.view(k1_off, T, 64, 65), 1.0, writes=[k1B])
        b.emit("gpsimd", "memset", self.ab.view(k2_off, T, 64, 65), 1.0, writes=[k2B])
        for tt in range(16):
            b.emit("gpsimd", "memset",
                   self.ab.view(va_off + tt * 130 + 64, 1), 1.0, writes=[vaB])
            b.emit("gpsimd", "memset",
                   self.ab.view(va_off + tt * 130 + 129, 1), 1.0, writes=[vaB])

        for hp in range(8):
            for wi, colbase in enumerate((hp * 128, D + hp * 128, 2 * D + hp * 128)):
                for kc in range(NC):
                    b.emit("gpsimd", "dma_start",
                           out=self.ab.view(wsl_off + wi * 1024 + kc * 128, 128),
                           in_=wqkv[kc * 128:(kc + 1) * 128, colbase:colbase + 128],
                           reads=[wslB], writes=[wslB])
            for (dst_off, dstB, head_half, wi, scl) in (
                    (q1_off, q1B, 0, 0, 0.125), (q2_off, q2B, 1, 0, 0.125),
                    (k1_off, k1B, 0, 1, None), (k2_off, k2B, 1, 1, None)):
                for tb in range(NTB):
                    ps = self.ps_a if tb % 2 == 0 else self.ps_b
                    psB = self.ps_aB if tb % 2 == 0 else self.ps_bB
                    for kc in range(NC):
                        b.emit("tensor", "matmul", ps[:],
                               self.ab.view(wsl_off + wi * 1024 + kc * 128, 128),
                               self.XH[:, kc * T + tb * TB: kc * T + (tb + 1) * TB],
                               start=(kc == 0), stop=(kc == NC - 1),
                               reads=[wslB, self.XHBufs[kc]], writes=[psB])
                    if scl is not None:
                        b.emit("vector", "tensor_scalar",
                               out=self.ab.view(dst_off + tb * TB, TB, 0, 64),
                               in0=ps[head_half * 64:(head_half + 1) * 64, :],
                               scalar1=scl, scalar2=None, op0=ALU.mult,
                               reads=[psB], writes=[dstB])
                    else:
                        b.emit("scalar", "copy",
                               out=self.ab.view(dst_off + tb * TB, TB, 0, 64),
                               in_=ps[head_half * 64:(head_half + 1) * 64, :],
                               reads=[psB], writes=[dstB])
            def mk_vp(tt):
                def f():
                    ps = self.ps_a if tt % 2 == 0 else self.ps_b
                    psB = self.ps_aB if tt % 2 == 0 else self.ps_bB
                    for kc in range(NC):
                        b.emit("tensor", "matmul", ps[:, 0:128],
                               self.XH[:, kc * T + tt * 128: kc * T + (tt + 1) * 128],
                               self.ab.view(wsl_off + 2 * 1024 + kc * 128, 128),
                               start=(kc == 0), stop=(kc == NC - 1),
                               reads=[wslB, self.XHBufs[kc]], writes=[psB])
                    b.emit("scalar", "copy",
                           out=self.ab.view(va_off, 16 * 130)[:, tt * 130: tt * 130 + 64],
                           in_=ps[:, 0:64], reads=[psB], writes=[vaB])
                    b.emit("scalar", "copy",
                           out=self.ab.view(va_off, 16 * 130)[:, tt * 130 + 65: tt * 130 + 129],
                           in_=ps[:, 64:128], reads=[psB], writes=[vaB])
                return f
            heads = ((q1_off, q1B, k1_off, k1B), (q2_off, q2B, k2_off, k2B))

            def l1_stats_steps(hi):
                qoffh, qBh, koffh, kBh = heads[hi]
                steps = []
                for qt in range(16):
                    def mk(qt=qt, qoffh=qoffh, qBh=qBh, koffh=koffh, kBh=kBh):
                        wq_allow = (qt // 2 + 1) * 256
                        nkb = (wq_allow + TB - 1) // TB
                        for kb in range(nkb):
                            w = min(TB, wq_allow - kb * TB)
                            b.emit("tensor", "matmul",
                                   self.ps_big[:, kb * TB: kb * TB + w],
                                   self.ab.view(qoffh + qt * 128, 128, 0, 64),
                                   self.ab.view(koffh + kb * TB, w, 0, 64),
                                   start=True, stop=True,
                                   reads=[qBh, kBh], writes=[self.ps_bigQ[kb]])
                        b.emit("vector", "reduce_max",
                               out=self.wsmall[:, qt:qt + 1],
                               in_=self.ps_big[:, 0:wq_allow],
                               axis=AX.X, negate=True,
                               reads=self.ps_bigQ[0:nkb], writes=[self.wsmallB])
                    steps.append(mk)
                return steps

            def l1_pass2_steps(hi):
                qoffh, qBh, koffh, kBh = heads[hi]
                steps = []

                def mk_s(qb, kc):
                    def f():
                        half_chunk = (kc >= 4 * qb + 2)
                        ps = self.ps_a if kc % 2 == 0 else self.ps_b
                        psB = self.ps_aB if kc % 2 == 0 else self.ps_bB
                        pslot = pt_off if kc % 2 == 0 else pt_off + TB
                        if half_chunk:
                            b.emit("gpsimd", "memset",
                                   self.ab.view(pslot, 256), 0.0, writes=[ptB])
                            b.emit("tensor", "matmul", ps[:, 0:256],
                                   self.ab.view(koffh + kc * 128, 128, 0, 65),
                                   self.ab.view(qoffh + qb * TB + 256, 256, 0, 65),
                                   start=True, stop=True,
                                   reads=[kBh, qBh], writes=[psB])
                            b.emit("scalar", "activation",
                                   out=self.ab.view(pslot + 256, 256),
                                   in_=ps[:, 0:256], func=AF.Exp,
                                   reads=[psB], writes=[ptB])
                        else:
                            b.emit("tensor", "matmul", ps[:],
                                   self.ab.view(koffh + kc * 128, 128, 0, 65),
                                   self.ab.view(qoffh + qb * TB, TB, 0, 65),
                                   start=True, stop=True,
                                   reads=[kBh, qBh], writes=[psB])
                            b.emit("scalar", "activation",
                                   out=self.ab.view(pslot, TB),
                                   in_=ps[:], func=AF.Exp,
                                   reads=[psB], writes=[ptB])
                    return f

                def mk_av(qb, kc, nkc):
                    def f():
                        pslot = pt_off if kc % 2 == 0 else pt_off + TB
                        b.emit("tensor", "matmul", self.ps_c[0:65, :],
                               self.ab.view(va_off, 16 * 130)[:, kc * 130 + hi * 65: kc * 130 + hi * 65 + 65],
                               self.ab.view(pslot, TB),
                               start=(kc == 0), stop=(kc == nkc - 1),
                               reads=[vaB, ptB], writes=[self.ps_cB])
                    return f

                def mk_norm(qb):
                    def f():
                        b.emit("vector", "reciprocal",
                               out=self.srow1[0:1, 0:TB], in_=self.ps_c[64:65, :],
                               reads=[self.ps_cB], writes=[self.srow1B])
                        self.bcast_row_fast(self.ps_d, self.ps_dB,
                                            self.srow1[0:1, 0:TB], self.srow1B, TB)
                        b.emit("scalar", "activation",
                               out=self.ab.view(pt_off, TB, 0, 64),
                               in_=self.ps_c[0:64, :], func=AF.Copy,
                               reads=[self.ps_cB], writes=[ptB])
                        b.emit("vector", "tensor_mul",
                               out=self.OF[hi * 64:(hi + 1) * 64,
                                           hp * T + qb * TB: hp * T + (qb + 1) * TB],
                               in0=self.ab.view(pt_off, TB, 0, 64),
                               in1=self.ps_d[0:64, :],
                               reads=[ptB, self.ps_dB], writes=[self.OFBufs[hp]])
                    return f

                for qb in range(NTB):
                    nkc = 4 * qb + 4
                    for kc in range(nkc):
                        steps.append(mk_s(qb, kc))
                        if kc > 0:
                            steps.append(mk_av(qb, kc - 1, nkc))
                    steps.append(mk_av(qb, nkc - 1, nkc))
                    steps.append(mk_norm(qb))
                return steps

            for f in _merge([mk_vp(tt) for tt in range(16)], l1_stats_steps(0)):
                f()
            self.flush_aug(q1_off, q1B)
            for f in _merge(l1_pass2_steps(0), l1_stats_steps(1)):
                f()
            self.flush_aug(q2_off, q2B)
            for f in l1_pass2_steps(1):
                f()
        self.ab.pop()
        self.af.pop()
        self.b.barrier()
        self.ab.push()
        self.af.push()
        st_off, stB = self.af.take(1024, "wstage2")
        wo_off, woB = self.ab.take(8 * 1024, "l1_wo")
        self.load_wt_bf16(wout, D, D, wo_off, woB, st_off, stB)
        for oc in range(NC):
            for tb in range(NTB):
                ps = self.ps_a if (oc + tb) % 2 == 0 else self.ps_b
                psB = self.ps_aB if (oc + tb) % 2 == 0 else self.ps_bB
                for kc in range(NC):
                    b.emit("tensor", "matmul", ps[:],
                           self.ab.view(wo_off + kc * 1024, 1024)[:, oc * 128:(oc + 1) * 128],
                           self.OF[:, kc * T + tb * TB: kc * T + (tb + 1) * TB],
                           start=(kc == 0), stop=(kc == NC - 1),
                           reads=[woB, self.OFBufs[kc]], writes=[psB])
                b.emit("vector", "scalar_tensor_tensor",
                       out=self.X[:, oc * T + tb * TB: oc * T + (tb + 1) * TB],
                       in0=ps[:], scalar=self.cols[:, cb_o + oc:cb_o + oc + 1],
                       in1=self.X[:, oc * T + tb * TB: oc * T + (tb + 1) * TB],
                       op0=ALU.add, op1=ALU.add,
                       reads=[psB, self.colsB, self.XBufs[oc]],
                       writes=[self.XBufs[oc]])
        self.ab.pop()
        self.af.pop()

    def layernorm_of_xh(self):
        cg = self.col(NC); self.load_cols("l1_g", (NC, 128), cg, NC)
        cbb = self.col(NC); self.load_cols("l1_b", (NC, 128), cbb, NC)
        self.ln_inner(self.XH, self.XHBufs, cg, cbb)

    def ln_inner(self, dst_t, dst_bufs, cg=None, cbb=None):
        b = self.b
        self.af.push()
        self.ab.push()
        sqoff, sqB = self.ab.take(TB, "ln2_sq")
        for tb in range(NTB):
            for c in range(NC):
                b.emit("scalar", "activation",
                       out=self.ab.view(sqoff, TB),
                       in_=self.XH[:, c * T + tb * TB: c * T + (tb + 1) * TB],
                       func=AF.Square,
                       reads=[self.XHBufs[c]], writes=[sqB])
                b.emit("tensor", "matmul", self.ps_c[0:1, :],
                       self.ones_col_bf[:],
                       self.XH[:, c * T + tb * TB: c * T + (tb + 1) * TB],
                       start=(c == 0), stop=(c == NC - 1),
                       reads=[self.XHBufs[c], self.constB], writes=[self.ps_cB])
                b.emit("tensor", "matmul", self.ps_d[0:1, :],
                       self.ones_col_bf[:], self.ab.view(sqoff, TB),
                       start=(c == 0), stop=(c == NC - 1),
                       reads=[sqB, self.constB], writes=[self.ps_dB])
            mr = self.m_row[0:1, :]
            rr = self.r_row[0:1, :]
            s1 = self.srow1[0:1, :]
            s2 = self.srow2[0:1, :]
            b.emit("vector", "tensor_scalar", out=mr, in0=self.ps_c[0:1, :],
                   scalar1=1.0 / D, scalar2=None, op0=ALU.mult,
                   reads=[self.ps_cB], writes=[self.m_rowB])
            b.emit("vector", "tensor_scalar", out=s1, in0=self.ps_d[0:1, :],
                   scalar1=1.0 / D, scalar2=None, op0=ALU.mult,
                   reads=[self.ps_dB], writes=[self.srow1B])
            b.emit("vector", "tensor_mul", out=s2, in0=mr, in1=mr,
                   reads=[self.m_rowB], writes=[self.srow2B])
            b.emit("vector", "tensor_sub", out=s1, in0=s1, in1=s2,
                   reads=[self.srow1B, self.srow2B], writes=[self.srow1B])
            b.emit("scalar", "activation", out=s1, in_=s1, func=AF.Sqrt,
                   bias=self.cols[0:1, self.c_eps:self.c_eps + 1],
                   reads=[self.srow1B, self.colsB], writes=[self.srow1B])
            b.emit("vector", "reciprocal", out=rr, in_=s1,
                   reads=[self.srow1B], writes=[self.r_rowB])
            self.bcast_row(self.ps_c, self.ps_cB, mr, self.m_rowB, TB)
            self.bcast_row(self.ps_d, self.ps_dB, rr, self.r_rowB, TB)
            for c in range(NC):
                sl = slice(c * T + tb * TB, c * T + (tb + 1) * TB)
                b.emit("vector", "tensor_sub",
                       out=self.ab.view(sqoff, TB),
                       in0=self.XH[:, sl], in1=self.ps_c[:],
                       reads=[self.XHBufs[c], self.ps_cB], writes=[sqB])
                b.emit("vector", "tensor_mul",
                       out=dst_t[:, sl],
                       in0=self.ab.view(sqoff, TB), in1=self.ps_d[:],
                       reads=[sqB, self.ps_dB], writes=[dst_bufs[c]])
                if cg is not None:
                    if cbb is not None:
                        b.emit("vector", "tensor_scalar",
                               out=dst_t[:, sl], in0=dst_t[:, sl],
                               scalar1=self.cols[:, cg + c:cg + c + 1],
                               scalar2=self.cols[:, cbb + c:cbb + c + 1],
                               op0=ALU.mult, op1=ALU.add,
                               reads=[dst_bufs[c], self.colsB], writes=[dst_bufs[c]])
                    else:
                        b.emit("vector", "tensor_scalar",
                               out=dst_t[:, sl], in0=dst_t[:, sl],
                               scalar1=self.cols[:, cg + c:cg + c + 1],
                               scalar2=None, op0=ALU.mult,
                               reads=[dst_bufs[c], self.colsB], writes=[dst_bufs[c]])
        self.af.pop()
        self.ab.pop()
        self.b.barrier()

    def l1_ffn(self):
        b = self.b
        self.layernorm()
        wr_t = self.din("l1f_wr_t", (D, D))
        wk_t = self.din("l1f_wk_t", (D, D))
        wv_t = self.din("l1f_wv_t", (D, D))
        cmu_r = self.col(NC); self.load_cols("l1f_mu_r", (NC, 128), cmu_r, NC)
        cmu_k = self.col(NC); self.load_cols("l1f_mu_k", (NC, 128), cmu_k, NC)
        c1m_r = self.col(NC)
        c1m_k = self.col(NC)
        for src, dst in ((cmu_r, c1m_r), (cmu_k, c1m_k)):
            b.emit("vector", "tensor_scalar",
                   out=self.cols[:, dst:dst + NC], in0=self.cols[:, src:src + NC],
                   scalar1=-1.0, scalar2=1.0, op0=ALU.mult, op1=ALU.add,
                   reads=[self.colsB], writes=[self.colsB])
        self.ab.push()
        self.af.push()
        xr_off, xrB = self.ab.take(NC * T, "l1f_xr")
        wsl_off, wslB = self.ab.take(2048, "l1f_wsl")
        st_off, stB = self.af.take(1024, "wstage")
        tm_off, tmB = self.af.take(T, "shift_tmp")
        OFall = self.b.buf("l1f_OFall")

        def token_shift(dst_off, dstB, cmu, c1m):
            for c in range(NC):
                b.emit("vector", "tensor_scalar",
                       out=self.af.view(tm_off, T),
                       in0=self.XH[:, c * T:(c + 1) * T],
                       scalar1=self.cols[:, cmu + c:cmu + c + 1], scalar2=None,
                       op0=ALU.mult,
                       reads=[self.XHBufs[c], self.colsB], writes=[tmB])
                b.emit("vector", "scalar_tensor_tensor",
                       out=self.ab.view(dst_off + c * T + 1, T - 1),
                       in0=self.XH[:, c * T: c * T + T - 1],
                       scalar=self.cols[:, c1m + c:c1m + c + 1],
                       in1=self.af.view(tm_off + 1, T - 1),
                       op0=ALU.mult, op1=ALU.add,
                       reads=[self.XHBufs[c], self.colsB, tmB], writes=[dstB])
                b.emit("vector", "tensor_copy",
                       out=self.ab.view(dst_off + c * T, 1),
                       in_=self.af.view(tm_off, 1),
                       reads=[tmB], writes=[dstB])

        def proj_from(in_off, inB, wdr, dst_t, dst_wr, dst_rd, act):
            for oc in range(NC):
                dbuf = oc % 2
                for kc in range(NC):
                    b.emit("sync", "dma_start",
                           out=self.af.view(st_off, 1024)[:, dbuf * 128:(dbuf + 1) * 128],
                           in_=wdr[kc * 128:(kc + 1) * 128, oc * 128:(oc + 1) * 128],
                           reads=[stB], writes=[stB])
                    b.emit("gpsimd", "tensor_copy",
                           out=self.ab.view(wsl_off + dbuf * 1024 + kc * 128, 128),
                           in_=self.af.view(st_off, 1024)[:, dbuf * 128:(dbuf + 1) * 128],
                           reads=[stB], writes=[wslB])
                for tb in range(NTB):
                    ps = self.ps_a if tb % 2 == 0 else self.ps_b
                    psB = self.ps_aB if tb % 2 == 0 else self.ps_bB
                    for kc in range(NC):
                        b.emit("tensor", "matmul", ps[:],
                               self.ab.view(wsl_off + dbuf * 1024 + kc * 128, 128),
                               self.ab.view(in_off + kc * T + tb * TB, TB),
                               start=(kc == 0), stop=(kc == NC - 1),
                               reads=[wslB, inB], writes=[psB])
                    b.emit("scalar", "activation",
                           out=dst_t[:, oc * T + tb * TB: oc * T + (tb + 1) * TB],
                           in_=ps[:], func=(AF.Sigmoid if act == "sig" else AF.Relu),
                           reads=[psB] + dst_rd, writes=[dst_wr[oc]])

        token_shift(xr_off, xrB, cmu_r, c1m_r)
        proj_from(xr_off, xrB, wr_t, self.OF, [OFall] * NC, [], "sig")
        token_shift(xr_off, xrB, cmu_k, c1m_k)
        proj_from(xr_off, xrB, wk_t, self.XH, self.XHBufs, [], "relu")
        for c in range(NC):
            sl = slice(c * T, (c + 1) * T)
            b.emit("scalar", "activation",
                   out=self.XH[:, sl], in_=self.XH[:, sl], func=AF.Square,
                   reads=[self.XHBufs[c]], writes=[self.XHBufs[c]])
        for oc in range(NC):
            dbuf = oc % 2
            for kc in range(NC):
                b.emit("gpsimd", "dma_start",
                       out=self.ab.view(wsl_off + dbuf * 1024 + kc * 128, 128),
                       in_=wv_t[kc * 128:(kc + 1) * 128, oc * 128:(oc + 1) * 128],
                       reads=[wslB], writes=[wslB])
            for tb in range(NTB):
                ps = self.ps_a if tb % 2 == 0 else self.ps_b
                psB = self.ps_aB if tb % 2 == 0 else self.ps_bB
                for kc in range(NC):
                    b.emit("tensor", "matmul", ps[:],
                           self.ab.view(wsl_off + dbuf * 1024 + kc * 128, 128),
                           self.XH[:, kc * T + tb * TB: kc * T + (tb + 1) * TB],
                           start=(kc == 0), stop=(kc == NC - 1),
                           reads=[wslB, self.XHBufs[kc]], writes=[psB])
                b.emit("vector", "tensor_mul",
                       out=self.af.view(tm_off, TB),
                       in0=ps[:],
                       in1=self.OF[:, oc * T + tb * TB: oc * T + (tb + 1) * TB],
                       reads=[psB, OFall], writes=[tmB])
                b.emit("vector", "tensor_add",
                       out=self.X[:, oc * T + tb * TB: oc * T + (tb + 1) * TB],
                       in0=self.X[:, oc * T + tb * TB: oc * T + (tb + 1) * TB],
                       in1=self.af.view(tm_off, TB),
                       reads=[self.XBufs[oc], tmB], writes=[self.XBufs[oc]])
        self.ab.pop()
        self.af.pop()

    def _rope64(self, dst_off, dstB, cs_off, sn_off, tabB, sc_off, scB, scale):
        b = self.b
        b.emit("vector", "tensor_mul",
               out=self.ab.view(sc_off, T, 0, 64),
               in0=self.ab.view(dst_off, T, 0, 64),
               in1=self.af.view(cs_off, T, 0, 64),
               reads=[dstB, tabB], writes=[scB])
        b.emit("vector", "tensor_mul",
               out=self.ab.view(sc_off + T, T, 0, 32),
               in0=self.ab.view(dst_off, T, 32, 64),
               in1=self.af.view(sn_off, T, 32, 64),
               reads=[dstB, tabB], writes=[scB])
        b.emit("vector", "tensor_mul",
               out=self.ab.view(sc_off + T, T, 32, 64),
               in0=self.ab.view(dst_off, T, 0, 32),
               in1=self.af.view(sn_off, T, 0, 32),
               reads=[dstB, tabB], writes=[scB])
        b.emit("vector", "tensor_sub",
               out=self.ab.view(dst_off, T, 0, 32),
               in0=self.ab.view(sc_off, T, 0, 32),
               in1=self.ab.view(sc_off + T, T, 0, 32),
               reads=[scB], writes=[dstB])
        b.emit("vector", "tensor_add",
               out=self.ab.view(dst_off, T, 32, 64),
               in0=self.ab.view(sc_off, T, 32, 64),
               in1=self.ab.view(sc_off + T, T, 32, 64),
               reads=[scB], writes=[dstB])
        if scale is not None:
            b.emit("vector", "tensor_scalar",
                   out=self.ab.view(dst_off, T, 0, 64),
                   in0=self.ab.view(dst_off, T, 0, 64),
                   scalar1=scale, scalar2=None, op0=ALU.mult,
                   reads=[dstB], writes=[dstB])

    def _attn_causal_core(self, hp, q1_off, q1B, q2_off, q2B, k1_off, k1B,
                          k2_off, k2B, va_off, vaB, pt_off, ptB,
                          mtri_off, mtrit_off, mtB, c_m, of_base=None,
                          pre_steps=None):
        b = self.b
        heads = ((q1_off, q1B, k1_off, k1B), (q2_off, q2B, k2_off, k2B))

        def stats_steps(hi):
            qoffh, qBh, koffh, kBh = heads[hi]
            steps = []
            for qt in range(16):
                def mk(qt=qt, qoffh=qoffh, qBh=qBh, koffh=koffh, kBh=kBh):
                    w_allow = 128 * qt + 128
                    nkb = (w_allow + TB - 1) // TB
                    for kb in range(nkb):
                        w = min(TB, w_allow - kb * TB)
                        b.emit("tensor", "matmul",
                               self.ps_big[:, kb * TB: kb * TB + w],
                               self.ab.view(qoffh + qt * 128, 128, 0, 64),
                               self.ab.view(koffh + kb * TB, w, 0, 64),
                               start=True, stop=True,
                               reads=[qBh, kBh], writes=[self.ps_bigQ[kb]])
                    b.emit("vector", "tensor_add",
                           out=self.ps_big[:, w_allow - 128:w_allow],
                           in0=self.ps_big[:, w_allow - 128:w_allow],
                           in1=self.af.view(mtrit_off, 128),
                           reads=[self.ps_bigQ[nkb - 1], mtB],
                           writes=[self.ps_bigQ[nkb - 1]])
                    b.emit("vector", "reduce_max",
                           out=self.wsmall[:, qt:qt + 1],
                           in_=self.ps_big[:, 0:w_allow],
                           axis=AX.X, negate=True,
                           reads=self.ps_bigQ[0:nkb], writes=[self.wsmallB])
                steps.append(mk)
            return steps

        def pass2_steps(hi):
            qoffh, qBh, koffh, kBh = heads[hi]
            steps = []

            def mk_s(qb, kc):
                def f():
                    diag = kc - 4 * qb
                    ps = self.ps_a if kc % 2 == 0 else self.ps_b
                    psB = self.ps_aB if kc % 2 == 0 else self.ps_bB
                    pslot = pt_off if kc % 2 == 0 else pt_off + TB
                    if diag < 0:
                        b.emit("tensor", "matmul", ps[:],
                               self.ab.view(koffh + kc * 128, 128, 0, 65),
                               self.ab.view(qoffh + qb * TB, TB, 0, 65),
                               start=True, stop=True,
                               reads=[kBh, qBh], writes=[psB])
                        b.emit("scalar", "activation",
                               out=self.ab.view(pslot, TB),
                               in_=ps[:], func=AF.Exp,
                               reads=[psB], writes=[ptB])
                    else:
                        u0 = 128 * diag
                        if u0 > 0:
                            b.emit("gpsimd", "memset",
                                   self.ab.view(pslot, u0), 0.0, writes=[ptB])
                        b.emit("tensor", "matmul", ps[:, u0:TB],
                               self.ab.view(koffh + kc * 128, 128, 0, 65),
                               self.ab.view(qoffh + qb * TB + u0, TB - u0, 0, 65),
                               start=True, stop=True,
                               reads=[kBh, qBh], writes=[psB])
                        b.emit("vector", "tensor_add",
                               out=ps[:, u0:u0 + 128],
                               in0=ps[:, u0:u0 + 128],
                               in1=self.af.view(mtri_off, 128),
                               reads=[psB, mtB], writes=[psB])
                        b.emit("scalar", "activation",
                               out=self.ab.view(pslot + u0, TB - u0),
                               in_=ps[:, u0:TB], func=AF.Exp,
                               reads=[psB], writes=[ptB])
                return f

            def mk_av(qb, kc, nkc):
                def f():
                    pslot = pt_off if kc % 2 == 0 else pt_off + TB
                    b.emit("tensor", "matmul", self.ps_c[0:65, :],
                           self.ab.view(va_off, 16 * 130)[:, kc * 130 + hi * 65: kc * 130 + hi * 65 + 65],
                           self.ab.view(pslot, TB),
                           start=(kc == 0), stop=(kc == nkc - 1),
                           reads=[vaB, ptB], writes=[self.ps_cB])
                return f

            def mk_norm(qb):
                def f():
                    b.emit("vector", "reciprocal",
                           out=self.srow1[0:1, 0:TB], in_=self.ps_c[64:65, :],
                           reads=[self.ps_cB], writes=[self.srow1B])
                    self.bcast_row_fast(self.ps_d, self.ps_dB,
                                        self.srow1[0:1, 0:TB], self.srow1B, TB)
                    b.emit("scalar", "activation",
                           out=self.ab.view(pt_off, TB, 0, 64),
                           in_=self.ps_c[0:64, :], func=AF.Copy,
                           reads=[self.ps_cB], writes=[ptB])
                    b.emit("vector", "tensor_mul",
                           out=self.OF[hi * 64:(hi + 1) * 64,
                                       hp * T + qb * TB: hp * T + (qb + 1) * TB],
                           in0=self.ab.view(pt_off, TB, 0, 64),
                           in1=self.ps_d[0:64, :],
                           reads=[ptB, self.ps_dB], writes=[self.OFBufs[hp]])
                return f

            for qb in range(NTB):
                nkc = 4 * qb + 4
                for kc in range(nkc):
                    steps.append(mk_s(qb, kc))
                    if kc > 0:
                        steps.append(mk_av(qb, kc - 1, nkc))
                steps.append(mk_av(qb, nkc - 1, nkc))
                steps.append(mk_norm(qb))
            return steps

        def merge(primary, filler):
            out = []
            si = 0
            n = max(1, len(primary))
            for i, f in enumerate(primary):
                out.append(f)
                want = (i + 1) * len(filler) // n
                while si < want:
                    out.append(filler[si])
                    si += 1
            out.extend(filler[si:])
            return out

        s0 = stats_steps(0)
        if pre_steps:
            for f in merge(list(pre_steps), s0):
                f()
        else:
            for f in s0:
                f()
        self.flush_aug(q1_off, q1B)
        for f in merge(pass2_steps(0), stats_steps(1)):
            f()
        self.flush_aug(q2_off, q2B)
        for f in pass2_steps(1):
            f()

    def l2_attn(self):
        b = self.b
        self.layernorm()
        wq = self.din("l2_wq", (D, D))
        wkv = self.din("l2_wkv", (D, 2 * D))
        wo = self.din("l2_wo", (D, D))
        cb_q = self.col(NC); self.load_cols("l2_bq", (NC, 128), cb_q, NC)
        cb_k = self.col(NC); self.load_cols("l2_bk", (NC, 128), cb_k, NC)
        cb_o = self.col(NC); self.load_cols("l2_bo", (NC, 128), cb_o, NC)
        bv_dram = self.din("l2_bv", (D,))
        cos_dr = self.din("l2_cos", (64, T))
        sin_dr = self.din("l2_sin", (64, T))
        mtri_dr = self.din("l2_mtri", (128, 128))
        mtrit_dr = self.din("l2_mtri_t", (128, 128))
        b.emit("vector", "tensor_scalar",
               out=self.cols[:, cb_q:cb_q + NC], in0=self.cols[:, cb_q:cb_q + NC],
               scalar1=0.125, scalar2=None, op0=ALU.mult,
               reads=[self.colsB], writes=[self.colsB])

        self.ab.push()
        self.af.push()
        wsl_off, wslB = self.ab.take(3 * 1024, "l2_wsl")
        q1_off, q1B = self.ab.take(T, "Q1")
        q2_off, q2B = self.ab.take(T, "Q2")
        k1_off, k1B = self.ab.take(T, "K1")
        k2_off, k2B = self.ab.take(T, "K2")
        va_off, vaB = self.ab.take(16 * 130, "VA")
        pt_off, ptB = self.ab.take(2 * TB, "PT")
        rs_off, rsB = self.ab.take(2 * T, "rope_scr")
        st_off, stB = self.af.take(1024, "wstage")
        cs_off, tabB = self.af.take(T, "cos")
        sn_off, _tab2 = self.af.take(T, "sin")
        mtri_off, mtB = self.af.take(128, "mtri")
        mtrit_off, _mt2 = self.af.take(128, "mtrit")
        vb_off, vbB = self.af.take(128, "vbias")
        c_m = self.col(2)
        b.emit("sync", "dma_start", out=self.af.view(cs_off, T)[0:64, :],
               in_=cos_dr, reads=[tabB], writes=[tabB])
        b.emit("sync", "dma_start", out=self.af.view(sn_off, T)[0:64, :],
               in_=sin_dr, reads=[tabB], writes=[tabB])
        b.emit("sync", "dma_start", out=self.af.view(mtri_off, 128),
               in_=mtri_dr, reads=[mtB], writes=[mtB])
        b.emit("sync", "dma_start", out=self.af.view(mtrit_off, 128),
               in_=mtrit_dr, reads=[mtB], writes=[mtB])

        b.emit("gpsimd", "memset", self.ab.view(k1_off, T, 64, 65), 1.0, writes=[k1B])
        b.emit("gpsimd", "memset", self.ab.view(k2_off, T, 64, 65), 1.0, writes=[k2B])
        for tt in range(16):
            b.emit("gpsimd", "memset",
                   self.ab.view(va_off + tt * 130 + 64, 1), 1.0, writes=[vaB])
            b.emit("gpsimd", "memset",
                   self.ab.view(va_off + tt * 130 + 129, 1), 1.0, writes=[vaB])

        for hp in range(8):
            for wi, (wdr, colbase) in enumerate(
                    ((wq, hp * 128), (wkv, hp * 128), (wkv, D + hp * 128))):
                for kc in range(NC):
                    b.emit("gpsimd", "dma_start",
                           out=self.ab.view(wsl_off + wi * 1024 + kc * 128, 128),
                           in_=wdr[kc * 128:(kc + 1) * 128, colbase:colbase + 128],
                           reads=[wslB], writes=[wslB])
            b.emit("sync", "dma_start",
                   out=self.af.view(vb_off, 128),
                   in_=AP(bv_dram.tensor, hp * 128, [[0, 128], [1, 128]]),
                   reads=[vbB], writes=[vbB])
            for (dst_off, dstB, head_half, wi, coff) in (
                    (q1_off, q1B, 0, 0, cb_q), (q2_off, q2B, 1, 0, cb_q),
                    (k1_off, k1B, 0, 1, cb_k), (k2_off, k2B, 1, 1, cb_k)):
                for tb in range(NTB):
                    ps = self.ps_a if tb % 2 == 0 else self.ps_b
                    psB = self.ps_aB if tb % 2 == 0 else self.ps_bB
                    for kc in range(NC):
                        b.emit("tensor", "matmul", ps[:],
                               self.ab.view(wsl_off + wi * 1024 + kc * 128, 128),
                               self.XH[:, kc * T + tb * TB: kc * T + (tb + 1) * TB],
                               start=(kc == 0), stop=(kc == NC - 1),
                               reads=[wslB, self.XHBufs[kc]], writes=[psB])
                    scl = 0.125 if coff == cb_q else None
                    if scl is not None:
                        b.emit("vector", "tensor_scalar",
                               out=self.ab.view(dst_off + tb * TB, TB, 0, 64),
                               in0=ps[head_half * 64:(head_half + 1) * 64, :],
                               scalar1=scl,
                               scalar2=self.cols[head_half * 64:(head_half + 1) * 64,
                                                 coff + hp:coff + hp + 1],
                               op0=ALU.mult, op1=ALU.add,
                               reads=[psB, self.colsB], writes=[dstB])
                    else:
                        b.emit("vector", "tensor_scalar",
                               out=self.ab.view(dst_off + tb * TB, TB, 0, 64),
                               in0=ps[head_half * 64:(head_half + 1) * 64, :],
                               scalar1=self.cols[head_half * 64:(head_half + 1) * 64,
                                                 coff + hp:coff + hp + 1],
                               scalar2=None, op0=ALU.add,
                               reads=[psB, self.colsB], writes=[dstB])
            for doff, dB in ((q1_off, q1B), (q2_off, q2B), (k1_off, k1B), (k2_off, k2B)):
                self._rope64(doff, dB, cs_off, sn_off, tabB, rs_off, rsB, None)
            def mk_vp(tt):
                def f():
                    ps = self.ps_a if tt % 2 == 0 else self.ps_b
                    psB = self.ps_aB if tt % 2 == 0 else self.ps_bB
                    for kc in range(NC):
                        b.emit("tensor", "matmul", ps[:, 0:128],
                               self.XH[:, kc * T + tt * 128: kc * T + (tt + 1) * 128],
                               self.ab.view(wsl_off + 2 * 1024 + kc * 128, 128),
                               start=(kc == 0), stop=(kc == NC - 1),
                               reads=[wslB, self.XHBufs[kc]], writes=[psB])
                    b.emit("vector", "tensor_add",
                           out=self.ab.view(va_off, 16 * 130)[:, tt * 130: tt * 130 + 64],
                           in0=ps[:, 0:64], in1=self.af.view(vb_off, 128)[:, 0:64],
                           reads=[psB, vbB], writes=[vaB])
                    b.emit("vector", "tensor_add",
                           out=self.ab.view(va_off, 16 * 130)[:, tt * 130 + 65: tt * 130 + 129],
                           in0=ps[:, 64:128], in1=self.af.view(vb_off, 128)[:, 64:128],
                           reads=[psB, vbB], writes=[vaB])
                return f
            self._attn_causal_core(hp, q1_off, q1B, q2_off, q2B, k1_off, k1B,
                                   k2_off, k2B, va_off, vaB, pt_off, ptB,
                                   mtri_off, mtrit_off, mtB, c_m,
                                   pre_steps=[mk_vp(tt) for tt in range(16)])
        self.ab.pop()
        self.af.pop()
        self.b.barrier()
        self.ab.push()
        self.af.push()
        st_off, stB = self.af.take(1024, "wstage2")
        wo_off, woB = self.ab.take(8 * 1024, "l2_wo")
        self.load_wt_bf16(wo, D, D, wo_off, woB, st_off, stB)
        for oc in range(NC):
            for tb in range(NTB):
                ps = self.ps_a if (oc + tb) % 2 == 0 else self.ps_b
                psB = self.ps_aB if (oc + tb) % 2 == 0 else self.ps_bB
                for kc in range(NC):
                    b.emit("tensor", "matmul", ps[:],
                           self.ab.view(wo_off + kc * 1024, 1024)[:, oc * 128:(oc + 1) * 128],
                           self.OF[:, kc * T + tb * TB: kc * T + (tb + 1) * TB],
                           start=(kc == 0), stop=(kc == NC - 1),
                           reads=[woB, self.OFBufs[kc]], writes=[psB])
                b.emit("vector", "scalar_tensor_tensor",
                       out=self.X[:, oc * T + tb * TB: oc * T + (tb + 1) * TB],
                       in0=ps[:], scalar=self.cols[:, cb_o + oc:cb_o + oc + 1],
                       in1=self.X[:, oc * T + tb * TB: oc * T + (tb + 1) * TB],
                       op0=ALU.add, op1=ALU.add,
                       reads=[psB, self.colsB, self.XBufs[oc]],
                       writes=[self.XBufs[oc]])
        self.ab.pop()
        self.af.pop()

    def _rope32(self, dst_off, dstB, rows, cs2_off, sn2_off, tabB,
                p32_off, p32B, sc_off, scB):
        b = self.b
        for tb in range(NTB):
            b.emit("vector", "tensor_mul",
                   out=self.ab.view(sc_off, TB, 0, rows),
                   in0=self.ab.view(dst_off + tb * TB, TB, 0, rows),
                   in1=self.af.view(sn2_off + tb * TB, TB, 0, rows),
                   reads=[dstB, tabB], writes=[scB])
            b.emit("vector", "tensor_mul",
                   out=self.ab.view(sc_off + TB, TB, 0, rows),
                   in0=self.ab.view(dst_off + tb * TB, TB, 0, rows),
                   in1=self.af.view(cs2_off + tb * TB, TB, 0, rows),
                   reads=[dstB, tabB], writes=[scB])
            ps = self.ps_a if tb % 2 == 0 else self.ps_b
            psB = self.ps_aB if tb % 2 == 0 else self.ps_bB
            b.emit("tensor", "matmul", ps[0:rows, :],
                   self.ab.view(p32_off, 128, 0, rows)[:, 0:rows],
                   self.ab.view(sc_off, TB, 0, rows),
                   start=True, stop=True,
                   reads=[p32B, scB], writes=[psB])
            b.emit("vector", "scalar_tensor_tensor",
                   out=self.ab.view(dst_off + tb * TB, TB, 0, rows),
                   in0=ps[0:rows, :], scalar=1.0,
                   in1=self.ab.view(sc_off + TB, TB, 0, rows),
                   op0=ALU.bypass, op1=ALU.add,
                   reads=[psB, scB], writes=[dstB])

    def l2_ffn(self):
        b = self.b
        nc = self.nc
        self.layernorm()
        for c in range(NC):
            b.emit("vector", "tensor_add",
                   out=self.X[:, c * T:(c + 1) * T],
                   in0=self.X[:, c * T:(c + 1) * T],
                   in1=self.XH[:, c * T:(c + 1) * T],
                   reads=[self.XBufs[c], self.XHBufs[c]], writes=[self.XBufs[c]])
        cln1 = self.col(NC); self.load_cols("l2f_ln1", (NC, 128), cln1, NC)
        cln2 = self.col(NC); self.load_cols("l2f_ln2", (NC, 128), cln2, NC)
        OFall = self.b.buf("l2f_xn1")
        self.ln_inner(self.OF, [OFall] * NC, cln1, None)

        kv_d_t = self.din("l2f_kv_d_t", (D, 256))
        q_d_t = self.din("l2f_q_d_t", (D, 256))
        k_u_t = self.din("l2f_k_u_t", (256, D))
        q_u_t = self.din("l2f_q_u_t", (256, D))
        v_u_t = self.din("l2f_v_u_t", (256, D))
        rope_q_t = self.din("l2f_rope_q_t", (256, 512))
        rope_k_t = self.din("l2f_rope_k_t", (D, 512))
        o_w_t = self.din("l2f_o_w_t", (D, D))
        mtri_dr = self.din("l2_mtri", (128, 128))
        mtrit_dr = self.din("l2_mtri_t", (128, 128))
        cos2_dr = self.din("l2f_cos2", (128, T))
        sin2_dr = self.din("l2f_sin2", (128, T))
        p32_dr = self.din("l2f_p32t", (128, 128))
        krot_dr = nc.dram_tensor("l2f_krot_scr", [512, T], BF16, kind="Internal").ap()
        krB = self.b.buf("l2f_krotdr")

        self.ab.push()
        self.af.push()
        kvl_off, kvlB = self.ab.take(2 * T, "kvl")
        ql_off, qlB = self.ab.take(2 * T, "ql")
        p32_off, p32B = self.ab.take(128, "p32")
        cs2_off, tab2B = self.af.take(T, "cos2")
        sn2_off, _tb2 = self.af.take(T, "sin2")
        mtri_off, mtB = self.af.take(128, "mtri")
        mtrit_off, _m2 = self.af.take(128, "mtrit")
        b.emit("gpsimd", "dma_start", out=self.ab.view(p32_off, 128),
               in_=p32_dr, reads=[p32B], writes=[p32B])
        b.emit("sync", "dma_start", out=self.af.view(cs2_off, T),
               in_=cos2_dr, reads=[tab2B], writes=[tab2B])
        b.emit("sync", "dma_start", out=self.af.view(sn2_off, T),
               in_=sin2_dr, reads=[tab2B], writes=[tab2B])
        b.emit("sync", "dma_start", out=self.af.view(mtri_off, 128),
               in_=mtri_dr, reads=[mtB], writes=[mtB])
        b.emit("sync", "dma_start", out=self.af.view(mtrit_off, 128),
               in_=mtrit_dr, reads=[mtB], writes=[mtB])

        self.ab.push()
        wsl_off, wslB = self.ab.take(1024, "wsl")
        for wdr, doff, dB in ((kv_d_t, kvl_off, kvlB), (q_d_t, ql_off, qlB)):
            for oc in range(2):
                for kc in range(NC):
                    b.emit("gpsimd", "dma_start",
                           out=self.ab.view(wsl_off + kc * 128, 128),
                           in_=wdr[kc * 128:(kc + 1) * 128, oc * 128:(oc + 1) * 128],
                           reads=[wslB], writes=[wslB])
                for tb in range(NTB):
                    ps = self.ps_a if tb % 2 == 0 else self.ps_b
                    psB = self.ps_aB if tb % 2 == 0 else self.ps_bB
                    for kc in range(NC):
                        b.emit("tensor", "matmul", ps[:],
                               self.ab.view(wsl_off + kc * 128, 128),
                               self.OF[:, kc * T + tb * TB: kc * T + (tb + 1) * TB],
                               start=(kc == 0), stop=(kc == NC - 1),
                               reads=[wslB, OFall], writes=[psB])
                    b.emit("scalar", "copy",
                           out=self.ab.view(doff + oc * T + tb * TB, TB),
                           in_=ps[:], reads=[psB], writes=[dB])
        kr_off, krsB = self.ab.take(T, "krot")
        rsc_off, rscB = self.ab.take(2 * TB, "rope_scr")
        for oc4 in range(4):
            for kc in range(NC):
                b.emit("gpsimd", "dma_start",
                       out=self.ab.view(wsl_off + kc * 128, 128),
                       in_=rope_k_t[kc * 128:(kc + 1) * 128, oc4 * 128:(oc4 + 1) * 128],
                       reads=[wslB], writes=[wslB])
            for tb in range(NTB):
                ps = self.ps_a if tb % 2 == 0 else self.ps_b
                psB = self.ps_aB if tb % 2 == 0 else self.ps_bB
                for kc in range(NC):
                    b.emit("tensor", "matmul", ps[:],
                           self.ab.view(wsl_off + kc * 128, 128),
                           self.OF[:, kc * T + tb * TB: kc * T + (tb + 1) * TB],
                           start=(kc == 0), stop=(kc == NC - 1),
                           reads=[wslB, OFall], writes=[psB])
                b.emit("scalar", "copy",
                       out=self.ab.view(kr_off + tb * TB, TB),
                       in_=ps[:], reads=[psB], writes=[krsB])
            self._rope32(kr_off, krsB, 128, cs2_off, sn2_off, tab2B,
                         p32_off, p32B, rsc_off, rscB)
            b.emit("gpsimd", "dma_start",
                   out=krot_dr[oc4 * 128:(oc4 + 1) * 128, :],
                   in_=self.ab.view(kr_off, T), reads=[krsB], writes=[krB])
        self.ab.pop()
        self.b.barrier()
        wsl_off, wslB = self.ab.take(512, "wslB")
        rsc_off, rscB = self.ab.take(2 * TB, "rope_scrB")
        q1_off, q1B = self.ab.take(T, "Q1")
        q2_off, q2B = self.ab.take(T, "Q2")
        k1_off, k1B = self.ab.take(T, "K1")
        k2_off, k2B = self.ab.take(T, "K2")
        va_off, vaB = self.ab.take(16 * 130, "VA")
        pt_off, ptB = self.ab.take(2 * TB, "PT")
        c_m = self.col(2)
        b.emit("gpsimd", "memset", self.ab.view(k1_off, T, 64, 65), 1.0, writes=[k1B])
        b.emit("gpsimd", "memset", self.ab.view(k2_off, T, 64, 65), 1.0, writes=[k2B])
        for tt in range(16):
            b.emit("gpsimd", "memset",
                   self.ab.view(va_off + tt * 130 + 64, 1), 1.0, writes=[vaB])
            b.emit("gpsimd", "memset",
                   self.ab.view(va_off + tt * 130 + 129, 1), 1.0, writes=[vaB])
        for hp in range(8):
            for kc2 in range(2):
                b.emit("gpsimd", "dma_start",
                       out=self.ab.view(wsl_off + kc2 * 128, 64),
                       in_=rope_q_t[kc2 * 128:(kc2 + 1) * 128, hp * 64:(hp + 1) * 64],
                       reads=[wslB], writes=[wslB])
            for tb in range(NTB):
                ps = self.ps_a if tb % 2 == 0 else self.ps_b
                psB = self.ps_aB if tb % 2 == 0 else self.ps_bB
                for kc2 in range(2):
                    b.emit("tensor", "matmul", ps[0:64, :],
                           self.ab.view(wsl_off + kc2 * 128, 64),
                           self.ab.view(ql_off + kc2 * T + tb * TB, TB),
                           start=(kc2 == 0), stop=(kc2 == 1),
                           reads=[wslB, qlB], writes=[psB])
                b.emit("scalar", "copy",
                       out=self.ab.view(q1_off + tb * TB, TB, 0, 32),
                       in_=ps[0:32, :], reads=[psB], writes=[q1B])
                b.emit("scalar", "copy",
                       out=self.ab.view(q2_off + tb * TB, TB, 0, 32),
                       in_=ps[32:64, :], reads=[psB], writes=[q2B])
            self._rope32(q1_off, q1B, 32, cs2_off, sn2_off, tab2B,
                         p32_off, p32B, rsc_off, rscB)
            self._rope32(q2_off, q2B, 32, cs2_off, sn2_off, tab2B,
                         p32_off, p32B, rsc_off, rscB)
            b.emit("sync", "dma_start",
                   out=self.ab.view(k1_off, T, 0, 32),
                   in_=AP(krot_dr.tensor, (64 * hp) * T, [[T, 32], [1, T]]),
                   reads=[krB], writes=[k1B])
            b.emit("sync", "dma_start",
                   out=self.ab.view(k2_off, T, 0, 32),
                   in_=AP(krot_dr.tensor, (64 * hp + 32) * T, [[T, 32], [1, T]]),
                   reads=[krB], writes=[k2B])
            for wdr, d1o, d1B, d2o, d2B, scl in (
                    (q_u_t, q1_off, q1B, q2_off, q2B, None),
                    (k_u_t, k1_off, k1B, k2_off, k2B, None)):
                for kc2 in range(2):
                    b.emit("gpsimd", "dma_start",
                           out=self.ab.view(wsl_off + kc2 * 128, 128),
                           in_=wdr[kc2 * 128:(kc2 + 1) * 128, hp * 128:(hp + 1) * 128],
                           reads=[wslB], writes=[wslB])
                for tb in range(NTB):
                    ps = self.ps_a if tb % 2 == 0 else self.ps_b
                    psB = self.ps_aB if tb % 2 == 0 else self.ps_bB
                    for kc2 in range(2):
                        b.emit("tensor", "matmul", ps[:],
                               self.ab.view(wsl_off + kc2 * 128, 128),
                               self.ab.view(kvl_off + kc2 * T + tb * TB, TB) if wdr is k_u_t
                               else self.ab.view(ql_off + kc2 * T + tb * TB, TB),
                               start=(kc2 == 0), stop=(kc2 == 1),
                               reads=[wslB, kvlB if wdr is k_u_t else qlB],
                               writes=[psB])
                    for half, (do_, dB_) in enumerate(((d1o, d1B), (d2o, d2B))):
                        if scl is not None:
                            b.emit("vector", "tensor_scalar",
                                   out=self.ab.view(do_ + tb * TB, TB, 32, 64),
                                   in0=ps[half * 64 + 32:(half + 1) * 64, :],
                                   scalar1=scl, scalar2=None, op0=ALU.mult,
                                   reads=[psB], writes=[dB_])
                        else:
                            b.emit("scalar", "copy",
                                   out=self.ab.view(do_ + tb * TB, TB, 32, 64),
                                   in_=ps[half * 64 + 32:(half + 1) * 64, :],
                                   reads=[psB], writes=[dB_])
            for qo, qB_ in ((q1_off, q1B), (q2_off, q2B)):
                b.emit("vector", "tensor_scalar",
                       out=self.ab.view(qo, T, 0, 64),
                       in0=self.ab.view(qo, T, 0, 64),
                       scalar1=0.125, scalar2=None, op0=ALU.mult,
                       reads=[qB_], writes=[qB_])
            for kc2 in range(2):
                b.emit("gpsimd", "dma_start",
                       out=self.ab.view(wsl_off + kc2 * 128, 128),
                       in_=v_u_t[kc2 * 128:(kc2 + 1) * 128, hp * 128:(hp + 1) * 128],
                       reads=[wslB], writes=[wslB])

            def mk_vp(tt):
                def f():
                    ps = self.ps_a if tt % 2 == 0 else self.ps_b
                    psB = self.ps_aB if tt % 2 == 0 else self.ps_bB
                    for kc2 in range(2):
                        b.emit("tensor", "matmul", ps[:, 0:128],
                               self.ab.view(kvl_off + kc2 * T + tt * 128, 128),
                               self.ab.view(wsl_off + kc2 * 128, 128),
                               start=(kc2 == 0), stop=(kc2 == 1),
                               reads=[wslB, kvlB], writes=[psB])
                    b.emit("scalar", "copy",
                           out=self.ab.view(va_off, 16 * 130)[:, tt * 130: tt * 130 + 64],
                           in_=ps[:, 0:64], reads=[psB], writes=[vaB])
                    b.emit("scalar", "copy",
                           out=self.ab.view(va_off, 16 * 130)[:, tt * 130 + 65: tt * 130 + 129],
                           in_=ps[:, 64:128], reads=[psB], writes=[vaB])
                return f
            self._attn_causal_core(hp, q1_off, q1B, q2_off, q2B, k1_off, k1B,
                                   k2_off, k2B, va_off, vaB, pt_off, ptB,
                                   mtri_off, mtrit_off, mtB, c_m,
                                   pre_steps=[mk_vp(tt) for tt in range(16)])
        self.ab.pop()
        self.af.pop()
        self.b.barrier()
        self.ab.push()
        self.af.push()
        wo_off, woB = self.ab.take(8 * 1024, "l2f_wo")
        for kc in range(NC):
            b.emit("gpsimd", "dma_start",
                   out=self.ab.view(wo_off + kc * 1024, 1024),
                   in_=o_w_t[kc * 128:(kc + 1) * 128, :],
                   reads=[woB], writes=[woB])
        for oc in range(NC):
            for tb in range(NTB):
                ps = self.ps_a if (oc + tb) % 2 == 0 else self.ps_b
                psB = self.ps_aB if (oc + tb) % 2 == 0 else self.ps_bB
                for kc in range(NC):
                    b.emit("tensor", "matmul", ps[:],
                           self.ab.view(wo_off + kc * 1024, 1024)[:, oc * 128:(oc + 1) * 128],
                           self.OF[:, kc * T + tb * TB: kc * T + (tb + 1) * TB],
                           start=(kc == 0), stop=(kc == NC - 1),
                           reads=[woB, self.OFBufs[kc]], writes=[psB])
                sl = slice(oc * T + tb * TB, oc * T + (tb + 1) * TB)
                b.emit("vector", "tensor_add",
                       out=self.X[:, sl], in0=self.X[:, sl], in1=ps[:],
                       reads=[self.XBufs[oc], psB], writes=[self.XBufs[oc]])
                b.emit("vector", "tensor_add",
                       out=self.XH[:, sl], in0=self.XH[:, sl], in1=ps[:],
                       reads=[self.XHBufs[oc], psB], writes=[self.XHBufs[oc]])
        self.ab.pop()
        self.af.pop()
        self.b.barrier()
        if L2F_PARTS == "attn":
            return
        self.ln_inner(self.XH, self.XHBufs, cln2, None)
        router_t = self.din("l2f_router_t", (D, 7))
        rbias_dr = self.din("l2f_rbias", (128, 7))
        gate_dr = self.din("l2f_gate_t", (8, D, 2048))
        up_dr = self.din("l2f_up_t", (8, D, 2048))
        down_dr = self.din("l2f_down_t", (8, 2048, D))
        self.ab.push()
        self.af.push()
        rw_off, rwB = self.ab.take(56, "routerw")
        pr_off, prB = self.af.take(7, "probs")

        rb_off, rbB = self.af.take(7, "rbias")
        wg_off, wgB = self.af.take(128, "WGT_stage")
        wb_off, wbB = self.af.take(4 * TB, "wgt_bcast")
        wgt_dr = nc.dram_tensor("l2f_wgt_scr", [7, T], F32, kind="Internal").ap()
        wgdB = self.b.buf("l2f_wgtdr")
        c_t2 = self.col(4)
        b.emit("gpsimd", "memset", self.wsmall[:, 7:16], 0.0,
               writes=[self.wsmallB])
        for kc in range(NC):
            b.emit("gpsimd", "dma_start",
                   out=self.ab.view(rw_off + kc * 7, 7),
                   in_=router_t[kc * 128:(kc + 1) * 128, :],
                   reads=[rwB], writes=[rwB])
        b.emit("sync", "dma_start", out=self.af.view(rb_off, 7),
               in_=rbias_dr, reads=[rbB], writes=[rbB])
        for tt in range(16):
            for kc in range(NC):
                b.emit("tensor", "matmul", self.ps_a[:, 0:7],
                       self.XH[:, kc * T + tt * 128: kc * T + (tt + 1) * 128],
                       self.ab.view(rw_off + kc * 7, 7),
                       start=(kc == 0), stop=(kc == NC - 1),
                       reads=[self.XHBufs[kc], rwB], writes=[self.ps_aB])
            b.emit("vector", "tensor_add",
                   out=self.af.view(pr_off, 7), in0=self.ps_a[:, 0:7],
                   in1=self.af.view(rb_off, 7),
                   reads=[self.ps_aB, rbB], writes=[prB])
            b.emit("scalar", "activation", out=self.af.view(pr_off, 7),
                   in_=self.af.view(pr_off, 7), func=AF.Sigmoid,
                   reads=[prB], writes=[prB])
            b.emit("vector", "reduce_max", out=self.cols[:, c_t2:c_t2 + 1],
                   in_=self.af.view(pr_off, 7), axis=AX.X,
                   reads=[prB], writes=[self.colsB])
            b.emit("vector", "tensor_scalar", out=self.wsmall[:, 0:7],
                   in0=self.af.view(pr_off, 7),
                   scalar1=self.cols[:, c_t2:c_t2 + 1], scalar2=None,
                   op0=ALU.is_ge, reads=[prB, self.colsB], writes=[self.wsmallB])
            b.emit("vector", "scalar_tensor_tensor",
                   out=self.af.view(pr_off, 7),
                   in0=self.wsmall[:, 0:7], scalar=-2.0,
                   in1=self.af.view(pr_off, 7),
                   op0=ALU.mult, op1=ALU.add,
                   reads=[prB, self.wsmallB], writes=[prB])
            b.emit("vector", "reduce_max", out=self.cols[:, c_t2 + 1:c_t2 + 2],
                   in_=self.af.view(pr_off, 7), axis=AX.X,
                   reads=[prB], writes=[self.colsB])
            b.emit("vector", "tensor_scalar", out=self.af.view(pr_off, 7),
                   in0=self.af.view(pr_off, 7),
                   scalar1=self.cols[:, c_t2 + 1:c_t2 + 2], scalar2=None,
                   op0=ALU.is_ge, reads=[prB, self.colsB], writes=[prB])
            b.emit("vector", "tensor_add", out=self.wsmall[:, 0:7],
                   in0=self.wsmall[:, 0:7],
                   in1=self.af.view(pr_off, 7),
                   reads=[self.wsmallB, prB], writes=[self.wsmallB])
            b.emit("vector", "tensor_add",
                   out=self.af.view(pr_off, 7), in0=self.ps_a[:, 0:7],
                   in1=self.af.view(rb_off, 7),
                   reads=[self.ps_aB, rbB], writes=[prB])
            b.emit("scalar", "activation", out=self.af.view(pr_off, 7),
                   in_=self.af.view(pr_off, 7), func=AF.Sigmoid,
                   reads=[prB], writes=[prB])
            b.emit("vector", "tensor_add", out=self.cols[:, c_t2 + 2:c_t2 + 3],
                   in0=self.cols[:, c_t2:c_t2 + 1],
                   in1=self.cols[:, c_t2 + 1:c_t2 + 2],
                   reads=[self.colsB], writes=[self.colsB])
            b.emit("vector", "reciprocal", out=self.cols[:, c_t2 + 3:c_t2 + 4],
                   in_=self.cols[:, c_t2 + 2:c_t2 + 3],
                   reads=[self.colsB], writes=[self.colsB])
            b.emit("vector", "tensor_mul", out=self.af.view(pr_off, 7),
                   in0=self.af.view(pr_off, 7),
                   in1=self.wsmall[:, 0:7],
                   reads=[prB, self.wsmallB], writes=[prB])
            b.emit("vector", "tensor_scalar", out=self.wsmall[:, 0:7],
                   in0=self.af.view(pr_off, 7),
                   scalar1=self.cols[:, c_t2 + 3:c_t2 + 4], scalar2=None,
                   op0=ALU.mult, reads=[prB, self.colsB], writes=[self.wsmallB])
            b.emit("tensor", "matmul", self.ps_b[0:16, 0:128],
                   self.wsmall[:, 0:16], self.id_bf[:],
                   start=True, stop=True,
                   reads=[self.wsmallB, self.constB], writes=[self.ps_bB])
            if L2F_DEBUG and tt == 15:
                did = self.nc.dram_tensor("dbg_id", [128, 128], F32,
                                          kind="ExternalOutput").ap()
                o4, s4B = self.af.take(128, "dbg4")
                b.emit("vector", "tensor_copy",
                       out=self.af.view(o4, 128),
                       in_=self.id_bf[:], reads=[self.constB], writes=[s4B])
                b.emit("gpsimd", "dma_start", out=did,
                       in_=self.af.view(o4, 128),
                       reads=[s4B], writes=[s4B])
                dps = self.nc.dram_tensor("dbg_ps", [7, 128], F32,
                                          kind="ExternalOutput").ap()
                o3, s3B = self.af.take(128, "dbg3")
                b.emit("vector", "tensor_copy",
                       out=self.af.view(o3, 128)[0:7, :],
                       in_=self.ps_b[0:7, 0:128],
                       reads=[self.ps_bB], writes=[s3B])
                b.emit("gpsimd", "dma_start", out=dps,
                       in_=self.af.view(o3, 128)[0:7, :],
                       reads=[s3B], writes=[s3B])
            b.emit("scalar", "copy",
                   out=self.af.view(wg_off, 128)[0:7, :],
                   in_=self.ps_b[0:7, 0:128],
                   reads=[self.ps_bB], writes=[wgB])
            b.emit("gpsimd", "dma_start",
                   out=AP(wgt_dr.tensor, tt * 128, [[T, 7], [1, 128]]),
                   in_=self.af.view(wg_off, 128)[0:7, :],
                   reads=[wgB], writes=[wgdB])
        if L2F_DEBUG:
            dst_stage = self.nc.dram_tensor("dbg_stage", [7, 128], F32,
                                            kind="ExternalOutput").ap()
            self.b.barrier()
            b.emit("gpsimd", "dma_start", out=dst_stage,
                   in_=self.af.view(wg_off, 128)[0:7, :],
                   reads=[wgB], writes=[wgB])
            dwg = self.nc.dram_tensor("dbg_wgt", [7, T], F32,
                                      kind="ExternalOutput").ap()
            self.b.barrier()
            b.emit("gpsimd", "dma_start", out=self.af.view(wg_off, 128)[0:7, :],
                   in_=AP(wgt_dr.tensor, 0, [[T, 7], [1, 128]]),
                   reads=[wgdB], writes=[wgB])
            b.emit("gpsimd", "dma_start",
                   out=AP(dwg.tensor, 0, [[T, 7], [1, 128]]),
                   in_=self.af.view(wg_off, 128)[0:7, :],
                   reads=[wgB], writes=[wgB])
            dpb = self.nc.dram_tensor("dbg_wgtb", [128, 7], F32,
                                      kind="ExternalOutput").ap()
            off2, s2B = self.af.take(7, "dbg2")
            b.emit("vector", "tensor_copy", out=self.af.view(off2, 7),
                   in_=self.wsmall[:, 0:7], reads=[self.wsmallB], writes=[s2B])
            b.emit("gpsimd", "dma_start", out=dpb, in_=self.af.view(off2, 7),
                   reads=[s2B], writes=[s2B])
            self.b.barrier()
        gu_off, _gu = self.ab.take(4096, "gu")
        guBs = [self.b.buf("gu0"), self.b.buf("gu1")]
        dn_off, _dn = self.ab.take(16384, "down")
        dnBs = [self.b.buf("dn0"), self.b.buf("dn1")]
        hs_off, hsB = self.af.take(2 * TB, "silu")
        h2_off, h2B = self.af.take(TB, "hid_tmp")

        def gu_load(n):
            e, ic = n // 16, n % 16
            slot = n % 2
            base = e * D * 2048 + ic * 128
            b.emit("gpsimd", "dma_start",
                   out=self.ab.view(gu_off + slot * 2048, 1024),
                   in_=AP(gate_dr.tensor, base,
                          [[2048, 128], [128 * 2048, 8], [1, 128]]),
                   reads=[guBs[slot]], writes=[guBs[slot]])
            b.emit("gpsimd", "dma_start",
                   out=self.ab.view(gu_off + slot * 2048 + 1024, 1024),
                   in_=AP(up_dr.tensor, base,
                          [[2048, 128], [128 * 2048, 8], [1, 128]]),
                   reads=[guBs[slot]], writes=[guBs[slot]])

        def dn_load(m):
            e, ig = m // 2, m % 2
            slot = m % 2
            base = e * 2048 * D + ig * 8 * 128 * D
            b.emit("gpsimd", "dma_start",
                   out=self.ab.view(dn_off + slot * 8192, 8192),
                   in_=AP(down_dr.tensor, base,
                          [[D, 128], [128 * D, 8], [1, D]]),
                   reads=[dnBs[slot]], writes=[dnBs[slot]])

        gu_load(0)
        gu_load(1)
        dn_load(0)
        dn_load(1)
        for e in range(8):
            if e > 0:
                for tb in range(NTB):
                    b.emit("scalar", "dma_start",
                           out=self.af.view(wb_off + tb * TB, TB),
                           in_=AP(wgt_dr.tensor, (e - 1) * T + tb * TB,
                                  [[0, 128], [1, TB]]),
                           reads=[wgdB], writes=[wbB])
            for ig in range(2):
                m = e * 2 + ig
                dslot = m % 2
                for il in range(8):
                    n = e * 16 + ig * 8 + il
                    slot = n % 2
                    for tb in range(NTB):
                        psg, psgB = ((self.ps_a, self.ps_aB) if tb % 2 == 0
                                     else (self.ps_c, self.ps_cB))
                        psu, psuB = ((self.ps_b, self.ps_bB) if tb % 2 == 0
                                     else (self.ps_d, self.ps_dB))
                        for kc in range(NC):
                            b.emit("tensor", "matmul", psg[:],
                                   self.ab.view(gu_off + slot * 2048 + kc * 128, 128),
                                   self.XH[:, kc * T + tb * TB: kc * T + (tb + 1) * TB],
                                   start=(kc == 0), stop=(kc == NC - 1),
                                   reads=[guBs[slot], self.XHBufs[kc]], writes=[psgB])
                            b.emit("tensor", "matmul", psu[:],
                                   self.ab.view(gu_off + slot * 2048 + 1024 + kc * 128, 128),
                                   self.XH[:, kc * T + tb * TB: kc * T + (tb + 1) * TB],
                                   start=(kc == 0), stop=(kc == NC - 1),
                                   reads=[guBs[slot], self.XHBufs[kc]], writes=[psuB])
                        hsl = hs_off + (tb % 2) * TB
                        b.emit("scalar", "activation",
                               out=self.af.view(hsl, TB), in_=psg[:],
                               func=AF.Silu, reads=[psgB], writes=[hsB])
                        ofslot = il * T + tb * TB
                        if e == 0:
                            b.emit("vector", "tensor_mul",
                                   out=self.OF[:, ofslot:ofslot + TB],
                                   in0=psu[:], in1=self.af.view(hsl, TB),
                                   reads=[psuB, hsB], writes=[self.OFBufs[il]])
                        else:
                            b.emit("vector", "tensor_mul",
                                   out=self.af.view(h2_off, TB),
                                   in0=psu[:], in1=self.af.view(hsl, TB),
                                   reads=[psuB, hsB], writes=[h2B])
                            b.emit("vector", "tensor_mul",
                                   out=self.OF[:, ofslot:ofslot + TB],
                                   in0=self.af.view(h2_off, TB),
                                   in1=self.af.view(wb_off + tb * TB, TB),
                                   reads=[wbB, h2B], writes=[self.OFBufs[il]])
                    if n + 2 < 128:
                        gu_load(n + 2)
                for tb in range(NTB):
                    for oc in range(NC):
                        ps = self.ps_big[:, (oc % 4) * TB:(oc % 4 + 1) * TB]
                        for il in range(8):
                            b.emit("tensor", "matmul", ps,
                                   self.ab.view(dn_off + dslot * 8192 + il * 1024, 1024)[:, oc * 128:(oc + 1) * 128],
                                   self.OF[:, il * T + tb * TB: il * T + (tb + 1) * TB],
                                   start=(il == 0), stop=(il == 7),
                                   reads=[dnBs[dslot], self.OFBufs[il]],
                                   writes=[self.ps_bigQ[oc % 4]])
                        sl = slice(oc * T + tb * TB, oc * T + (tb + 1) * TB)
                        b.emit("vector", "tensor_add",
                               out=self.X[:, sl], in0=self.X[:, sl], in1=ps,
                               reads=[self.XBufs[oc], self.ps_bigQ[oc % 4]],
                               writes=[self.XBufs[oc]])
                if m + 2 < 16:
                    dn_load(m + 2)
        self.ab.pop()
        self.af.pop()

    def l3_attn(self):
        b = self.b
        self.layernorm()
        wq = self.din("l3_wq", (D, D))
        wk = self.din("l3_wk", (D, D))
        wv = self.din("l3_wv", (D, D))
        cb_q = self.col(NC); self.load_cols("l3_bq", (NC, 128), cb_q, NC)
        c_tmp = self.col(16); self.load_cols("l3_temp", (16, 128), c_tmp, 16)
        bk_b = self.din("l3_bk", (D,))
        bv_b = self.din("l3_bv", (D,))
        yk_dr = self.nc.dram_tensor("l3_yk_scr", [T, D], BF16, kind="Internal").ap()
        yv_dr = self.nc.dram_tensor("l3_yv_scr", [T, D], BF16, kind="Internal").ap()
        ykB = self.b.buf("l3_ykdr")
        yvB = self.b.buf("l3_yvdr")
        OFall = self.b.buf("l3_OFall")

        self.ab.push()
        self.af.push()
        st_off, stB = self.af.take(1024, "wstage")
        wsl_off, wslB = self.ab.take(2048, "wsl")
        for oc in range(NC):
            dbuf = oc % 2
            for kc in range(NC):
                b.emit("gpsimd", "dma_start",
                       out=self.ab.view(wsl_off + dbuf * 1024 + kc * 128, 128),
                       in_=wq[kc * 128:(kc + 1) * 128, oc * 128:(oc + 1) * 128],
                       reads=[wslB], writes=[wslB])
            for tb in range(NTB):
                ps = self.ps_a if tb % 2 == 0 else self.ps_b
                psB = self.ps_aB if tb % 2 == 0 else self.ps_bB
                for kc in range(NC):
                    b.emit("tensor", "matmul", ps[:],
                           self.ab.view(wsl_off + dbuf * 1024 + kc * 128, 128),
                           self.XH[:, kc * T + tb * TB: kc * T + (tb + 1) * TB],
                           start=(kc == 0), stop=(kc == NC - 1),
                           reads=[wslB, self.XHBufs[kc]], writes=[psB])
                b.emit("vector", "tensor_scalar",
                       out=self.OF[:, oc * T + tb * TB: oc * T + (tb + 1) * TB],
                       in0=ps[:], scalar1=self.cols[:, cb_q + oc:cb_q + oc + 1],
                       scalar2=None, op0=ALU.add,
                       reads=[psB, self.colsB], writes=[OFall])
        bk_off, bkbB = self.af.take(512, "bkb")
        ev_off, evB = self.ab.take(512, "yevac")
        wh_off, whB = self.ab.take(8192, "whalf")
        for wdr, bias_dr, dst, dstB, temp_scaled in (
                (wk, bk_b, yk_dr, ykB, True), (wv, bv_b, yv_dr, yvB, False)):
            for oh in range(2):
                for kc in range(NC):
                    b.emit("gpsimd", "dma_start",
                           out=self.ab.view(wh_off + kc * 512, 512),
                           in_=wdr[kc * 128:(kc + 1) * 128, oh * 512:(oh + 1) * 512],
                           reads=[whB], writes=[whB])
                b.emit("gpsimd", "dma_start",
                       out=self.af.view(bk_off, 512),
                       in_=AP(bias_dr.tensor, oh * 512, [[0, 128], [1, 512]]),
                       reads=[bkbB], writes=[bkbB])
                for tt in range(16):
                    h = tt
                    ps = self.ps_a if tt % 2 == 0 else self.ps_b
                    psB = self.ps_aB if tt % 2 == 0 else self.ps_bB
                    for kc in range(NC):
                        b.emit("tensor", "matmul", ps[:],
                               self.XH[:, kc * T + tt * 128: kc * T + (tt + 1) * 128],
                               self.ab.view(wh_off + kc * 512, 512),
                               start=(kc == 0), stop=(kc == NC - 1),
                               reads=[whB, self.XHBufs[kc]], writes=[psB])
                    if temp_scaled:
                        b.emit("vector", "scalar_tensor_tensor",
                               out=self.ab.view(ev_off, 512),
                               in0=ps[:], scalar=self.cols[:, c_tmp + h:c_tmp + h + 1],
                               in1=self.af.view(bk_off, 512),
                               op0=ALU.bypass, op1=ALU.add,
                               reads=[psB, bkbB, self.colsB], writes=[evB])
                        b.emit("vector", "tensor_scalar",
                               out=self.ab.view(ev_off, 512),
                               in0=self.ab.view(ev_off, 512),
                               scalar1=self.cols[:, c_tmp + h:c_tmp + h + 1],
                               scalar2=None, op0=ALU.mult,
                               reads=[evB, self.colsB], writes=[evB])
                    else:
                        b.emit("vector", "tensor_add",
                               out=self.ab.view(ev_off, 512),
                               in0=ps[:], in1=self.af.view(bk_off, 512),
                               reads=[psB, bkbB], writes=[evB])
                    b.emit("gpsimd", "dma_start",
                           out=dst[tt * 128:(tt + 1) * 128, oh * 512:(oh + 1) * 512],
                           in_=self.ab.view(ev_off, 512),
                           reads=[evB], writes=[dstB])
        self.ab.pop()
        self.af.pop()
        self.b.barrier()
        self.ab.push()
        self.af.push()
        km_off, kmB = self.ab.take(T, "Kmat")
        vm_off, vmB = self.ab.take(T, "Vmat")
        p_off, pB = self.ab.take(T, "Ptile")
        qs_off, qsB = self.ab.take(128, "QS")
        os_off, osB = self.af.take(T, "out_f32")
        rc_off = self.col(16)
        cm2 = self.col(2)
        c_pm = self.col(4)
        c_acc = self.col(4)
        for h in range(16):
            for src_dr, srcB, moff, mB in ((yk_dr, ykB, km_off, kmB),
                                           (yv_dr, yvB, vm_off, vmB)):
                for flag in range(2):
                    b.emit("sync", "dma_start",
                           out=self.ab.view(moff + flag * 1024, 1024, 0, 64),
                           in_=AP(src_dr.tensor, (128 * h + flag) * D,
                                  [[2 * D, 64], [1, D]]),
                           reads=[srcB], writes=[mB])
            psacc = (self.ps_a, self.ps_b, self.ps_c, self.ps_d)
            psaccB = (self.ps_aB, self.ps_bB, self.ps_cB, self.ps_dB)

            def l3_s(tc):
                for sb in range(NTB):
                    b.emit("tensor", "matmul",
                           self.ps_big[:, sb * TB:(sb + 1) * TB],
                           self.ab.view(km_off + tc * 128, 128, 0, 64),
                           self.ab.view(vm_off + sb * TB, TB, 0, 64),
                           start=True, stop=True,
                           reads=[kmB, vmB], writes=[self.ps_bigQ[sb]])
                    b.emit("vector", "reduce_max",
                           out=self.cols[:, c_pm + sb:c_pm + sb + 1],
                           in_=self.ps_big[:, sb * TB:(sb + 1) * TB],
                           axis=AX.X,
                           reads=[self.ps_bigQ[sb]], writes=[self.colsB2])

            l3_s(0)
            for tc in range(16):
                b.emit("vector", "reduce_max",
                       out=self.cols[:, cm2:cm2 + 1],
                       in_=self.cols[:, c_pm:c_pm + 4],
                       axis=AX.X, negate=True,
                       reads=[self.colsB2], writes=[self.colsB])
                for sb in range(NTB):
                    b.emit("scalar", "activation",
                           out=self.ab.view(p_off + sb * TB, TB),
                           in_=self.ps_big[:, sb * TB:(sb + 1) * TB],
                           func=AF.Exp, bias=self.cols[:, cm2:cm2 + 1],
                           accum_out=self.cols[:, c_acc + sb:c_acc + sb + 1],
                           reads=[self.ps_bigQ[sb], self.colsB],
                           writes=[pB, self.colsB2])
                b.emit("vector", "reduce_sum",
                       out=self.cols[:, cm2 + 1:cm2 + 2],
                       in_=self.cols[:, c_acc:c_acc + 4],
                       axis=AX.X, reads=[self.colsB2], writes=[self.colsB])
                b.emit("vector", "reciprocal",
                       out=self.cols[:, rc_off:rc_off + 1],
                       in_=self.cols[:, cm2 + 1:cm2 + 2],
                       reads=[self.colsB], writes=[self.colsB])
                qsl = qs_off + (tc % 2) * 64
                qview = AP(self.OF, (tc % 8) * T + 128 * h + (tc // 8),
                           [[NC * T, 128], [2, 64]])
                b.emit("vector", "tensor_scalar",
                       out=self.ab.view(qsl, 64), in0=qview,
                       scalar1=self.cols[:, rc_off:rc_off + 1], scalar2=None,
                       op0=ALU.mult,
                       reads=[OFall, self.colsB], writes=[qsB])
                if tc + 1 < 16:
                    l3_s(tc + 1)
                for sb in range(NTB):
                    b.emit("tensor", "matmul", psacc[sb][0:64, :],
                           self.ab.view(qsl, 64),
                           self.ab.view(p_off + sb * TB, TB),
                           start=(tc == 0), stop=(tc == 15),
                           reads=[qsB, pB], writes=[psaccB[sb]])
            for sb in range(NTB):
                b.emit("scalar", "copy",
                       out=self.af.view(os_off + sb * TB, TB)[0:64, :],
                       in_=psacc[sb][0:64, :],
                       reads=[psaccB[sb]], writes=[osB])
            for sc in range(16):
                cf = sc % 8
                flag = sc // 8
                ps = self.ps_a if sc % 2 == 0 else self.ps_b
                psB = self.ps_aB if sc % 2 == 0 else self.ps_bB
                b.emit("tensor", "transpose", ps[:, 0:64],
                       self.af.view(os_off + sc * 128, 128)[0:64, :],
                       self.id_f32[0:64, 0:64],
                       reads=[osB, self.constB], writes=[psB])
                xview = AP(self.X, cf * T + 128 * h + flag, [[NC * T, 128], [2, 64]])
                b.emit("vector", "tensor_add",
                       out=xview, in0=xview, in1=ps[:, 0:64],
                       reads=[self.XBufs[cf], psB], writes=[self.XBufs[cf]])
        self.ab.pop()
        self.af.pop()

    def l3_ffn(self):
        b = self.b
        self.layernorm()
        w_t = self.din("l3f_w_t", (D, D))
        vr = self.din("l3f_vr", (D, 32))
        c_t = self.din("l3f_c_t", (32, D))
        cb = self.col(NC); self.load_cols("l3f_b", (NC, 128), cb, NC)
        self.ab.push()
        self.af.push()
        st_off, stB = self.af.take(1024, "wstage")
        w_off, wB = self.ab.take(8 * 1024, "l3f_w")
        xr_off, xrB = self.ab.take(T, "l3f_xr")
        ct_off, ctB = self.ab.take(1024, "l3f_ct")
        self.load_wt_bf16(w_t, D, D, w_off, wB, st_off, stB)
        b.emit("gpsimd", "dma_start", out=self.ab.view(ct_off, 1024)[0:32, :],
               in_=c_t, reads=[ctB], writes=[ctB])
        vr_off, vrB = self.ab.take(256, "l3f_vr")
        for kc in range(NC):
            b.emit("gpsimd", "dma_start",
                   out=self.ab.view(vr_off + kc * 32, 32),
                   in_=vr[kc * 128:(kc + 1) * 128, :], reads=[vrB], writes=[vrB])
        for tb in range(NTB):
            ps = self.ps_c
            for kc in range(NC):
                b.emit("tensor", "matmul", ps[0:32, :],
                       self.ab.view(vr_off + kc * 32, 32),
                       self.XH[:, kc * T + tb * TB: kc * T + (tb + 1) * TB],
                       start=(kc == 0), stop=(kc == NC - 1),
                       reads=[vrB, self.XHBufs[kc]], writes=[self.ps_cB])
            b.emit("scalar", "copy",
                   out=self.ab.view(xr_off + tb * TB, TB)[0:32, :],
                   in_=ps[0:32, :], reads=[self.ps_cB], writes=[xrB])
        for oc in range(NC):
            for tb in range(NTB):
                ps = self.ps_a if (oc + tb) % 2 == 0 else self.ps_b
                psB = self.ps_aB if (oc + tb) % 2 == 0 else self.ps_bB
                for kc in range(NC):
                    b.emit("tensor", "matmul", ps[:],
                           self.ab.view(w_off + kc * 1024, 1024)[:, oc * 128:(oc + 1) * 128],
                           self.XH[:, kc * T + tb * TB: kc * T + (tb + 1) * TB],
                           start=(kc == 0), stop=False,
                           reads=[wB, self.XHBufs[kc]], writes=[psB])
                b.emit("tensor", "matmul", ps[:],
                       self.ab.view(ct_off, 1024)[0:32, oc * 128:(oc + 1) * 128],
                       self.ab.view(xr_off + tb * TB, TB)[0:32, :],
                       start=False, stop=True,
                       reads=[ctB, xrB], writes=[psB])
                b.emit("vector", "scalar_tensor_tensor",
                       out=self.X[:, oc * T + tb * TB: oc * T + (tb + 1) * TB],
                       in0=ps[:], scalar=self.cols[:, cb + oc:cb + oc + 1],
                       in1=self.X[:, oc * T + tb * TB: oc * T + (tb + 1) * TB],
                       op0=ALU.add, op1=ALU.add,
                       reads=[psB, self.colsB, self.XBufs[oc]],
                       writes=[self.XBufs[oc]])
        self.ab.pop()
        self.af.pop()



BLOCK_INPUT_KEYS = {
    "l0_attn": ["l0_wq_t", "l0_wk_t", "l0_wv_t", "l0_wo_t", "l0_bq", "l0_bk",
                "l0_bv", "l0_bo", "l0_ext", "l0_ext_rev", "l0_rel_raw"],
}
CONST_KEYS = ["c_id_bf", "c_id_f32", "c_J_f32", "c_ones_row_f32",
              "c_ones_col_f32"]


def build_program(blocks, avail):
    nc = bass.Bass("TRN2", target_bir_lowering=False, debug=False)
    k = Ker(nc, avail)
    k.zero_init()
    k.b.barrier()
    k.load_consts()
    k.prologue()
    for blk in blocks:
        getattr(k, blk)()
        k.b.barrier()
    k.epilogue()
    k.b.finalize()
    k.b.es.close()
    return nc, sorted(k.dram.keys())


def run_blocks(inputs, blocks, n_cores=8):
    common = host_prep_common(inputs)
    x = np.ascontiguousarray(np.asarray(inputs["L0_attn_x"], np.float32))
    xs = [x[i] for i in range(n_cores)]
    avail = set(common.keys()) | {"x_in"}
    nc, used = build_program(blocks, avail)
    in_maps = []
    for i in range(n_cores):
        m = {n: common[n] for n in used if n != "x_in"}
        m["x_in"] = xs[i]
        in_maps.append(m)
    res = run_bass_kernel_spmd(nc, in_maps, core_ids=list(range(n_cores)))
    out = np.stack([np.asarray(r["y_out"]) for r in res.results], axis=0)
    return out


ALL_BLOCKS = ["l0_attn", "l0_ffn", "l1_attn", "l1_ffn", "l2_attn", "l2_ffn", "l3_attn", "l3_ffn"]


def kernel(**inputs):
    return run_blocks(inputs, ALL_BLOCKS).astype(np.float32)
